# version 07
# baseline (speedup 1.0000x reference)
import numpy as np
from contextlib import ExitStack
import ml_dtypes
import concourse.bass as bass
import concourse.mybir as mybir
from concourse.bass_utils import run_bass_kernel_spmd

F32 = mybir.dt.float32
BF16 = mybir.dt.bfloat16
ALU = mybir.AluOpType
ACT = mybir.ActivationFunctionType
AX = mybir.AxisListType
BFNP = ml_dtypes.bfloat16

NCORES = 8
SEQ = 2048
D = 1024
H = 16
HD = 64
C = 64
NT = SEQ // C
HID = 4096
MAGIC = 12582912.0
A1STAGE = 99


class TEW:

    def __init__(self, te):
        object.__setattr__(self, "_te", te)
        object.__setattr__(self, "last", None)

    def matmul(self, *a, **k):
        inst = self._te.matmul(*a, **k)
        object.__setattr__(self, "last", inst)
        return inst

    def __getattr__(self, n):
        return getattr(self._te, n)


class Prog:
    def __init__(self, nc, blk):
        self.nc = nc
        self.blk = blk
        self.sems = {}
        self.cnt = {}
        for s in ["ld", "lg", "st", "pe", "dv", "sc", "gp"]:
            self.sems[s] = nc.alloc_semaphore("sem_" + s)
            self.cnt[s] = 0

    def inc(self, inst, name, n=1):
        inst.then_inc(self.sems[name], n)
        self.cnt[name] += n
        return self.cnt[name]

    def flush(self, eng, inst, sem):
        n = self.inc(inst, sem, 1)
        eng.wait_ge(self.sems[sem], n)

    def wait(self, eng, name, val=None):
        v = self.cnt[name] if val is None else val
        if v > 0:
            eng.wait_ge(self.sems[name], v)


def pstep(t):
    return t.ap().ap[0][0]


def dma_x(P, eng, sbuf, dram_ap3, j, store=False, sem="ld"):
    for s in range(2):
        sl = sbuf[64 * s:64 * s + 64, :]
        dr = dram_ap3[s, j * C:(j + 1) * C, :]
        if store:
            P.inc(eng.dma_start(dr, sl), sem, 16)
        else:
            P.inc(eng.dma_start(sl, dr), sem, 16)


def build_program(nq=2, na1=NT, na2=NT, nbpre=NT, nbmain=NT, dbg=False):
    nc = bass.Bass("TRN2", target_bir_lowering=False, detect_race_conditions=False)

    D_IN = {}
    def din(name, shape, dt=F32):
        D_IN[name] = nc.dram_tensor(name, shape, dt, kind="ExternalInput")
        return D_IN[name]

    X = din("x", [2, SEQ, D])
    WQ = din("wq", [D, D]); WK = din("wk", [D, D]); WV = din("wv", [D, D])
    WAB = din("wab", [D, 2 * H]); BAB = din("bab", [1, 2 * H])
    BABT = din("babt", [32, 1])
    WG = din("wg", [D, D]); WO = din("wo", [D, D]); GN = din("gn", [128, 8])
    W1 = din("w1", [HID, D]); W2 = din("w2", [D, HID])
    CIDB = din("cidb", [128, 128], BF16); CIDF = din("cidf", [128, 128])
    CMB = din("cmaskb", [128, 128], BF16); CMM = din("cmaskm", [128, 128], BF16)
    CON1 = din("cones", [1, 128]); CON128 = din("cone128", [128, 1])
    CREP = din("crep", [16, H * 128], BF16)
    CMB4 = din("cmaskb4", [128, 512], BF16)
    CMM4 = din("cmaskm4", [128, 512], BF16)

    OUT = nc.dram_tensor("out", [2, SEQ, D], F32, kind="ExternalOutput")
    dkind = "ExternalOutput" if dbg else "Internal"
    DBG = nc.dram_tensor("dbgx", [8, 128, 2048], F32, kind=dkind) if dbg else None
    YD = nc.dram_tensor("ydram", [NT, 128, D], BF16, kind=dkind)
    X2D = nc.dram_tensor("x2dram", [NT, 128, D], F32, kind=dkind)
    W1T = nc.dram_tensor("w1tT", [D, HID], BF16)
    W2T = nc.dram_tensor("w2tT", [HID, D], BF16)

    with nc.Block() as blk:
        def blkT(f):
            blk.tensor(lambda te: f(TEW(te)))

        P = Prog(nc, blk)

        _esp = ExitStack()
        cidb = _esp.enter_context(nc.sbuf_tensor("cidb_s", [128, 128], BF16))
        cidf = _esp.enter_context(nc.sbuf_tensor("cidf_s", [128, 128], F32))
        maskB = _esp.enter_context(nc.sbuf_tensor("maskB_s", [128, 128], BF16))
        maskM = _esp.enter_context(nc.sbuf_tensor("maskM_s", [128, 128], BF16))
        ones1 = _esp.enter_context(nc.sbuf_tensor("ones1_s", [1, 128], F32))
        one128 = _esp.enter_context(nc.sbuf_tensor("one128_s", [128, 1], F32))
        gnsb = _esp.enter_context(nc.sbuf_tensor("gn_s", [128, 8], F32))
        babsb = _esp.enter_context(nc.sbuf_tensor("bab_s", [1, 2 * H], F32))
        crep = _esp.enter_context(nc.sbuf_tensor("crep_s", [16, H * 128], BF16))
        maskB4 = _esp.enter_context(nc.sbuf_tensor("maskB4_s", [128, 512], BF16))
        maskM4 = _esp.enter_context(nc.sbuf_tensor("maskM4_s", [128, 512], BF16))
        nbab_a = _esp.enter_context(nc.sbuf_tensor("nbab_a", [16, 1], F32))
        nbab_b = _esp.enter_context(nc.sbuf_tensor("nbab_b", [16, 1], F32))
        xbuf = [_esp.enter_context(nc.sbuf_tensor(f"xbuf{i}", [128, D], F32)) for i in range(2)]
        LNm = _esp.enter_context(nc.sbuf_tensor("LNm", [128, NT], F32))
        LNr = _esp.enter_context(nc.sbuf_tensor("LNr", [128, NT], F32))
        RSTY = _esp.enter_context(nc.sbuf_tensor("RSTY", [128, NT], F32))
        MEAN2 = _esp.enter_context(nc.sbuf_tensor("MEAN2", [128, NT], F32))
        CC2 = _esp.enter_context(nc.sbuf_tensor("CC2", [128, NT], F32))
        SCQ = _esp.enter_context(nc.sbuf_tensor("SCQ", [128, NT], F32))
        s1col = _esp.enter_context(nc.sbuf_tensor("s1col", [128, 1], F32))
        s2col = _esp.enter_context(nc.sbuf_tensor("s2col", [128, 1], F32))
        zcol = _esp.enter_context(nc.sbuf_tensor("zcol", [128, 1], F32))
        onecol = _esp.enter_context(nc.sbuf_tensor("onecol", [128, 1], F32))
        nonecol = _esp.enter_context(nc.sbuf_tensor("nonecol", [128, 1], F32))

        def L0(sy):
            for dst, src in [(cidb, CIDB), (cidf, CIDF), (maskB, CMB), (maskM, CMM),
                             (ones1, CON1), (one128, CON128), (gnsb, GN), (babsb, BAB),
                         (crep, CREP), (maskB4, CMB4), (maskM4, CMM4)]:
                P.inc(sy.dma_start(dst[:, :], src.ap()), "ld", 16)
            P.inc(sy.dma_start(nbab_a[:, 0:1], BABT.ap()[0:16, :]), "ld", 16)
            P.inc(sy.dma_start(nbab_b[:, 0:1], BABT.ap()[16:32, :]), "ld", 16)
            dma_x(P, sy, xbuf[0], X.ap(), 0)
        blk.sync(L0)

        def L0g(g):
            P.wait(g, "ld")
            g.tensor_scalar(nbab_a[:, 0:1], nbab_a[:, 0:1], -1.0, None, ALU.mult)
            g.tensor_scalar(nbab_b[:, 0:1], nbab_b[:, 0:1], -1.0, None, ALU.mult)
            g.memset(zcol[:, 0:1], 0.0)
            g.memset(onecol[:, 0:1], 1.0)
            P.inc(g.memset(nonecol[:, 0:1], -1.0), "gp", 1)
        blk.gpsimd(L0g)

        _es0 = ExitStack()
        wb0 = _es0.enter_context(nc.sbuf_tensor("wb0", [128, HID], F32))
        wb1 = _es0.enter_context(nc.sbuf_tensor("wb1", [128, HID], F32))
        wpart = _es0.enter_context(nc.sbuf_tensor("wpart", [128, 32], F32))
        wsum = _es0.enter_context(nc.sbuf_tensor("wsum", [128, 1], F32))
        wcmp = _es0.enter_context(nc.sbuf_tensor("wcmp", [128, HID], F32))
        terb = _es0.enter_context(nc.sbuf_tensor("terb", [128, HID], BF16))
        tslab = _es0.enter_context(nc.sbuf_tensor("tslab", [128, 1024], BF16))
        halfs = _es0.enter_context(nc.sbuf_tensor("halfs", [128, 1], F32))
        nhalfs = _es0.enter_context(nc.sbuf_tensor("nhalfs", [128, 1], F32))
        stot = _es0.enter_context(nc.sbuf_tensor("stot", [1, 2], F32))
        wabs = _es0.enter_context(nc.sbuf_tensor("wabs", [128, HID], F32))
        pq0 = _es0.enter_context(nc.psum_tensor("pq0", [128, 512], F32))
        pqt = _es0.enter_context(nc.psum_tensor("pqt", [128, 1024], BF16))
        if True:
            for wi, (WSRC, WDST, NTW, FW, WDc, scol) in enumerate([
                    (W1, W1T, 32, D, HID, s1col), (W2, W2T, 8, HID, D, s2col)][:nq]):
                wap = WSRC.ap().rearrange("(t p) f -> t p f", p=128)
                bufs = [wb0, wb1]

                base_ld = P.cnt["ld"]
                def p1ld(sy, base=base_ld):
                    for t in range(NTW):
                        if t >= 2:
                            P.wait(sy, "sc", 0)
                        P.inc(sy.dma_start(bufs[t % 2][:, 0:FW], wap[t]), "ld", 16)
                sc_base = P.cnt["sc"]
                def p1(sy):
                    for t in range(NTW):
                        if t >= 2:
                            sy.wait_ge(P.sems["sc"], sc_base + (t - 1))
                        P.inc(sy.dma_start(bufs[t % 2][:, 0:FW], wap[t]), "ld", 16)
                blk.sync(p1)

                def p1s(sc, base=base_ld):
                    P.wait(sc, "gp")
                    for t in range(NTW):
                        sc.wait_ge(P.sems["ld"], base + 16 * (t + 1))
                        P.flush(sc, sc.activation(wabs[:, 0:FW], bufs[t % 2][:, 0:FW],
                                                   ACT.Abs,
                                                   accum_out=wpart[:, t:t + 1],
                                                   bias=zcol[:, 0:1],
                                                   scale=onecol[:, 0:1]), "sc")
                blk.scalar(p1s)

                def p1v(v):
                    P.wait(v, "sc")
                    P.flush(v, v.tensor_reduce(wsum[:, 0:1], wpart[:, 0:NTW], AX.X,
                                               ALU.add), "dv")
                blk.vector(p1v)

                def p1pe(te):
                    P.wait(te, "dv")
                    te.matmul(pq0[0:1, 0:1], one128[:, 0:1], wsum[:, 0:1],
                              start=True, stop=True)
                    P.inc(te.last, "pe", 1)
                blkT(p1pe)

                def p1v2(v, wi=wi):
                    P.wait(v, "pe")
                    P.flush(v, v.tensor_scalar(stot[0:1, wi:wi + 1], pq0[0:1, 0:1],
                                                1.0 / (NTW * 128 * FW), 1e-5,
                                                ALU.mult, ALU.max), "dv")
                blk.vector(p1v2)

                def p1pe2(te, wi=wi):
                    P.wait(te, "dv")
                    te.matmul(pq0[:, 1:2], ones1[0:1, :], stot[0:1, wi:wi + 1],
                              start=True, stop=True)
                    P.inc(te.last, "pe", 1)
                blkT(p1pe2)

                def p1v3(v, scol=scol):
                    P.wait(v, "pe")
                    P.flush(v, v.tensor_scalar(halfs[:, 0:1], pq0[:, 1:2], 0.5, None,
                                                ALU.mult), "dv")
                    P.flush(v, v.tensor_scalar(nhalfs[:, 0:1], pq0[:, 1:2], -0.5, None,
                                               ALU.mult), "dv")
                    P.flush(v, v.tensor_scalar(scol[:, 0:1], pq0[:, 1:2], -1.0, None,
                                               ALU.mult), "dv")
                blk.vector(p1v3)
                ev_scales = P.cnt["dv"]

                base2 = P.cnt["ld"]
                nblk = FW // 128
                for t in range(NTW):
                    def p2ld(sy, t=t):
                        if t >= 2:
                            P.wait(sy, "dv")
                        P.inc(sy.dma_start(bufs[t % 2][:, 0:FW], wap[t]), "ld", 16)
                    blk.sync(p2ld)
                    ld_t = P.cnt["ld"]

                    def p2v(v, t=t, ld_t=ld_t):
                        v.wait_ge(P.sems["ld"], ld_t)
                        if t == 0:
                            P.wait(v, "dv", ev_scales)
                        if t >= 1:
                            P.wait(v, "gp")
                        P.flush(v, v.tensor_scalar(wcmp[:, 0:FW], bufs[t % 2][:, 0:FW],
                                                    halfs[:, 0:1], None, ALU.is_gt),
                                "dv")
                        P.inc(v.scalar_tensor_tensor(terb[:, 0:FW], bufs[t % 2][:, 0:FW],
                                                     nhalfs[:, 0:1], wcmp[:, 0:FW],
                                                     ALU.is_lt, ALU.subtract), "dv", 1)
                    blk.vector(p2v)
                    ev_ter = P.cnt["dv"]

                    for w0 in range(0, nblk, 8):
                        def p2pe(te, w0=w0, ev=ev_ter):
                            P.wait(te, "dv", ev)
                            P.wait(te, "sc")
                            for b in range(8):
                                te.matmul(pqt[:, b * 128:(b + 1) * 128],
                                          terb[:, (w0 + b) * 128:(w0 + b + 1) * 128],
                                          cidb[:, :], start=True, stop=True,
                                          is_transpose=True)
                            P.inc(te.last, "pe", 1)
                        blkT(p2pe)
                        ev_tr = P.cnt["pe"]

                        def p2gp(sc, ev=ev_tr):
                            P.wait(sc, "pe", ev)
                            P.wait(sc, "st")
                            P.inc(sc.activation(tslab[:, :], pqt[:, :], ACT.Copy,
                                                bias=0.0, scale=onecol[:, 0:1]), "sc", 1)
                        blk.scalar(p2gp)
                        ev_ev = P.cnt["sc"]

                        def p2st(sy, t=t, w0=w0, ev=ev_ev, WDST=WDST, WDc=WDc):
                            P.wait(sy, "sc", ev)
                            off = (w0 * 128) * WDc + t * 128
                            dst = bass.AP(WDST, off, [[WDc, 128], [128 * WDc, 8], [1, 128]])
                            P.inc(sy.dma_start(
                                dst, tslab[:, :].rearrange("p (b f) -> p b f", b=8)),
                                "st", 16)
                        blk.sync(p2st)
        _es0.close()
        quant_st_done = P.cnt["st"]

        _es = ExitStack()
        Wk_s = _es.enter_context(nc.sbuf_tensor("Wk_s", [128, 8, D], BF16))
        Wq_s = _es.enter_context(nc.sbuf_tensor("Wq_s", [128, 8, D], BF16))
        Wv_s = _es.enter_context(nc.sbuf_tensor("Wv_s", [128, 8, D], BF16))
        Wab_s = _es.enter_context(nc.sbuf_tensor("Wab_s", [128, 8, 2 * H], BF16))
        xln = _es.enter_context(nc.sbuf_tensor("xln", [128, D], BF16))
        xlnT = _es.enter_context(nc.sbuf_tensor("xlnT", [128, D], BF16))
        Ktok = _es.enter_context(nc.sbuf_tensor("Ktok", [128, D], BF16))
        Qtok = _es.enter_context(nc.sbuf_tensor("Qtok", [128, D], BF16))
        Vtok = _es.enter_context(nc.sbuf_tensor("Vtok", [128, D], BF16))
        Ktb = _es.enter_context(nc.sbuf_tensor("Ktb", [128, H * 128], BF16))
        Qtb = _es.enter_context(nc.sbuf_tensor("Qtb", [128, H * 128], BF16))
        Kfb = _es.enter_context(nc.sbuf_tensor("Kfb", [128, H * 128], BF16))
        Qfb = _es.enter_context(nc.sbuf_tensor("Qfb", [128, H * 128], BF16))
        Cf1 = _es.enter_context(nc.sbuf_tensor("Cf1", [128, H * 128], BF16))
        Cf2 = _es.enter_context(nc.sbuf_tensor("Cf2", [128, H * 128], BF16))
        Cf4 = _es.enter_context(nc.sbuf_tensor("Cf4", [128, H * 128], BF16))
        Cf8 = _es.enter_context(nc.sbuf_tensor("Cf8", [128, H * 128], BF16))
        Cf16 = _es.enter_context(nc.sbuf_tensor("Cf16", [128, H * 128], BF16))
        Cf32 = _es.enter_context(nc.sbuf_tensor("Cf32", [128, H * 128], BF16))
        Dsl = _es.enter_context(nc.sbuf_tensor("Dsl", [128, 8 * 128], BF16))
        Mts = _es.enter_context(nc.sbuf_tensor("Mts", [128, H * 128], BF16))
        Usb = _es.enter_context(nc.sbuf_tensor("Usb", [128, H * HD], BF16))
        Usc = _es.enter_context(nc.sbuf_tensor("Usc", [128, H * HD], BF16))
        Ssl = _es.enter_context(nc.sbuf_tensor("Ssl", [128, H * HD], F32))
        Sbf = _es.enter_context(nc.sbuf_tensor("Sbf", [128, H * HD], BF16))
        ytile = _es.enter_context(nc.sbuf_tensor("ytile", [128, D], BF16))
        Escr = _es.enter_context(nc.sbuf_tensor("Escr", [128, 1024], F32))
        Epre = _es.enter_context(nc.sbuf_tensor("Epre", [128, 1024], BF16))
        scrA = _es.enter_context(nc.sbuf_tensor("scrA", [128, 1024], F32))
        stat32 = _es.enter_context(nc.sbuf_tensor("stat32", [128, 32], F32))
        st_hi = _es.enter_context(nc.sbuf_tensor("st_hi", [128, 32], BF16))
        st_lo = _es.enter_context(nc.sbuf_tensor("st_lo", [128, 32], BF16))
        Prow = _es.enter_context(nc.sbuf_tensor("Prow", [16, 128], F32))
        zero16 = _es.enter_context(nc.sbuf_tensor("zero16", [16, 128], F32))
        zaT = _es.enter_context(nc.sbuf_tensor("zaT", [16, 128], F32))
        zbT = _es.enter_context(nc.sbuf_tensor("zbT", [16, 128], F32))
        nlaT = _es.enter_context(nc.sbuf_tensor("nlaT", [16, 128], F32))
        nlbT = _es.enter_context(nc.sbuf_tensor("nlbT", [16, 128], F32))
        lkT = _es.enter_context(nc.sbuf_tensor("lkT", [16, 128], F32))
        lqT = _es.enter_context(nc.sbuf_tensor("lqT", [16, 128], F32))
        r_c1n = _es.enter_context(nc.sbuf_tensor("r_c1n", [16, 128], F32))
        r_s1b = _es.enter_context(nc.sbuf_tensor("r_s1b", [16, 128], F32))
        r_s1m = _es.enter_context(nc.sbuf_tensor("r_s1m", [16, 128], F32))
        r_rhsc = _es.enter_context(nc.sbuf_tensor("r_rhsc", [16, 128], BF16))
        r_g = _es.enter_context(nc.sbuf_tensor("r_g", [16, 128], BF16))
        r_rc = _es.enter_context(nc.sbuf_tensor("r_rc", [16, 128], BF16))
        r_sigb = _es.enter_context(nc.sbuf_tensor("r_sigb", [16, 128], BF16))
        r_gc = _es.enter_context(nc.sbuf_tensor("r_gc", [16, 128], BF16))
        r_gcpre = _es.enter_context(nc.sbuf_tensor("r_gcpre", [16, 128], F32))
        r_rcpre = _es.enter_context(nc.sbuf_tensor("r_rcpre", [16, 128], F32))
        c1n_hi = _es.enter_context(nc.sbuf_tensor("c1n_hi", [16, 128], BF16))
        c1n_lo = _es.enter_context(nc.sbuf_tensor("c1n_lo", [16, 128], BF16))
        s1b_hi = _es.enter_context(nc.sbuf_tensor("s1b_hi", [16, 128], BF16))
        s1b_lo = _es.enter_context(nc.sbuf_tensor("s1b_lo", [16, 128], BF16))
        s1m_hi = _es.enter_context(nc.sbuf_tensor("s1m_hi", [16, 128], BF16))
        s1m_lo = _es.enter_context(nc.sbuf_tensor("s1m_lo", [16, 128], BF16))
        tsplit = _es.enter_context(nc.sbuf_tensor("tsplit", [16, 128], F32))
        backTb = _es.enter_context(nc.sbuf_tensor("backTb", [128, 80], BF16))
        backTc = _es.enter_context(nc.sbuf_tensor("backTc", [128, 16], F32))
        ysq = _es.enter_context(nc.sbuf_tensor("ysq", [128, 1], F32))
        mcol = _es.enter_context(nc.sbuf_tensor("mcol", [128, 1], F32))
        vcol = _es.enter_context(nc.sbuf_tensor("vcol", [128, 1], F32))
        negm = _es.enter_context(nc.sbuf_tensor("negm", [128, 1], F32))
        gate = _es.enter_context(nc.sbuf_tensor("gate", [128, D], BF16))
        x2t = _es.enter_context(nc.sbuf_tensor("x2t", [128, D], F32))
        dstage = _es.enter_context(nc.sbuf_tensor("dstage", [128, 2048], F32)) if dbg else None
        pA = _es.enter_context(nc.psum_tensor("pA", [128, 1024], F32))
        pB = _es.enter_context(nc.psum_tensor("pB", [128, 1024], F32))
        pC = _es.enter_context(nc.psum_tensor("pC", [128, 1024], F32))
        pD = _es.enter_context(nc.psum_tensor("pD", [128, 512], F32))
        pT = _es.enter_context(nc.psum_tensor("pT", [128, 1024], BF16))
        if True:
            Cfs = {1: Cf1, 2: Cf2, 4: Cf4, 8: Cf8, 16: Cf16, 32: Cf32}

            def LW(g):
                P.wait(g, "st")
                P.wait(g, "dv")
                P.wait(g, "sc")
                P.wait(g, "pe")
                for dst, src, cols in [(Wk_s, WK, D), (Wq_s, WQ, D), (Wv_s, WV, D),
                                       (Wab_s, WAB, 2 * H)]:
                    sap = bass.AP(src, 0, [[cols, 128], [128 * cols, 8], [1, cols]])
                    P.inc(g.dma_start(dst[:, :, :], sap), "lg", 16)
                g.memset(Ktb[:, :], 0.0)
                g.memset(Qtb[:, :], 0.0)
                g.memset(Ssl[:, :], 0.0)
                g.memset(Sbf[:, :], 0.0)
                P.inc(g.memset(zero16[:, :], 0.0), "gp", 1)
            blk.gpsimd(LW)

            def dbgdump(slot, src_ap, cols, j):
                if not dbg or j != 0:
                    return

                def dv(v):
                    P.wait(v, "pe")
                    P.wait(v, "sc")
                    P.wait(v, "gp")
                    P.inc(v.tensor_copy(dstage[:, 0:cols], src_ap), "dv", 1)
                blk.vector(dv)

                def dsy(sy):
                    P.wait(sy, "dv")
                    P.inc(sy.dma_start(DBG.ap()[slot, :, 0:cols], dstage[:, 0:cols]),
                          "st", 16)
                blk.sync(dsy)

            R_RHSC, R_G, R_RC, R_SIGB, R_GC = 0, 16, 32, 48, 64

            for j in range(na1):
                xb = xbuf[j % 2]
                if j + 1 < NT:
                    def pref(sy, j=j):
                        P.wait(sy, "dv")
                        dma_x(P, sy, xbuf[(j + 1) % 2], X.ap(), j + 1)
                    blk.sync(pref)

                ld_need = P.cnt["ld"]

                def a1(v, j=j, xb=xb):
                    v.wait_ge(P.sems["ld"], ld_need)
                    if j == 0:
                        P.wait(v, "gp")
                    P.flush(v, v.tensor_reduce(mcol[:, 0:1], xb[:, :], AX.X, ALU.add),
                            "dv")
                    P.flush(v, v.tensor_scalar(mcol[:, 0:1], mcol[:, 0:1], 1.0 / D,
                                               None, ALU.mult), "dv")
                    P.flush(v, v.tensor_scalar(negm[:, 0:1], mcol[:, 0:1], -1.0, None,
                                               ALU.mult), "dv")
                blk.vector(a1)

                def a2s(sc, xb=xb):
                    P.wait(sc, "dv")
                    P.flush(sc, sc.activation(scrA[:, :], xb[:, :], ACT.Square,
                                               bias=negm[:, 0:1], scale=onecol[:, 0:1],
                                               accum_out=vcol[:, 0:1]), "sc")
                blk.scalar(a2s)

                def a2v(v):
                    P.wait(v, "sc")
                    P.flush(v, v.tensor_scalar(vcol[:, 0:1], vcol[:, 0:1], 1.0 / D, 1e-5,
                                               ALU.mult, ALU.add), "dv")
                blk.vector(a2v)

                def a2s2(sc):
                    P.wait(sc, "dv")
                    P.flush(sc, sc.activation(vcol[:, 0:1], vcol[:, 0:1], ACT.Ln,
                                              bias=zcol[:, 0:1], scale=onecol[:, 0:1]),
                            "sc")
                blk.scalar(a2s2)

                def a2v2(v):
                    P.wait(v, "sc")
                    P.flush(v, v.tensor_scalar(vcol[:, 0:1], vcol[:, 0:1], -0.5, None,
                                               ALU.mult), "dv")
                blk.vector(a2v2)

                def a2s3(sc):
                    P.wait(sc, "dv")
                    P.flush(sc, sc.activation(vcol[:, 0:1], vcol[:, 0:1], ACT.Exp,
                                              bias=zcol[:, 0:1], scale=onecol[:, 0:1]),
                            "sc")
                blk.scalar(a2s3)

                def a3v(v, j=j, xb=xb):
                    P.wait(v, "sc")
                    P.inc(v.tensor_scalar(xln[:, :], xb[:, :], mcol[:, 0:1], vcol[:, 0:1],
                                          ALU.subtract, ALU.mult), "dv", 1)
                    v.tensor_copy(LNm[:, j:j + 1], mcol[:, 0:1])
                    P.inc(v.tensor_copy(LNr[:, j:j + 1], vcol[:, 0:1]), "dv", 1)
                blk.vector(a3v)

                dbgdump(0, xln[:, :], D, j)
                def a4pe(te):
                    P.wait(te, "dv")
                    P.wait(te, "sc")
                    for p in range(8):
                        te.matmul(pT[:, p * 128:(p + 1) * 128], xln[:, p * 128:(p + 1) * 128],
                                  cidb[:, :], start=True, stop=True, is_transpose=True)
                    P.inc(te.last, "pe", 1)
                blkT(a4pe)

                def a4gp(sc):
                    P.wait(sc, "pe")
                    P.inc(sc.activation(xlnT[:, :], pT[:, :], ACT.Copy,
                                        bias=0.0, scale=onecol[:, 0:1]), "sc", 1)
                blk.scalar(a4gp)

                if A1STAGE < 2:
                    continue
                def a5pe(te):
                    P.wait(te, "sc")
                    P.wait(te, "gp")
                    P.wait(te, "lg")
                    P.wait(te, "dv")
                    P.wait(te, "sc")
                    for p in range(8):
                        for ps, Wsb in [(pA, Wk_s), (pB, Wq_s), (pC, Wv_s)]:
                            te.matmul(ps[:, 0:512], xlnT[:, p * 128:(p + 1) * 128],
                                      Wsb[:, p, 0:512], start=(p == 0), stop=(p == 7))
                            te.matmul(ps[:, 512:1024], xlnT[:, p * 128:(p + 1) * 128],
                                      Wsb[:, p, 512:1024], start=(p == 0), stop=(p == 7))
                        te.matmul(pD[0:16, 0:128], Wab_s[:, p, 0:16],
                                  xlnT[:, p * 128:(p + 1) * 128],
                                  start=(p == 0), stop=False)
                        te.matmul(pD[0:16, 128:256], Wab_s[:, p, 16:32],
                                  xlnT[:, p * 128:(p + 1) * 128],
                                  start=False, stop=(p == 7))
                    P.inc(te.last, "pe", 1)
                blkT(a5pe)
                ev_proj = P.cnt["pe"]

                def a6v(v):
                    P.wait(v, "pe", ev_proj)
                    P.inc(v.tensor_copy(Ktok[:, :], pA[:, :]), "dv", 1)
                blk.vector(a6v)

                def a6g(v):
                    P.wait(v, "pe", ev_proj)
                    P.inc(v.tensor_copy(Qtok[:, :], pB[:, :]), "dv", 1)
                    P.inc(v.tensor_copy(Vtok[:, :], pC[:, :]), "dv", 1)
                blk.vector(a6g)

                def a6s(sc):
                    P.wait(sc, "pe", ev_proj)
                    P.wait(sc, "dv")
                    P.inc(sc.activation(scrA[:, :], pA[:, :], ACT.Square, bias=zcol[:, 0:1], scale=onecol[:, 0:1]), "sc", 1)
                blk.scalar(a6s)

                def a7v(v):
                    P.wait(v, "sc")
                    P.flush(v, v.tensor_reduce(
                        stat32[:, 0:16].rearrange("p (a b) -> p a b", a=16, b=1),
                        scrA[:, :].rearrange("p (a b) -> p a b", a=16, b=64),
                        AX.X, ALU.add), "dv")
                blk.vector(a7v)

                def a7s(sc):
                    P.wait(sc, "dv")
                    P.inc(sc.activation(scrA[:, :], pB[:, :], ACT.Square, bias=zcol[:, 0:1], scale=onecol[:, 0:1]), "sc", 1)
                blk.scalar(a7s)

                def a8v(v):
                    P.wait(v, "sc")
                    P.flush(v, v.tensor_reduce(
                        stat32[:, 16:32].rearrange("p (a b) -> p a b", a=16, b=1),
                        scrA[:, :].rearrange("p (a b) -> p a b", a=16, b=64),
                        AX.X, ALU.add), "dv")
                    P.inc(v.tensor_copy(Sbf[:, :], Ssl[:, :]), "dv", 1)
                blk.vector(a8v)

                dbgdump(1, Ktok[:, :], D, j)
                dbgdump(2, Vtok[:, :], D, j)
                if A1STAGE < 3:
                    continue
                def a10s(sc):
                    P.wait(sc, "pe")
                    P.wait(sc, "dv")
                    P.inc(sc.activation(zaT[:, :], pD[0:16, 0:128], ACT.Exp,
                                        bias=nbab_a[:, 0:1], scale=nonecol[0:16, 0:1]),
                          "sc", 1)
                    P.inc(sc.activation(zbT[:, :], pD[0:16, 128:256], ACT.Exp,
                                        bias=nbab_b[:, 0:1], scale=nonecol[0:16, 0:1]),
                          "sc", 1)
                blk.scalar(a10s)

                def a9v(v):
                    P.wait(v, "sc")
                    P.flush(v, v.tensor_copy(st_hi[:, :], stat32[:, :]), "dv")
                    P.flush(v, v.tensor_sub(st_lo[:, :], stat32[:, :], st_hi[:, :]),
                            "dv")
                    P.inc(v.tensor_scalar(zaT[:, :], zaT[:, :], 1.0, None, ALU.add),
                          "dv", 1)
                    P.inc(v.tensor_scalar(zbT[:, :], zbT[:, :], 1.0, None, ALU.add),
                          "dv", 1)
                blk.vector(a9v)

                def a9pe(te):
                    P.wait(te, "dv")
                    P.wait(te, "sc")
                    te.matmul(pT[0:16, 0:128], st_hi[:, 0:16], cidb[:, :],
                              start=True, stop=True, is_transpose=True)
                    te.matmul(pT[0:16, 128:256], st_lo[:, 0:16], cidb[:, :],
                              start=True, stop=True, is_transpose=True)
                    te.matmul(pT[0:16, 256:384], st_hi[:, 16:32], cidb[:, :],
                              start=True, stop=True, is_transpose=True)
                    te.matmul(pT[0:16, 384:512], st_lo[:, 16:32], cidb[:, :],
                              start=True, stop=True, is_transpose=True)
                    P.inc(te.last, "pe", 1)
                blkT(a9pe)

                def a9v2(v):
                    P.wait(v, "pe")
                    P.flush(v, v.tensor_copy(lkT[:, :], pT[0:16, 0:128]), "dv")
                    P.flush(v, v.tensor_add(lkT[:, :], lkT[:, :], pT[0:16, 128:256]),
                            "dv")
                    P.flush(v, v.tensor_copy(lqT[:, :], pT[0:16, 256:384]), "dv")
                    P.flush(v, v.tensor_add(lqT[:, :], lqT[:, :], pT[0:16, 384:512]),
                            "dv")
                    P.flush(v, v.tensor_scalar(lkT[:, :], lkT[:, :], 1e-24, None,
                                               ALU.add), "dv")
                    P.flush(v, v.tensor_scalar(lqT[:, :], lqT[:, :], 1e-24, None,
                                               ALU.add), "dv")
                blk.vector(a9v2)

                if dbg and j == 0:
                    def dpre(v):
                        P.wait(v, "pe")
                        v.tensor_copy(dstage[0:128, 0:32], stat32[:, :])
                        v.tensor_copy(dstage[0:128, 32:64], st_hi[:, :])
                        v.tensor_copy(dstage[0:128, 64:96], st_lo[:, :])
                        v.tensor_copy(dstage[0:16, 96:224], lkT[:, :])
                        v.tensor_copy(dstage[0:16, 224:352], lqT[:, :])
                        v.tensor_copy(dstage[0:16, 352:480], pT[0:16, 0:128])
                        P.inc(v.tensor_copy(dstage[0:16, 480:608], pT[0:16, 128:256]),
                              "dv", 1)
                    blk.vector(dpre)

                    def dpre2(sy):
                        P.wait(sy, "dv")
                        P.inc(sy.dma_start(DBG.ap()[3], dstage[:, :]), "st", 16)
                    blk.sync(dpre2)

                def a11s(sc):
                    P.wait(sc, "dv")
                    z16, o16 = zcol[0:16, 0:1], onecol[0:16, 0:1]
                    P.flush(sc, sc.activation(nlaT[:, :], zaT[:, :], ACT.Ln, bias=z16,
                                               scale=o16), "sc")
                    P.flush(sc, sc.activation(nlbT[:, :], zbT[:, :], ACT.Ln, bias=z16,
                                              scale=o16), "sc")
                    P.flush(sc, sc.activation(lkT[:, :], lkT[:, :], ACT.Ln, bias=z16,
                                              scale=o16), "sc")
                    P.flush(sc, sc.activation(lqT[:, :], lqT[:, :], ACT.Ln, bias=z16,
                                              scale=o16), "sc")
                blk.scalar(a11s)

                if A1STAGE < 4:
                    continue

                def a12v(v):
                    P.wait(v, "sc")
                    P.flush(v, v.tensor_tensor_scan(Prow[:, 0:64], nlaT[:, 0:64],
                                               zero16[:, 0:64], 0.0, ALU.add, ALU.add), "dv")
                    P.flush(v, v.tensor_tensor_scan(Prow[:, 64:128], nlaT[:, 64:128],
                                               zero16[:, 64:128], 0.0, ALU.add, ALU.add),
                          "dv")
                    P.flush(v, v.scalar_tensor_tensor(r_c1n[:, :], lkT[:, :], -0.5,
                                                 Prow[:, :], ALU.mult, ALU.add), "dv")
                    P.flush(v, v.tensor_copy(c1n_hi[:, :], r_c1n[:, :]), "dv")
                    P.flush(v, v.tensor_sub(c1n_lo[:, :], r_c1n[:, :], c1n_hi[:, :]), "dv")
                    P.flush(v, v.tensor_sub(r_s1b[:, :], nlaT[:, :], nlbT[:, :]), "dv")
                    P.flush(v, v.tensor_sub(r_s1b[:, :], r_s1b[:, :], Prow[:, :]), "dv")
                    P.flush(v, v.scalar_tensor_tensor(r_s1b[:, :], lkT[:, :], -0.5,
                                                 r_s1b[:, :], ALU.mult, ALU.add), "dv")
                    P.flush(v, v.tensor_copy(s1b_hi[:, :], r_s1b[:, :]), "dv")
                    P.flush(v, v.tensor_sub(s1b_lo[:, :], r_s1b[:, :], s1b_hi[:, :]), "dv")
                    P.flush(v, v.scalar_tensor_tensor(r_s1m[:, :], lqT[:, :], -0.5,
                                                 Prow[:, :], ALU.mult, ALU.subtract),
                          "dv")
                    P.flush(v, v.tensor_copy(s1m_hi[:, :], r_s1m[:, :]), "dv")
                    P.flush(v, v.tensor_sub(s1m_lo[:, :], r_s1m[:, :], s1m_hi[:, :]), "dv")
                    P.flush(v, v.tensor_scalar(r_rcpre[:, 0:64], r_c1n[:, 0:64],
                                          Prow[:, 63:64], None, ALU.subtract), "dv")
                    P.flush(v, v.tensor_scalar(r_rcpre[:, 64:128], r_c1n[:, 64:128],
                                          Prow[:, 127:128], None, ALU.subtract), "dv")
                    P.flush(v, v.tensor_scalar(r_gcpre[:, 0:64], zero16[:, 0:64],
                                          Prow[:, 63:64], None, ALU.subtract), "dv")
                    P.flush(v, v.tensor_scalar(r_gcpre[:, 64:128], zero16[:, 64:128],
                                          Prow[:, 127:128], None, ALU.subtract), "dv")
                blk.vector(a12v)

                def a13s(sc):
                    P.wait(sc, "dv")
                    z16, o16, n16 = zcol[0:16, 0:1], onecol[0:16, 0:1], nonecol[0:16, 0:1]
                    P.flush(sc, sc.activation(r_rhsc[:, :], r_s1b[:, :], ACT.Exp,
                                        bias=z16, scale=o16), "sc")
                    P.flush(sc, sc.activation(r_g[:, :], r_s1m[:, :], ACT.Exp,
                                        bias=z16, scale=o16), "sc")
                    P.flush(sc, sc.activation(r_rc[:, :], r_rcpre[:, :], ACT.Exp,
                                        bias=z16, scale=o16), "sc")
                    P.flush(sc, sc.activation(r_sigb[:, :], nlbT[:, :], ACT.Exp,
                                        bias=z16, scale=n16), "sc")
                    P.flush(sc, sc.activation(r_gc[:, :], r_gcpre[:, :], ACT.Exp,
                                        bias=z16, scale=o16), "sc")
                blk.scalar(a13s)

                def a14pe(te):
                    P.wait(te, "sc")
                    for i, cls in enumerate([r_rhsc, r_g, r_rc, r_sigb, r_gc,
                                             c1n_hi, c1n_lo]):
                        te.matmul(pT[:, 512 + i * 16:512 + (i + 1) * 16], cls[:, :],
                                  cidb[0:16, 0:16], start=True, stop=True,
                                  is_transpose=True)
                    P.inc(te.last, "pe", 1)
                blkT(a14pe)

                def a14v(v):
                    P.wait(v, "pe")
                    P.flush(v, v.tensor_copy(backTb[:, :], pT[:, 512:592]), "dv")
                    P.flush(v, v.tensor_copy(backTc[:, :], pT[:, 592:608]), "dv")
                    P.flush(v, v.tensor_add(backTc[:, :], backTc[:, :], pT[:, 608:624]),
                            "dv")
                blk.vector(a14v)

                if A1STAGE < 5:
                    continue
                def a15v(v):
                    P.wait(v, "pe")
                    ps_k, ps_kt = pstep(Ktok), pstep(Ktb)
                    P.inc(v.tensor_copy(
                        bass.AP(Ktb, 0, [[ps_kt, 64], [128, H], [1, 64]]),
                        bass.AP(Ktok, 0, [[ps_k, 64], [64, H], [1, 64]])), "dv", 1)
                    ps_q, ps_qt = pstep(Qtok), pstep(Qtb)
                    P.inc(v.tensor_copy(
                        bass.AP(Qtb, 0, [[ps_qt, 64], [128, H], [1, 64]]),
                        bass.AP(Qtok, 0, [[ps_q, 64], [64, H], [1, 64]])), "dv", 1)
                blk.vector(a15v)

                def a15g(g):
                    P.wait(g, "dv")
                    ps_k, ps_kt = pstep(Ktok), pstep(Ktb)
                    P.inc(g.tensor_copy(
                        bass.AP(Ktb, 64 * ps_kt + 64, [[ps_kt, 64], [128, H], [1, 64]]),
                        bass.AP(Ktok, 64 * ps_k, [[ps_k, 64], [64, H], [1, 64]])), "gp", 1)
                    ps_q, ps_qt = pstep(Qtok), pstep(Qtb)
                    P.inc(g.tensor_copy(
                        bass.AP(Qtb, 64 * ps_qt + 64, [[ps_qt, 64], [128, H], [1, 64]]),
                        bass.AP(Qtok, 64 * ps_q, [[ps_q, 64], [64, H], [1, 64]])), "gp", 1)
                blk.gpsimd(a15g)

                for (tb, fb) in [(Ktb, Kfb), (Qtb, Qfb)]:
                    for w in range(2):
                        def a16pe(te, tb=tb, w=w):
                            P.wait(te, "gp")
                            P.wait(te, "sc")
                            P.wait(te, "dv")
                            for b in range(8):
                                hh = w * 8 + b
                                te.matmul(pT[:, b * 128:(b + 1) * 128],
                                          tb[:, hh * 128:(hh + 1) * 128],
                                          cidb[:, :], start=True, stop=True,
                                          is_transpose=True)
                            P.inc(te.last, "pe", 1)
                        blkT(a16pe)

                        def a16g(sc, fb=fb, w=w):
                            P.wait(sc, "pe")
                            P.inc(sc.activation(fb[:, w * 1024:(w + 1) * 1024], pT[:, :],
                                                ACT.Copy, bias=0.0,
                                                scale=onecol[:, 0:1]), "sc", 1)
                        blk.scalar(a16g)

                if A1STAGE < 6:
                    continue
                for half in range(2):
                    h0 = half * 8

                    def g1(te, h0=h0):
                        P.wait(te, "gp")
                        P.wait(te, "dv")
                        P.wait(te, "sc")
                        for hh in range(8):
                            ksl = Kfb[:, (h0 + hh) * 128:(h0 + hh + 1) * 128]
                            te.matmul(pA[:, hh * 128:(hh + 1) * 128], ksl, ksl,
                                      start=True, stop=True)
                            te.matmul(pC[:, hh * 128:(hh + 1) * 128], ksl,
                                      Qfb[:, (h0 + hh) * 128:(h0 + hh + 1) * 128],
                                      start=True, stop=True)
                            te.matmul(pD[:, hh * 64:(hh + 1) * 64], ksl,
                                      Sbf[:, (h0 + hh) * 64:(h0 + hh + 1) * 64],
                                      start=True, stop=True)
                        for bank in range(2):
                            for q in range(4):
                                hh = bank * 4 + q
                                oh = crep[:, (h0 + hh) * 128:(h0 + hh + 1) * 128]
                                te.matmul(pB[:, hh * 128:(hh + 1) * 128], oh,
                                          s1b_hi[:, :], start=(q == 0), stop=False)
                                te.matmul(pB[:, hh * 128:(hh + 1) * 128], oh,
                                          s1b_lo[:, :], start=False, stop=False)
                            te.matmul(pB[:, bank * 512:(bank + 1) * 512], cidb[:, :],
                                      maskB4[:, :], start=False, stop=True)
                        P.inc(te.last, "pe", 1)
                    blkT(g1)

                    def e1(v, h0=h0):
                        P.wait(v, "pe")
                        cb = bass.AP(backTc, h0,
                                     [[pstep(backTc), 128], [1, 8], [0, 128]])
                        P.inc(v.tensor_tensor(
                            Escr[:, :].rearrange("p (a b) -> p a b", a=8, b=128),
                            pB[:, :].rearrange("p (a b) -> p a b", a=8, b=128),
                            cb, ALU.add), "dv", 1)
                    blk.vector(e1)

                    def e1s(sc):
                        P.wait(sc, "dv")
                        P.inc(sc.activation(Epre[:, :], Escr[:, :], ACT.Exp, bias=zcol[:, 0:1], scale=onecol[:, 0:1]), "sc", 1)
                    blk.scalar(e1s)

                    def e1m(v, h0=h0):
                        P.wait(v, "sc")
                        P.inc(v.tensor_tensor(Cf1[:, h0 * 128:(h0 + 8) * 128],
                                              Epre[:, :], pA[:, :], ALU.mult), "dv", 1)
                        rb = bass.AP(backTb, R_RHSC + h0,
                                     [[pstep(backTb), 128], [1, 8], [0, 64]])
                        P.inc(v.tensor_tensor(
                            scrA[:, 0:512].rearrange("p (a b) -> p a b", a=8, b=64),
                            pD[:, 0:512].rearrange("p (a b) -> p a b", a=8, b=64),
                            rb, ALU.mult), "dv", 1)
                        sb = bass.AP(backTb, R_SIGB + h0,
                                     [[pstep(backTb), 128], [1, 8], [0, 64]])
                        P.inc(v.tensor_tensor(
                            scrA[:, 512:1024].rearrange("p (a b) -> p a b", a=8, b=64),
                            Vtok[:, h0 * 64:(h0 + 8) * 64].rearrange(
                                "p (a b) -> p a b", a=8, b=64),
                            sb, ALU.mult), "dv", 1)
                        P.inc(v.tensor_sub(Usb[:, h0 * 64:(h0 + 8) * 64],
                                           scrA[:, 512:1024], scrA[:, 0:512]), "dv", 1)
                    blk.vector(e1m)

                    if A1STAGE < 7:
                        continue
                    if half == 0:
                        dbgdump(4, Cf1[:, 0:1024], 1024, j)
                        dbgdump(5, Usb[:, 0:512], 512, j)
                    for k in [1, 2, 4, 8, 16]:
                        def sq1(te, k=k, h0=h0):
                            P.wait(te, "dv")
                            P.wait(te, "sc")
                            for b in range(8):
                                te.matmul(pT[:, b * 128:(b + 1) * 128],
                                          Cfs[k][:, (h0 + b) * 128:(h0 + b + 1) * 128],
                                          cidb[:, :], start=True, stop=True,
                                          is_transpose=True)
                            P.inc(te.last, "pe", 1)
                        blkT(sq1)

                        def sq2(sc):
                            P.wait(sc, "pe")
                            P.inc(sc.activation(Dsl[:, :], pT[:, :], ACT.Copy,
                                                bias=0.0, scale=onecol[:, 0:1]), "sc", 1)
                        blk.scalar(sq2)

                        def sq3(te, k=k, h0=h0):
                            P.wait(te, "sc")
                            P.wait(te, "dv")
                            for b in range(8):
                                te.matmul(pA[:, b * 128:(b + 1) * 128],
                                          Dsl[:, b * 128:(b + 1) * 128],
                                          Cfs[k][:, (h0 + b) * 128:(h0 + b + 1) * 128],
                                          start=True, stop=True)
                            P.inc(te.last, "pe", 1)
                        blkT(sq3)

                        def sq4(v, k=k, h0=h0):
                            P.wait(v, "pe")
                            P.inc(v.tensor_copy(Cfs[2 * k][:, h0 * 128:(h0 + 8) * 128],
                                                pA[:, :]), "dv", 1)
                        blk.vector(sq4)

                    for k in [32, 16, 8, 4, 2, 1]:
                        def ap1(te, k=k, h0=h0):
                            P.wait(te, "dv")
                            for b in range(8):
                                csl = Cfs[k][:, (h0 + b) * 128:(h0 + b + 1) * 128]
                                te.matmul(pD[:, b * 64:(b + 1) * 64], csl,
                                          Usb[:, (h0 + b) * 64:(h0 + b + 1) * 64],
                                          start=True, stop=True)
                            P.inc(te.last, "pe", 1)
                        blkT(ap1)

                        def ap2(v, k=k, h0=h0):
                            P.wait(v, "pe")
                            op = ALU.subtract if k == 1 else ALU.add
                            P.inc(v.tensor_tensor(Usb[:, h0 * 64:(h0 + 8) * 64],
                                                  Usb[:, h0 * 64:(h0 + 8) * 64],
                                                  pD[:, 0:512], op), "dv", 1)
                        blk.vector(ap2)

                    if A1STAGE < 8:
                        continue
                    if half == 0:
                        dbgdump(6, Usb[:, 0:512], 512, j)
                    def m1(te, h0=h0):
                        P.wait(te, "dv")
                        for bank in range(2):
                            for q in range(4):
                                hh = bank * 4 + q
                                oh = crep[:, (h0 + hh) * 128:(h0 + hh + 1) * 128]
                                te.matmul(pB[:, hh * 128:(hh + 1) * 128], oh,
                                          s1m_hi[:, :], start=(q == 0), stop=False)
                                te.matmul(pB[:, hh * 128:(hh + 1) * 128], oh,
                                          s1m_lo[:, :], start=False, stop=False)
                            te.matmul(pB[:, bank * 512:(bank + 1) * 512], cidb[:, :],
                                      maskM4[:, :], start=False, stop=True)
                        P.inc(te.last, "pe", 1)
                    blkT(m1)

                    def m2(v, h0=h0):
                        P.wait(v, "pe")
                        cb = bass.AP(backTc, h0,
                                     [[pstep(backTc), 128], [1, 8], [0, 128]])
                        P.inc(v.tensor_tensor(
                            Escr[:, :].rearrange("p (a b) -> p a b", a=8, b=128),
                            pB[:, :].rearrange("p (a b) -> p a b", a=8, b=128),
                            cb, ALU.add), "dv", 1)
                    blk.vector(m2)

                    def m3(sc):
                        P.wait(sc, "dv")
                        P.inc(sc.activation(Epre[:, :], Escr[:, :], ACT.Exp, bias=zcol[:, 0:1], scale=onecol[:, 0:1]), "sc", 1)
                    blk.scalar(m3)

                    def m4(v, h0=h0):
                        P.wait(v, "sc")
                        P.inc(v.tensor_tensor(Mts[:, h0 * 128:(h0 + 8) * 128],
                                              Epre[:, :], pC[:, :], ALU.mult), "dv", 1)
                        rc = bass.AP(backTb, R_RC + h0,
                                     [[pstep(backTb), 128], [1, 8], [0, 64]])
                        P.inc(v.tensor_tensor(
                            Usc[:, h0 * 64:(h0 + 8) * 64].rearrange(
                                "p (a b) -> p a b", a=8, b=64),
                            Usb[:, h0 * 64:(h0 + 8) * 64].rearrange(
                                "p (a b) -> p a b", a=8, b=64),
                            rc, ALU.mult), "dv", 1)
                    blk.vector(m4)

                    def o1(te, h0=h0):
                        P.wait(te, "dv")
                        for b in range(8):
                            msl = Mts[:, (h0 + b) * 128:(h0 + b + 1) * 128]
                            te.matmul(pD[:, b * 64:(b + 1) * 64], msl,
                                      Usb[:, (h0 + b) * 64:(h0 + b + 1) * 64],
                                      start=True, stop=True)
                        for b in range(8):
                            qsl = Qfb[:, (h0 + b) * 128:(h0 + b + 1) * 128]
                            te.matmul(pB[:, b * 64:(b + 1) * 64], qsl,
                                      Sbf[:, (h0 + b) * 64:(h0 + b + 1) * 64],
                                      start=True, stop=True)
                        P.inc(te.last, "pe", 1)
                    blkT(o1)

                    def o2(v, h0=h0):
                        P.wait(v, "pe")
                        gb = bass.AP(backTb, R_G + h0,
                                     [[pstep(backTb), 128], [1, 8], [0, 64]])
                        P.inc(v.tensor_tensor(
                            scrA[:, 0:512].rearrange("p (a b) -> p a b", a=8, b=64),
                            pB[:, 0:512].rearrange("p (a b) -> p a b", a=8, b=64),
                            gb, ALU.mult), "dv", 1)
                        P.inc(v.tensor_tensor(ytile[:, h0 * 64:(h0 + 8) * 64],
                                              scrA[:, 0:512], pD[:, 0:512], ALU.add),
                              "dv", 1)
                    blk.vector(o2)

                    def su1(te, h0=h0):
                        P.wait(te, "dv")
                        for b in range(8):
                            ksl = Ktb[:, (h0 + b) * 128:(h0 + b + 1) * 128]
                            te.matmul(pD[:, b * 64:(b + 1) * 64], ksl,
                                      Usc[:, (h0 + b) * 64:(h0 + b + 1) * 64],
                                      start=True, stop=True)
                        P.inc(te.last, "pe", 1)
                    blkT(su1)

                    def su2(v, h0=h0):
                        P.wait(v, "pe")
                        gc = bass.AP(backTb, R_GC + h0,
                                     [[pstep(backTb), 128], [1, 8], [0, 64]])
                        P.inc(v.tensor_tensor(
                            scrA[:, 0:512].rearrange("p (a b) -> p a b", a=8, b=64),
                            Ssl[:, h0 * 64:(h0 + 8) * 64].rearrange(
                                "p (a b) -> p a b", a=8, b=64),
                            gc, ALU.mult), "dv", 1)
                        P.inc(v.tensor_tensor(Ssl[:, h0 * 64:(h0 + 8) * 64],
                                              scrA[:, 0:512], pD[:, 0:512], ALU.add),
                              "dv", 1)
                    blk.vector(su2)

                dbgdump(7, ytile[:, :], D, j)
                def ys1(v, j=j):
                    P.flush(v, v.tensor_mul(scrA[:, :], ytile[:, :], ytile[:, :]),
                            "dv")
                    P.flush(v, v.tensor_reduce(ysq[:, 0:1], scrA[:, :], AX.X, ALU.add),
                            "dv")
                blk.vector(ys1)

                def ys1b(v):
                    P.flush(v, v.tensor_scalar(ysq[:, 0:1], ysq[:, 0:1], 1.0 / D, 1e-5,
                                               ALU.mult, ALU.add), "dv")
                blk.vector(ys1b)

                def ys2(sc, j=j):
                    P.wait(sc, "dv")
                    P.flush(sc, sc.activation(ysq[:, 0:1], ysq[:, 0:1], ACT.Ln,
                                               bias=zcol[:, 0:1], scale=onecol[:, 0:1]),
                            "sc")
                blk.scalar(ys2)

                def ys2b(v):
                    P.wait(v, "sc")
                    P.flush(v, v.tensor_scalar(ysq[:, 0:1], ysq[:, 0:1], -0.5, None,
                                               ALU.mult), "dv")
                blk.vector(ys2b)

                def ys2c(sc, j=j):
                    P.wait(sc, "dv")
                    P.flush(sc, sc.activation(RSTY[:, j:j + 1], ysq[:, 0:1], ACT.Exp,
                                               bias=zcol[:, 0:1], scale=onecol[:, 0:1]),
                            "sc")
                blk.scalar(ys2c)

                def ys3(sy, j=j):
                    P.wait(sy, "dv")
                    P.inc(sy.dma_start(YD.ap()[j], ytile[:, :]), "st", 16)
                blk.sync(ys3)

            def LW2(g):
                P.wait(g, "pe")
                for dst, src in [(Wk_s, WG), (Wq_s, WO)]:
                    sap = bass.AP(src, 0, [[D, 128], [128 * D, 8], [1, D]])
                    P.inc(g.dma_start(dst[:, :, :], sap), "lg", 16)
            blk.gpsimd(LW2)
            ld_w2 = P.cnt["lg"]

            def foldgn(v):
                v.wait_ge(P.sems["lg"], ld_w2)
                for p in range(8):
                    P.flush(v, v.tensor_scalar(Wq_s[:, p, :], Wq_s[:, p, :],
                                               gnsb[:, p:p + 1], None, ALU.mult), "dv")
            blk.vector(foldgn)

            for j in range(na2):
                xb = xbuf[j % 2]

                def b0(sy, j=j, xb=xb):
                    P.wait(sy, "dv")
                    P.wait(sy, "sc")
                    dma_x(P, sy, xb, X.ap(), j)
                    P.inc(sy.dma_start(ytile[:, :], YD.ap()[j]), "ld", 16)
                blk.sync(b0)
                ldj = P.cnt["ld"]

                def b1(v, j=j, xb=xb):
                    v.wait_ge(P.sems["ld"], ldj)
                    P.inc(v.tensor_scalar(xln[:, :], xb[:, :], LNm[:, j:j + 1],
                                          LNr[:, j:j + 1], ALU.subtract, ALU.mult),
                          "dv", 1)
                blk.vector(b1)

                def b2(te):
                    P.wait(te, "dv")
                    P.wait(te, "sc")
                    for p in range(8):
                        te.matmul(pT[:, p * 128:(p + 1) * 128],
                                  xln[:, p * 128:(p + 1) * 128],
                                  cidb[:, :], start=True, stop=True, is_transpose=True)
                    P.inc(te.last, "pe", 1)
                blkT(b2)

                def b3(sc):
                    P.wait(sc, "pe")
                    P.inc(sc.activation(xlnT[:, :], pT[:, :], ACT.Copy,
                                        bias=0.0, scale=onecol[:, 0:1]), "sc", 1)
                blk.scalar(b3)

                def b4(te):
                    P.wait(te, "lg")
                    P.wait(te, "sc")
                    for p in range(8):
                        te.matmul(pA[:, 0:512], xlnT[:, p * 128:(p + 1) * 128],
                                  Wk_s[:, p, 0:512], start=(p == 0), stop=(p == 7))
                        te.matmul(pA[:, 512:1024], xlnT[:, p * 128:(p + 1) * 128],
                                  Wk_s[:, p, 512:1024], start=(p == 0), stop=(p == 7))
                    P.inc(te.last, "pe", 1)
                blkT(b4)

                def b5(sc):
                    P.wait(sc, "pe")
                    P.inc(sc.activation(gate[:, :], pA[:, :], ACT.Silu, bias=zcol[:, 0:1], scale=onecol[:, 0:1]), "sc", 1)
                blk.scalar(b5)

                def b6(v):
                    P.wait(v, "sc")
                    P.inc(v.tensor_mul(xln[:, :], ytile[:, :], gate[:, :]), "dv", 1)
                blk.vector(b6)

                def b7(te):
                    P.wait(te, "dv")
                    P.wait(te, "sc")
                    for p in range(8):
                        te.matmul(pT[:, p * 128:(p + 1) * 128],
                                  xln[:, p * 128:(p + 1) * 128],
                                  cidb[:, :], start=True, stop=True, is_transpose=True)
                    P.inc(te.last, "pe", 1)
                blkT(b7)

                def b8(sc):
                    P.wait(sc, "pe")
                    P.inc(sc.activation(xlnT[:, :], pT[:, :], ACT.Copy,
                                        bias=0.0, scale=onecol[:, 0:1]), "sc", 1)
                blk.scalar(b8)

                def b9(te):
                    P.wait(te, "sc")
                    P.wait(te, "dv")
                    for p in range(8):
                        te.matmul(pC[:, 0:512], xlnT[:, p * 128:(p + 1) * 128],
                                  Wq_s[:, p, 0:512], start=(p == 0), stop=(p == 7))
                        te.matmul(pC[:, 512:1024], xlnT[:, p * 128:(p + 1) * 128],
                                  Wq_s[:, p, 512:1024], start=(p == 0), stop=(p == 7))
                    P.inc(te.last, "pe", 1)
                blkT(b9)

                def b10(v, j=j, xb=xb):
                    P.wait(v, "pe")
                    if dbg and j == 0:
                        P.inc(v.tensor_copy(dstage[:, 0:1024], pC[:, :]), "dv", 1)
                        P.inc(v.tensor_copy(dstage[:, 1024:1040], RSTY[:, 0:16]),
                              "dv", 1)
                    P.inc(v.scalar_tensor_tensor(x2t[:, :], pC[:, :], RSTY[:, j:j + 1],
                                                 xb[:, :], ALU.mult, ALU.add), "dv", 1)
                blk.vector(b10)
                if dbg and j == 0:
                    def db10(sy):
                        P.wait(sy, "dv")
                        P.inc(sy.dma_start(DBG.ap()[5], dstage[:, :]), "st", 16)
                    blk.sync(db10)

                def b11(sy, j=j):
                    P.wait(sy, "dv")
                    P.inc(sy.dma_start(X2D.ap()[j], x2t[:, :]), "st", 16)
                blk.sync(b11)

        _es.close()
        _esb = ExitStack()
        w1t_s = _esb.enter_context(nc.sbuf_tensor("w1t_s", [128, 8, HID], BF16))
        w2t_s = _esb.enter_context(nc.sbuf_tensor("w2t_s", [128, 32, D], BF16))
        x2b = _esb.enter_context(nc.sbuf_tensor("x2b", [128, D], F32))
        x2b2 = _esb.enter_context(nc.sbuf_tensor("x2b2", [128, D], F32))
        xqb = _esb.enter_context(nc.sbuf_tensor("xqb", [128, D], BF16))
        xqTb = _esb.enter_context(nc.sbuf_tensor("xqTb", [128, D], BF16))
        hbuf = _esb.enter_context(nc.sbuf_tensor("hbuf", [128, HID], F32))
        hqb = _esb.enter_context(nc.sbuf_tensor("hqb", [128, HID], BF16))
        hqT = _esb.enter_context(nc.sbuf_tensor("hqT", [128, HID], BF16))
        obuf = _esb.enter_context(nc.sbuf_tensor("obuf", [128, D], F32))
        bm = _esb.enter_context(nc.sbuf_tensor("bm", [128, 1], F32))
        bnm = _esb.enter_context(nc.sbuf_tensor("bnm", [128, 1], F32))
        bv = _esb.enter_context(nc.sbuf_tensor("bv", [128, 1], F32))
        bq = _esb.enter_context(nc.sbuf_tensor("bq", [128, 1], F32))
        chc = _esb.enter_context(nc.sbuf_tensor("chc", [128, 1], F32))
        c2c = _esb.enter_context(nc.sbuf_tensor("c2c", [128, 1], F32))
        bscr = _esb.enter_context(nc.sbuf_tensor("bscr", [128, 1024], F32))
        pH = _esb.enter_context(nc.psum_tensor("pH", [128, 1024], F32))
        pO = _esb.enter_context(nc.psum_tensor("pO", [128, 1024], F32))
        pTb = _esb.enter_context(nc.psum_tensor("pTb", [128, 1024], BF16))
        if True:
            def bl0(sy):
                P.wait(sy, "st")
                P.wait(sy, "dv")
                P.wait(sy, "sc")
                P.wait(sy, "pe")
                P.wait(sy, "gp")
                sap1 = bass.AP(W1T, 0, [[HID, 128], [128 * HID, 8], [1, HID]])
                P.inc(sy.dma_start(w1t_s[:, :, :], sap1), "ld", 16)
                sap2 = bass.AP(W2T, 0, [[D, 128], [128 * D, 32], [1, D]])
                P.inc(sy.dma_start(w2t_s[:, :, :], sap2), "ld", 16)
            blk.sync(bl0)

            for j in range(nbpre):
                def c0(sy, j=j):
                    P.wait(sy, "sc")
                    P.wait(sy, "dv")
                    if j == 0:
                        P.wait(sy, "st")
                    P.inc(sy.dma_start((x2b if j % 2 == 0 else x2b2)[:, :], X2D.ap()[j]),
                          "ld", 16)
                blk.sync(c0)
                ldj = P.cnt["ld"]

                def c1(v, j=j):
                    v.wait_ge(P.sems["ld"], ldj)
                    buf = x2b if j % 2 == 0 else x2b2
                    P.flush(v, v.tensor_reduce(bm[:, 0:1], buf[:, :], AX.X, ALU.add), "dv")
                    P.flush(v, v.tensor_scalar(bm[:, 0:1], bm[:, 0:1], 1.0 / D, None,
                                          ALU.mult), "dv")
                    P.flush(v, v.tensor_scalar(bnm[:, 0:1], bm[:, 0:1], -1.0, None, ALU.mult),
                          "dv")
                    v.tensor_copy(MEAN2[:, j:j + 1], bm[:, 0:1])
                    P.flush(v, v.tensor_scalar(bscr[:, :], buf[:, :], bm[:, 0:1], None,
                                          ALU.subtract), "dv")
                    P.flush(v, v.tensor_reduce(bq[:, 0:1], bscr[:, :], AX.X, ALU.max,
                                          apply_absolute_value=True), "dv")
                blk.vector(c1)

                def c2(sc, j=j):
                    P.wait(sc, "dv")
                    buf = x2b if j % 2 == 0 else x2b2
                    P.flush(sc, sc.activation(bscr[:, :], buf[:, :], ACT.Square,
                                               bias=bnm[:, 0:1], scale=onecol[:, 0:1],
                                               accum_out=bv[:, 0:1]), "sc")
                blk.scalar(c2)

                def c2v(v):
                    P.wait(v, "sc")
                    P.flush(v, v.tensor_scalar(bv[:, 0:1], bv[:, 0:1], 1.0 / D,
                                               1e-5, ALU.mult, ALU.add), "dv")
                blk.vector(c2v)

                def c2s(sc):
                    P.wait(sc, "dv")
                    P.flush(sc, sc.activation(bv[:, 0:1], bv[:, 0:1], ACT.Ln,
                                               bias=zcol[:, 0:1], scale=onecol[:, 0:1]),
                            "sc")
                blk.scalar(c2s)

                def c2v2(v):
                    P.wait(v, "sc")
                    P.flush(v, v.tensor_scalar(bv[:, 0:1], bv[:, 0:1], -0.5, None,
                                               ALU.mult), "dv")
                blk.vector(c2v2)

                def c2s2(sc):
                    P.wait(sc, "dv")
                    P.flush(sc, sc.activation(bv[:, 0:1], bv[:, 0:1], ACT.Exp,
                                               bias=zcol[:, 0:1], scale=onecol[:, 0:1]),
                            "sc")
                blk.scalar(c2s2)

                def c3(v, j=j):
                    P.wait(v, "sc")
                    P.flush(v, v.tensor_scalar(SCQ[:, j:j + 1], bq[:, 0:1], bv[:, 0:1], 1e-5,
                                          ALU.mult, ALU.max), "dv")
                    P.flush(v, v.reciprocal(bq[:, 0:1], SCQ[:, j:j + 1]), "dv")
                    P.flush(v, v.tensor_scalar(bq[:, 0:1], bq[:, 0:1], 127.0, None, ALU.mult),
                          "dv")
                    P.flush(v, v.tensor_mul(CC2[:, j:j + 1], bv[:, 0:1], bq[:, 0:1]), "dv")
                blk.vector(c3)

            ld_w = P.cnt["ld"]
            for j in range(nbmain):
                def d0(sy, j=j):
                    P.wait(sy, "dv")
                    P.inc(sy.dma_start((x2b if j % 2 == 0 else x2b2)[:, :], X2D.ap()[j]),
                          "ld", 16)
                blk.sync(d0)
                ldj = P.cnt["ld"]

                def d1(v, j=j):
                    v.wait_ge(P.sems["ld"], max(ldj, ld_w))
                    buf = x2b if j % 2 == 0 else x2b2
                    P.flush(v, v.tensor_scalar(bscr[:, :], buf[:, :], MEAN2[:, j:j + 1],
                                          CC2[:, j:j + 1], ALU.subtract, ALU.mult),
                          "dv")
                    P.flush(v, v.tensor_scalar(xqb[:, :], bscr[:, :], MAGIC, -MAGIC,
                                          ALU.add, ALU.add), "dv")
                    P.flush(v, v.tensor_scalar(chc[:, 0:1], SCQ[:, j:j + 1], s1col[:, 0:1],
                                          1.0 / 127.0, ALU.mult, ALU.mult), "dv")
                blk.vector(d1)

                def d2(te):
                    P.wait(te, "dv")
                    P.wait(te, "sc")
                    for p in range(8):
                        te.matmul(pTb[:, p * 128:(p + 1) * 128],
                                  xqb[:, p * 128:(p + 1) * 128],
                                  cidb[:, :], start=True, stop=True, is_transpose=True)
                    P.inc(te.last, "pe", 1)
                blkT(d2)

                def d3(sc):
                    P.wait(sc, "pe")
                    P.inc(sc.activation(xqTb[:, :], pTb[:, :], ACT.Copy,
                                        bias=0.0, scale=onecol[:, 0:1]), "sc", 1)
                blk.scalar(d3)
                ev_xqT = P.cnt["sc"]

                for q in range(4):
                    def d4(te, q=q):
                        P.wait(te, "sc", ev_xqT)
                        P.wait(te, "sc")
                        for p in range(8):
                            te.matmul(pH[:, 0:512], xqTb[:, p * 128:(p + 1) * 128],
                                      w1t_s[:, p, q * 1024:q * 1024 + 512],
                                      start=(p == 0), stop=(p == 7))
                            te.matmul(pH[:, 512:1024], xqTb[:, p * 128:(p + 1) * 128],
                                      w1t_s[:, p, q * 1024 + 512:(q + 1) * 1024],
                                      start=(p == 0), stop=(p == 7))
                        P.inc(te.last, "pe", 1)
                    blkT(d4)

                    def d5(sc, q=q):
                        P.wait(sc, "pe")
                        P.inc(sc.activation(hbuf[:, q * 1024:(q + 1) * 1024], pH[:, :],
                                            ACT.Gelu, bias=zcol[:, 0:1], scale=chc[:, 0:1]),
                              "sc", 1)
                    blk.scalar(d5)

                def d6(v, j=j):
                    P.wait(v, "sc")
                    P.flush(v, v.tensor_reduce(bq[:, 0:1], hbuf[:, :], AX.X, ALU.max,
                                          apply_absolute_value=True), "dv")
                    P.flush(v, v.tensor_scalar(bq[:, 0:1], bq[:, 0:1], 1e-5, None, ALU.max),
                          "dv")
                    P.flush(v, v.tensor_scalar(c2c[:, 0:1], bq[:, 0:1], s2col[:, 0:1],
                                          1.0 / 127.0, ALU.mult, ALU.mult), "dv")
                    P.flush(v, v.reciprocal(bq[:, 0:1], bq[:, 0:1]), "dv")
                    P.flush(v, v.tensor_scalar(bq[:, 0:1], bq[:, 0:1], 127.0, None, ALU.mult),
                          "dv")
                    P.flush(v, v.tensor_scalar(bscr[:, :], hbuf[:, 0:1024], bq[:, 0:1],
                                          None, ALU.mult), "dv")
                    P.flush(v, v.tensor_scalar(hqb[:, 0:1024], bscr[:, :], MAGIC, -MAGIC,
                                          ALU.add, ALU.add), "dv")
                blk.vector(d6)

                def d6g(g):
                    P.wait(g, "dv")
                    for q in range(1, 4):
                        g.tensor_scalar(bscr[:, :] if False else hqb[:, q * 1024:(q + 1) * 1024],
                                        hbuf[:, q * 1024:(q + 1) * 1024], bq[:, 0:1],
                                        None, ALU.mult)
                        g.tensor_scalar(hqb[:, q * 1024:(q + 1) * 1024],
                                        hqb[:, q * 1024:(q + 1) * 1024], MAGIC, -MAGIC,
                                        ALU.add, ALU.add)
                    P.inc(g.engine_nop(), "gp", 1)
                blk.gpsimd(d6g)

                for w in range(4):
                    def d7(te, w=w):
                        P.wait(te, "gp")
                        if w > 0:
                            P.wait(te, "dv")
                        for b in range(8):
                            te.matmul(pTb[:, b * 128:(b + 1) * 128],
                                      hqb[:, (w * 8 + b) * 128:(w * 8 + b + 1) * 128],
                                      cidb[:, :], start=True, stop=True,
                                      is_transpose=True)
                        P.inc(te.last, "pe", 1)
                    blkT(d7)

                    def d8(v, w=w):
                        P.wait(v, "pe")
                        P.inc(v.tensor_copy(hqT[:, w * 1024:(w + 1) * 1024], pTb[:, :]),
                              "dv", 1)
                    blk.vector(d8)

                def d9(te):
                    P.wait(te, "dv")
                    for p in range(32):
                        te.matmul(pO[:, 0:512], hqT[:, p * 128:(p + 1) * 128],
                                  w2t_s[:, p, 0:512], start=(p == 0), stop=(p == 31))
                        te.matmul(pO[:, 512:1024], hqT[:, p * 128:(p + 1) * 128],
                                  w2t_s[:, p, 512:1024], start=(p == 0), stop=(p == 31))
                    P.inc(te.last, "pe", 1)
                blkT(d9)

                def d10(v, j=j):
                    P.wait(v, "pe")
                    buf = x2b if j % 2 == 0 else x2b2
                    P.inc(v.scalar_tensor_tensor(obuf[:, :], pO[:, :], c2c[:, 0:1],
                                                 buf[:, :], ALU.mult, ALU.add), "dv", 1)
                blk.vector(d10)

                def d11(sy, j=j):
                    P.wait(sy, "dv")
                    dma_x(P, sy, obuf, OUT.ap(), j, store=True, sem="st")
                blk.sync(d11)

        _esb.close()
        def fin(sy):
            P.wait(sy, "st")
        blk.sync(fin)

    nc.finalize()
    return nc


_NC_CACHE = None


def host_inputs(inputs, core):
    ii = np.arange(128)
    tt = np.arange(128)
    maskb = np.where(tt[None, :] > ii[:, None], 0.0, -1e30).astype(np.float32)
    maskm = np.where(tt[None, :] >= ii[:, None], 0.0, -1e30).astype(np.float32)
    d = {
        "x": np.ascontiguousarray(inputs["L0_attn_x"][2 * core:2 * core + 2]),
        "wq": inputs["L0_attn_Wq"], "wk": inputs["L0_attn_Wk"],
        "wv": inputs["L0_attn_Wv"],
        "wab": np.ascontiguousarray(
            np.concatenate([inputs["L0_attn_Wa"], inputs["L0_attn_Wb"]], axis=1)),
        "bab": np.concatenate([inputs["L0_attn_ba"],
                               inputs["L0_attn_bb"]]).reshape(1, 32),
        "wg": inputs["L0_attn_Wg"], "wo": inputs["L0_attn_Wo"],
        "gn": np.ascontiguousarray(inputs["L0_attn_g_norm"].reshape(8, 128).T),
        "w1": inputs["L0_ffn_w1"], "w2": inputs["L0_ffn_w2"],
        "cidb": np.eye(128, dtype=BFNP),
        "cidf": np.eye(128, dtype=np.float32),
        "cmaskb": maskb.astype(BFNP), "cmaskm": maskm.astype(BFNP),
        "cones": np.ones((1, 128), dtype=np.float32),
        "cone128": np.ones((128, 1), dtype=np.float32),
        "crep": np.repeat(np.eye(16, dtype=BFNP), 128, axis=1) if False else np.kron(np.eye(16), np.ones((1, 128))).astype(BFNP),
        "cmaskb4": np.tile(maskb, (1, 4)).astype(BFNP),
        "cmaskm4": np.tile(maskm, (1, 4)).astype(BFNP),
        "babt": np.concatenate([inputs["L0_attn_ba"], inputs["L0_attn_bb"]]).reshape(32, 1),
    }
    return {k: np.ascontiguousarray(np.asarray(v)) for k, v in d.items()}


def kernel(**inputs):
    global _NC_CACHE
    if _NC_CACHE is None:
        _NC_CACHE = build_program()
    nc = _NC_CACHE
    in_maps = [host_inputs(inputs, c) for c in range(NCORES)]
    res = run_bass_kernel_spmd(nc, in_maps, core_ids=list(range(NCORES)))
    out = np.empty((16, SEQ, D), dtype=np.float32)
    for c in range(NCORES):
        out[2 * c:2 * c + 2] = np.asarray(res.results[c]["out"])
    return out


# version 08
# speedup vs baseline: 1.3407x; 1.3407x over previous
import numpy as np
from contextlib import ExitStack
import ml_dtypes
import concourse.bass as bass
import concourse.mybir as mybir
from concourse.bass_utils import run_bass_kernel_spmd

F32 = mybir.dt.float32
BF16 = mybir.dt.bfloat16
ALU = mybir.AluOpType
ACT = mybir.ActivationFunctionType
AX = mybir.AxisListType
BFNP = ml_dtypes.bfloat16

NCORES = 8
SEQ = 2048
D = 1024
H = 16
HD = 64
C = 64
NT = SEQ // C
HID = 4096
MAGIC = 12582912.0
A1STAGE = 99


class TEW:

    def __init__(self, te):
        object.__setattr__(self, "_te", te)
        object.__setattr__(self, "last", None)

    def matmul(self, *a, **k):
        inst = self._te.matmul(*a, **k)
        object.__setattr__(self, "last", inst)
        return inst

    def __getattr__(self, n):
        return getattr(self._te, n)


class Prog:
    def __init__(self, nc, blk):
        self.nc = nc
        self.blk = blk
        self.sems = {}
        self.cnt = {}
        for s in ["ld", "lg", "st", "pe", "dv", "sc", "gp"]:
            self.sems[s] = nc.alloc_semaphore("sem_" + s)
            self.cnt[s] = 0

    def inc(self, inst, name, n=1):
        inst.then_inc(self.sems[name], n)
        self.cnt[name] += n
        return self.cnt[name]

    def flush(self, eng, inst, sem):
        n = self.inc(inst, sem, 1)
        eng.wait_ge(self.sems[sem], n)

    def wait(self, eng, name, val=None):
        v = self.cnt[name] if val is None else val
        if v > 0:
            eng.wait_ge(self.sems[name], v)


def pstep(t):
    return t.ap().ap[0][0]


def dma_x(P, eng, sbuf, dram_ap3, j, store=False, sem="ld"):
    for s in range(2):
        sl = sbuf[64 * s:64 * s + 64, :]
        dr = dram_ap3[s, j * C:(j + 1) * C, :]
        if store:
            P.inc(eng.dma_start(dr, sl), sem, 16)
        else:
            P.inc(eng.dma_start(sl, dr), sem, 16)


def act_tab(P, idx):
    def f(sc):
        sc.add_instruction(mybir.InstLoadActFuncSet(
            name=P.nc.get_next_instruction_name(),
            act_func_set_id=idx, ins=[], outs=[]))
    P.blk.scalar(f)


def build_program(nq=2, na1=NT, na2=NT, nbpre=NT, nbmain=NT, dbg=False):
    nc = bass.Bass("TRN2", target_bir_lowering=False, detect_race_conditions=False)

    D_IN = {}
    def din(name, shape, dt=F32):
        D_IN[name] = nc.dram_tensor(name, shape, dt, kind="ExternalInput")
        return D_IN[name]

    X = din("x", [2, SEQ, D])
    WQ = din("wq", [D, D]); WK = din("wk", [D, D]); WV = din("wv", [D, D])
    WAB = din("wab", [D, 2 * H]); BAB = din("bab", [1, 2 * H])
    BABT = din("babt", [32, 1])
    WG = din("wg", [D, D]); WO = din("wo", [D, D]); GN = din("gn", [128, 8])
    W1 = din("w1", [HID, D]); W2 = din("w2", [D, HID])
    CIDB = din("cidb", [128, 128], BF16); CIDF = din("cidf", [128, 128])
    CMB = din("cmaskb", [128, 128], BF16); CMM = din("cmaskm", [128, 128], BF16)
    CON1 = din("cones", [1, 128]); CON128 = din("cone128", [128, 1])
    CREP = din("crep", [16, H * 128], BF16)
    CMB4 = din("cmaskb4", [128, 512], BF16)
    CMM4 = din("cmaskm4", [128, 512], BF16)

    OUT = nc.dram_tensor("out", [2, SEQ, D], F32, kind="ExternalOutput")
    dkind = "ExternalOutput" if dbg else "Internal"
    DBG = nc.dram_tensor("dbgx", [8, 128, 2048], F32, kind=dkind) if dbg else None
    YD = nc.dram_tensor("ydram", [NT, 128, D], BF16, kind=dkind)
    X2D = nc.dram_tensor("x2dram", [NT, 128, D], F32, kind=dkind)
    W1T = nc.dram_tensor("w1tT", [D, HID], BF16)
    W2T = nc.dram_tensor("w2tT", [HID, D], BF16)

    with nc.Block() as blk:
        def blkT(f):
            blk.tensor(lambda te: f(TEW(te)))

        P = Prog(nc, blk)

        _esp = ExitStack()
        cidb = _esp.enter_context(nc.sbuf_tensor("cidb_s", [128, 128], BF16))
        cidf = _esp.enter_context(nc.sbuf_tensor("cidf_s", [128, 128], F32))
        maskB = _esp.enter_context(nc.sbuf_tensor("maskB_s", [128, 128], BF16))
        maskM = _esp.enter_context(nc.sbuf_tensor("maskM_s", [128, 128], BF16))
        ones1 = _esp.enter_context(nc.sbuf_tensor("ones1_s", [1, 128], F32))
        one128 = _esp.enter_context(nc.sbuf_tensor("one128_s", [128, 1], F32))
        gnsb = _esp.enter_context(nc.sbuf_tensor("gn_s", [128, 8], F32))
        babsb = _esp.enter_context(nc.sbuf_tensor("bab_s", [1, 2 * H], F32))
        crep = _esp.enter_context(nc.sbuf_tensor("crep_s", [16, H * 128], BF16))
        maskB4 = _esp.enter_context(nc.sbuf_tensor("maskB4_s", [128, 512], BF16))
        maskM4 = _esp.enter_context(nc.sbuf_tensor("maskM4_s", [128, 512], BF16))
        nbab_a = _esp.enter_context(nc.sbuf_tensor("nbab_a", [16, 1], F32))
        nbab_b = _esp.enter_context(nc.sbuf_tensor("nbab_b", [16, 1], F32))
        xbuf = [_esp.enter_context(nc.sbuf_tensor(f"xbuf{i}", [128, D], F32)) for i in range(2)]
        LNm = _esp.enter_context(nc.sbuf_tensor("LNm", [128, NT], F32))
        LNr = _esp.enter_context(nc.sbuf_tensor("LNr", [128, NT], F32))
        RSTY = _esp.enter_context(nc.sbuf_tensor("RSTY", [128, NT], F32))
        MEAN2 = _esp.enter_context(nc.sbuf_tensor("MEAN2", [128, NT], F32))
        CC2 = _esp.enter_context(nc.sbuf_tensor("CC2", [128, NT], F32))
        SCQ = _esp.enter_context(nc.sbuf_tensor("SCQ", [128, NT], F32))
        s1col = _esp.enter_context(nc.sbuf_tensor("s1col", [128, 1], F32))
        s2col = _esp.enter_context(nc.sbuf_tensor("s2col", [128, 1], F32))
        zcol = _esp.enter_context(nc.sbuf_tensor("zcol", [128, 1], F32))
        onecol = _esp.enter_context(nc.sbuf_tensor("onecol", [128, 1], F32))
        nonecol = _esp.enter_context(nc.sbuf_tensor("nonecol", [128, 1], F32))

        def L0(sy):
            for dst, src in [(cidb, CIDB), (cidf, CIDF), (maskB, CMB), (maskM, CMM),
                             (ones1, CON1), (one128, CON128), (gnsb, GN), (babsb, BAB),
                         (crep, CREP), (maskB4, CMB4), (maskM4, CMM4)]:
                P.inc(sy.dma_start(dst[:, :], src.ap()), "ld", 16)
            P.inc(sy.dma_start(nbab_a[:, 0:1], BABT.ap()[0:16, :]), "ld", 16)
            P.inc(sy.dma_start(nbab_b[:, 0:1], BABT.ap()[16:32, :]), "ld", 16)
            dma_x(P, sy, xbuf[0], X.ap(), 0)
        blk.sync(L0)

        def L0g(g):
            P.wait(g, "ld")
            g.tensor_scalar(nbab_a[:, 0:1], nbab_a[:, 0:1], -1.0, None, ALU.mult)
            g.tensor_scalar(nbab_b[:, 0:1], nbab_b[:, 0:1], -1.0, None, ALU.mult)
            g.memset(zcol[:, 0:1], 0.0)
            g.memset(onecol[:, 0:1], 1.0)
            P.inc(g.memset(nonecol[:, 0:1], -1.0), "gp", 1)
        blk.gpsimd(L0g)

        act_tab(P, 6)

        _es0 = ExitStack()
        wb0 = _es0.enter_context(nc.sbuf_tensor("wb0", [128, HID], F32))
        wb1 = _es0.enter_context(nc.sbuf_tensor("wb1", [128, HID], F32))
        wpart = _es0.enter_context(nc.sbuf_tensor("wpart", [128, 32], F32))
        wsum = _es0.enter_context(nc.sbuf_tensor("wsum", [128, 1], F32))
        wcmp = _es0.enter_context(nc.sbuf_tensor("wcmp", [128, HID], F32))
        terb = _es0.enter_context(nc.sbuf_tensor("terb", [128, HID], BF16))
        tslab = _es0.enter_context(nc.sbuf_tensor("tslab", [128, 1024], BF16))
        halfs = _es0.enter_context(nc.sbuf_tensor("halfs", [128, 1], F32))
        nhalfs = _es0.enter_context(nc.sbuf_tensor("nhalfs", [128, 1], F32))
        stot = _es0.enter_context(nc.sbuf_tensor("stot", [1, 2], F32))
        wabs = _es0.enter_context(nc.sbuf_tensor("wabs", [128, HID], F32))
        pq0 = _es0.enter_context(nc.psum_tensor("pq0", [128, 512], F32))
        pqt = _es0.enter_context(nc.psum_tensor("pqt", [128, 1024], BF16))
        if True:
            for wi, (WSRC, WDST, NTW, FW, WDc, scol) in enumerate([
                    (W1, W1T, 32, D, HID, s1col), (W2, W2T, 8, HID, D, s2col)][:nq]):
                wap = WSRC.ap().rearrange("(t p) f -> t p f", p=128)
                bufs = [wb0, wb1]

                base_ld = P.cnt["ld"]
                def p1ld(sy, base=base_ld):
                    for t in range(NTW):
                        if t >= 2:
                            P.wait(sy, "sc", 0)
                        P.inc(sy.dma_start(bufs[t % 2][:, 0:FW], wap[t]), "ld", 16)
                sc_base = P.cnt["sc"]
                def p1(sy):
                    for t in range(NTW):
                        if t >= 2:
                            sy.wait_ge(P.sems["sc"], sc_base + (t - 1))
                        P.inc(sy.dma_start(bufs[t % 2][:, 0:FW], wap[t]), "ld", 16)
                blk.sync(p1)

                def p1s(sc, base=base_ld):
                    P.wait(sc, "gp")
                    for t in range(NTW):
                        sc.wait_ge(P.sems["ld"], base + 16 * (t + 1))
                        P.flush(sc, sc.activation(wabs[:, 0:FW], bufs[t % 2][:, 0:FW],
                                                   ACT.Abs,
                                                   accum_out=wpart[:, t:t + 1],
                                                   bias=zcol[:, 0:1],
                                                   scale=onecol[:, 0:1]), "sc")
                blk.scalar(p1s)

                def p1v(v):
                    P.wait(v, "sc")
                    P.flush(v, v.tensor_reduce(wsum[:, 0:1], wpart[:, 0:NTW], AX.X,
                                               ALU.add), "dv")
                blk.vector(p1v)

                def p1pe(te):
                    P.wait(te, "dv")
                    te.matmul(pq0[0:1, 0:1], one128[:, 0:1], wsum[:, 0:1],
                              start=True, stop=True)
                    P.inc(te.last, "pe", 1)
                blkT(p1pe)

                def p1v2(v, wi=wi):
                    P.wait(v, "pe")
                    P.flush(v, v.tensor_scalar(stot[0:1, wi:wi + 1], pq0[0:1, 0:1],
                                                1.0 / (NTW * 128 * FW), 1e-5,
                                                ALU.mult, ALU.max), "dv")
                blk.vector(p1v2)

                def p1pe2(te, wi=wi):
                    P.wait(te, "dv")
                    te.matmul(pq0[:, 1:2], ones1[0:1, :], stot[0:1, wi:wi + 1],
                              start=True, stop=True)
                    P.inc(te.last, "pe", 1)
                blkT(p1pe2)

                def p1v3(v, scol=scol):
                    P.wait(v, "pe")
                    P.flush(v, v.tensor_scalar(halfs[:, 0:1], pq0[:, 1:2], 0.5, None,
                                                ALU.mult), "dv")
                    P.flush(v, v.tensor_scalar(nhalfs[:, 0:1], pq0[:, 1:2], -0.5, None,
                                               ALU.mult), "dv")
                    P.flush(v, v.tensor_scalar(scol[:, 0:1], pq0[:, 1:2], -1.0, None,
                                               ALU.mult), "dv")
                blk.vector(p1v3)
                ev_scales = P.cnt["dv"]

                base2 = P.cnt["ld"]
                nblk = FW // 128
                for t in range(NTW):
                    def p2ld(sy, t=t):
                        if t >= 2:
                            P.wait(sy, "dv")
                        P.inc(sy.dma_start(bufs[t % 2][:, 0:FW], wap[t]), "ld", 16)
                    blk.sync(p2ld)
                    ld_t = P.cnt["ld"]

                    def p2v(v, t=t, ld_t=ld_t):
                        v.wait_ge(P.sems["ld"], ld_t)
                        if t == 0:
                            P.wait(v, "dv", ev_scales)
                        if t >= 1:
                            P.wait(v, "gp")
                        P.flush(v, v.tensor_scalar(wcmp[:, 0:FW], bufs[t % 2][:, 0:FW],
                                                    halfs[:, 0:1], None, ALU.is_gt),
                                "dv")
                        P.inc(v.scalar_tensor_tensor(terb[:, 0:FW], bufs[t % 2][:, 0:FW],
                                                     nhalfs[:, 0:1], wcmp[:, 0:FW],
                                                     ALU.is_lt, ALU.subtract), "dv", 1)
                    blk.vector(p2v)
                    ev_ter = P.cnt["dv"]

                    for w0 in range(0, nblk, 8):
                        def p2pe(te, w0=w0, ev=ev_ter):
                            P.wait(te, "dv", ev)
                            P.wait(te, "sc")
                            for b in range(8):
                                te.matmul(pqt[:, b * 128:(b + 1) * 128],
                                          terb[:, (w0 + b) * 128:(w0 + b + 1) * 128],
                                          cidb[:, :], start=True, stop=True,
                                          is_transpose=True)
                            P.inc(te.last, "pe", 1)
                        blkT(p2pe)
                        ev_tr = P.cnt["pe"]

                        def p2gp(sc, ev=ev_tr):
                            P.wait(sc, "pe", ev)
                            P.wait(sc, "st")
                            P.inc(sc.activation(tslab[:, :], pqt[:, :], ACT.Copy,
                                                bias=0.0, scale=onecol[:, 0:1]), "sc", 1)
                        blk.scalar(p2gp)
                        ev_ev = P.cnt["sc"]

                        def p2st(sy, t=t, w0=w0, ev=ev_ev, WDST=WDST, WDc=WDc):
                            P.wait(sy, "sc", ev)
                            off = (w0 * 128) * WDc + t * 128
                            dst = bass.AP(WDST, off, [[WDc, 128], [128 * WDc, 8], [1, 128]])
                            P.inc(sy.dma_start(
                                dst, tslab[:, :].rearrange("p (b f) -> p b f", b=8)),
                                "st", 16)
                        blk.sync(p2st)
        _es0.close()
        quant_st_done = P.cnt["st"]

        _es = ExitStack()
        Wk_s = _es.enter_context(nc.sbuf_tensor("Wk_s", [128, 8, D], BF16))
        Wq_s = _es.enter_context(nc.sbuf_tensor("Wq_s", [128, 8, D], BF16))
        Wv_s = _es.enter_context(nc.sbuf_tensor("Wv_s", [128, 8, D], BF16))
        Wab_s = _es.enter_context(nc.sbuf_tensor("Wab_s", [128, 8, 2 * H], BF16))
        xln = _es.enter_context(nc.sbuf_tensor("xln", [128, D], BF16))
        xlnT = _es.enter_context(nc.sbuf_tensor("xlnT", [128, D], BF16))
        Ktok = _es.enter_context(nc.sbuf_tensor("Ktok", [128, D], BF16))
        Qtok = _es.enter_context(nc.sbuf_tensor("Qtok", [128, D], BF16))
        Vtok = _es.enter_context(nc.sbuf_tensor("Vtok", [128, D], BF16))
        Ktb = _es.enter_context(nc.sbuf_tensor("Ktb", [128, H * 128], BF16))
        Qtb = _es.enter_context(nc.sbuf_tensor("Qtb", [128, H * 128], BF16))
        Kfb = _es.enter_context(nc.sbuf_tensor("Kfb", [128, H * 128], BF16))
        Qfb = _es.enter_context(nc.sbuf_tensor("Qfb", [128, H * 128], BF16))
        Cf1 = _es.enter_context(nc.sbuf_tensor("Cf1", [128, H * 128], BF16))
        Cf2 = _es.enter_context(nc.sbuf_tensor("Cf2", [128, H * 128], BF16))
        Cf4 = _es.enter_context(nc.sbuf_tensor("Cf4", [128, H * 128], BF16))
        Cf8 = _es.enter_context(nc.sbuf_tensor("Cf8", [128, H * 128], BF16))
        Cf16 = _es.enter_context(nc.sbuf_tensor("Cf16", [128, H * 128], BF16))
        Cf32 = _es.enter_context(nc.sbuf_tensor("Cf32", [128, H * 128], BF16))
        Dsl = _es.enter_context(nc.sbuf_tensor("Dsl", [128, 8 * 128], BF16))
        Mts = _es.enter_context(nc.sbuf_tensor("Mts", [128, H * 128], BF16))
        Usb = _es.enter_context(nc.sbuf_tensor("Usb", [128, H * HD], BF16))
        Usc = _es.enter_context(nc.sbuf_tensor("Usc", [128, H * HD], BF16))
        Ssl = _es.enter_context(nc.sbuf_tensor("Ssl", [128, H * HD], F32))
        Sbf = _es.enter_context(nc.sbuf_tensor("Sbf", [128, H * HD], BF16))
        ytile = _es.enter_context(nc.sbuf_tensor("ytile", [128, D], BF16))
        Escr = _es.enter_context(nc.sbuf_tensor("Escr", [128, 1024], F32))
        Epre = _es.enter_context(nc.sbuf_tensor("Epre", [128, 1024], BF16))
        scrA = _es.enter_context(nc.sbuf_tensor("scrA", [128, 1024], F32))
        stat32 = _es.enter_context(nc.sbuf_tensor("stat32", [128, 32], F32))
        st_hi = _es.enter_context(nc.sbuf_tensor("st_hi", [128, 32], BF16))
        st_lo = _es.enter_context(nc.sbuf_tensor("st_lo", [128, 32], BF16))
        Prow = _es.enter_context(nc.sbuf_tensor("Prow", [16, 128], F32))
        zero16 = _es.enter_context(nc.sbuf_tensor("zero16", [16, 128], F32))
        zaT = _es.enter_context(nc.sbuf_tensor("zaT", [16, 128], F32))
        zbT = _es.enter_context(nc.sbuf_tensor("zbT", [16, 128], F32))
        nlaT = _es.enter_context(nc.sbuf_tensor("nlaT", [16, 128], F32))
        nlbT = _es.enter_context(nc.sbuf_tensor("nlbT", [16, 128], F32))
        lkT = _es.enter_context(nc.sbuf_tensor("lkT", [16, 128], F32))
        lqT = _es.enter_context(nc.sbuf_tensor("lqT", [16, 128], F32))
        r_c1n = _es.enter_context(nc.sbuf_tensor("r_c1n", [16, 128], F32))
        r_s1b = _es.enter_context(nc.sbuf_tensor("r_s1b", [16, 128], F32))
        r_s1m = _es.enter_context(nc.sbuf_tensor("r_s1m", [16, 128], F32))
        r_rhsc = _es.enter_context(nc.sbuf_tensor("r_rhsc", [16, 128], BF16))
        r_g = _es.enter_context(nc.sbuf_tensor("r_g", [16, 128], BF16))
        r_rc = _es.enter_context(nc.sbuf_tensor("r_rc", [16, 128], BF16))
        r_sigb = _es.enter_context(nc.sbuf_tensor("r_sigb", [16, 128], BF16))
        r_gc = _es.enter_context(nc.sbuf_tensor("r_gc", [16, 128], BF16))
        r_gcpre = _es.enter_context(nc.sbuf_tensor("r_gcpre", [16, 128], F32))
        r_rcpre = _es.enter_context(nc.sbuf_tensor("r_rcpre", [16, 128], F32))
        c1n_hi = _es.enter_context(nc.sbuf_tensor("c1n_hi", [16, 128], BF16))
        c1n_lo = _es.enter_context(nc.sbuf_tensor("c1n_lo", [16, 128], BF16))
        s1b_hi = _es.enter_context(nc.sbuf_tensor("s1b_hi", [16, 128], BF16))
        s1b_lo = _es.enter_context(nc.sbuf_tensor("s1b_lo", [16, 128], BF16))
        s1m_hi = _es.enter_context(nc.sbuf_tensor("s1m_hi", [16, 128], BF16))
        s1m_lo = _es.enter_context(nc.sbuf_tensor("s1m_lo", [16, 128], BF16))
        tsplit = _es.enter_context(nc.sbuf_tensor("tsplit", [16, 128], F32))
        backTb = _es.enter_context(nc.sbuf_tensor("backTb", [128, 80], BF16))
        backTc = _es.enter_context(nc.sbuf_tensor("backTc", [128, 16], F32))
        ysq = _es.enter_context(nc.sbuf_tensor("ysq", [128, 1], F32))
        mcol = _es.enter_context(nc.sbuf_tensor("mcol", [128, 1], F32))
        vcol = _es.enter_context(nc.sbuf_tensor("vcol", [128, 1], F32))
        negm = _es.enter_context(nc.sbuf_tensor("negm", [128, 1], F32))
        gate = _es.enter_context(nc.sbuf_tensor("gate", [128, D], BF16))
        x2t = _es.enter_context(nc.sbuf_tensor("x2t", [128, D], F32))
        dstage = _es.enter_context(nc.sbuf_tensor("dstage", [128, 2048], F32)) if dbg else None
        pA = _es.enter_context(nc.psum_tensor("pA", [128, 1024], F32))
        pB = _es.enter_context(nc.psum_tensor("pB", [128, 1024], F32))
        pC = _es.enter_context(nc.psum_tensor("pC", [128, 1024], F32))
        pD = _es.enter_context(nc.psum_tensor("pD", [128, 512], F32))
        pT = _es.enter_context(nc.psum_tensor("pT", [128, 1024], BF16))
        if True:
            Cfs = {1: Cf1, 2: Cf2, 4: Cf4, 8: Cf8, 16: Cf16, 32: Cf32}

            def LW(g):
                P.wait(g, "st")
                P.wait(g, "dv")
                P.wait(g, "sc")
                P.wait(g, "pe")
                for dst, src, cols in [(Wk_s, WK, D), (Wq_s, WQ, D), (Wv_s, WV, D),
                                       (Wab_s, WAB, 2 * H)]:
                    sap = bass.AP(src, 0, [[cols, 128], [128 * cols, 8], [1, cols]])
                    P.inc(g.dma_start(dst[:, :, :], sap), "lg", 16)
                g.memset(Ktb[:, :], 0.0)
                g.memset(Qtb[:, :], 0.0)
                g.memset(Ssl[:, :], 0.0)
                g.memset(Sbf[:, :], 0.0)
                P.inc(g.memset(zero16[:, :], 0.0), "gp", 1)
            blk.gpsimd(LW)

            def dbgdump(slot, src_ap, cols, j):
                if not dbg or j != 0:
                    return

                def dv(v):
                    P.wait(v, "pe")
                    P.wait(v, "sc")
                    P.wait(v, "gp")
                    P.inc(v.tensor_copy(dstage[:, 0:cols], src_ap), "dv", 1)
                blk.vector(dv)

                def dsy(sy):
                    P.wait(sy, "dv")
                    P.inc(sy.dma_start(DBG.ap()[slot, :, 0:cols], dstage[:, 0:cols]),
                          "st", 16)
                blk.sync(dsy)

            R_RHSC, R_G, R_RC, R_SIGB, R_GC = 0, 16, 32, 48, 64

            for j in range(na1):
                xb = xbuf[j % 2]
                if j + 1 < NT:
                    def pref(sy, j=j):
                        P.wait(sy, "dv")
                        dma_x(P, sy, xbuf[(j + 1) % 2], X.ap(), j + 1)
                    blk.sync(pref)

                ld_need = P.cnt["ld"]

                def a1(v, j=j, xb=xb):
                    v.wait_ge(P.sems["ld"], ld_need)
                    if j == 0:
                        P.wait(v, "gp")
                    P.flush(v, v.tensor_reduce(mcol[:, 0:1], xb[:, :], AX.X, ALU.add),
                            "dv")
                    P.flush(v, v.tensor_scalar(mcol[:, 0:1], mcol[:, 0:1], 1.0 / D,
                                               None, ALU.mult), "dv")
                    P.flush(v, v.tensor_scalar(negm[:, 0:1], mcol[:, 0:1], -1.0, None,
                                               ALU.mult), "dv")
                blk.vector(a1)

                def a2s(sc, xb=xb):
                    P.wait(sc, "dv")
                    P.flush(sc, sc.activation(scrA[:, :], xb[:, :], ACT.Square,
                                               bias=negm[:, 0:1], scale=onecol[:, 0:1],
                                               accum_out=vcol[:, 0:1]), "sc")
                blk.scalar(a2s)

                def a2v(v):
                    P.wait(v, "sc")
                    P.flush(v, v.tensor_scalar(vcol[:, 0:1], vcol[:, 0:1], 1.0 / D, 1e-5,
                                               ALU.mult, ALU.add), "dv")
                blk.vector(a2v)

                def a2s2(sc):
                    P.wait(sc, "dv")
                    P.flush(sc, sc.activation(vcol[:, 0:1], vcol[:, 0:1], ACT.Ln,
                                              bias=zcol[:, 0:1], scale=onecol[:, 0:1]),
                            "sc")
                blk.scalar(a2s2)

                def a2v2(v):
                    P.wait(v, "sc")
                    P.flush(v, v.tensor_scalar(vcol[:, 0:1], vcol[:, 0:1], -0.5, None,
                                               ALU.mult), "dv")
                blk.vector(a2v2)

                def a2s3(sc):
                    P.wait(sc, "dv")
                    P.flush(sc, sc.activation(vcol[:, 0:1], vcol[:, 0:1], ACT.Exp,
                                              bias=zcol[:, 0:1], scale=onecol[:, 0:1]),
                            "sc")
                blk.scalar(a2s3)

                def a3v(v, j=j, xb=xb):
                    P.wait(v, "sc")
                    P.inc(v.tensor_scalar(xln[:, :], xb[:, :], mcol[:, 0:1], vcol[:, 0:1],
                                          ALU.subtract, ALU.mult), "dv", 1)
                    v.tensor_copy(LNm[:, j:j + 1], mcol[:, 0:1])
                    P.inc(v.tensor_copy(LNr[:, j:j + 1], vcol[:, 0:1]), "dv", 1)
                blk.vector(a3v)

                dbgdump(0, xln[:, :], D, j)
                def a4pe(te):
                    P.wait(te, "dv")
                    P.wait(te, "sc")
                    for p in range(8):
                        te.matmul(pT[:, p * 128:(p + 1) * 128], xln[:, p * 128:(p + 1) * 128],
                                  cidb[:, :], start=True, stop=True, is_transpose=True)
                    P.inc(te.last, "pe", 1)
                blkT(a4pe)

                def a4gp(sc):
                    P.wait(sc, "pe")
                    P.inc(sc.activation(xlnT[:, :], pT[:, :], ACT.Copy,
                                        bias=0.0, scale=onecol[:, 0:1]), "sc", 1)
                blk.scalar(a4gp)

                if A1STAGE < 2:
                    continue
                def a5pe(te):
                    P.wait(te, "sc")
                    P.wait(te, "gp")
                    P.wait(te, "lg")
                    P.wait(te, "dv")
                    P.wait(te, "sc")
                    for p in range(8):
                        for ps, Wsb in [(pA, Wk_s), (pB, Wq_s), (pC, Wv_s)]:
                            te.matmul(ps[:, 0:512], xlnT[:, p * 128:(p + 1) * 128],
                                      Wsb[:, p, 0:512], start=(p == 0), stop=(p == 7))
                            te.matmul(ps[:, 512:1024], xlnT[:, p * 128:(p + 1) * 128],
                                      Wsb[:, p, 512:1024], start=(p == 0), stop=(p == 7))
                        te.matmul(pD[0:16, 0:128], Wab_s[:, p, 0:16],
                                  xlnT[:, p * 128:(p + 1) * 128],
                                  start=(p == 0), stop=False)
                        te.matmul(pD[0:16, 128:256], Wab_s[:, p, 16:32],
                                  xlnT[:, p * 128:(p + 1) * 128],
                                  start=False, stop=(p == 7))
                    P.inc(te.last, "pe", 1)
                blkT(a5pe)
                ev_proj = P.cnt["pe"]

                def a6v(v):
                    P.wait(v, "pe", ev_proj)
                    P.inc(v.tensor_copy(Ktok[:, :], pA[:, :]), "dv", 1)
                blk.vector(a6v)

                def a6g(v):
                    P.wait(v, "pe", ev_proj)
                    P.inc(v.tensor_copy(Qtok[:, :], pB[:, :]), "dv", 1)
                    P.inc(v.tensor_copy(Vtok[:, :], pC[:, :]), "dv", 1)
                blk.vector(a6g)

                def a6s(sc):
                    P.wait(sc, "pe", ev_proj)
                    P.wait(sc, "dv")
                    P.inc(sc.activation(scrA[:, :], pA[:, :], ACT.Square, bias=zcol[:, 0:1], scale=onecol[:, 0:1]), "sc", 1)
                blk.scalar(a6s)

                def a7v(v):
                    P.wait(v, "sc")
                    P.flush(v, v.tensor_reduce(
                        stat32[:, 0:16].rearrange("p (a b) -> p a b", a=16, b=1),
                        scrA[:, :].rearrange("p (a b) -> p a b", a=16, b=64),
                        AX.X, ALU.add), "dv")
                blk.vector(a7v)

                def a7s(sc):
                    P.wait(sc, "dv")
                    P.inc(sc.activation(scrA[:, :], pB[:, :], ACT.Square, bias=zcol[:, 0:1], scale=onecol[:, 0:1]), "sc", 1)
                blk.scalar(a7s)

                def a8v(v):
                    P.wait(v, "sc")
                    P.flush(v, v.tensor_reduce(
                        stat32[:, 16:32].rearrange("p (a b) -> p a b", a=16, b=1),
                        scrA[:, :].rearrange("p (a b) -> p a b", a=16, b=64),
                        AX.X, ALU.add), "dv")
                    P.inc(v.tensor_copy(Sbf[:, :], Ssl[:, :]), "dv", 1)
                blk.vector(a8v)

                dbgdump(1, Ktok[:, :], D, j)
                dbgdump(2, Vtok[:, :], D, j)
                if A1STAGE < 3:
                    continue
                def a10s(sc):
                    P.wait(sc, "pe")
                    P.wait(sc, "dv")
                    P.inc(sc.activation(zaT[:, :], pD[0:16, 0:128], ACT.Exp,
                                        bias=nbab_a[:, 0:1], scale=nonecol[0:16, 0:1]),
                          "sc", 1)
                    P.inc(sc.activation(zbT[:, :], pD[0:16, 128:256], ACT.Exp,
                                        bias=nbab_b[:, 0:1], scale=nonecol[0:16, 0:1]),
                          "sc", 1)
                blk.scalar(a10s)

                def a9v(v):
                    P.wait(v, "sc")
                    P.flush(v, v.tensor_copy(st_hi[:, :], stat32[:, :]), "dv")
                    P.flush(v, v.tensor_sub(st_lo[:, :], stat32[:, :], st_hi[:, :]),
                            "dv")
                    P.inc(v.tensor_scalar(zaT[:, :], zaT[:, :], 1.0, None, ALU.add),
                          "dv", 1)
                    P.inc(v.tensor_scalar(zbT[:, :], zbT[:, :], 1.0, None, ALU.add),
                          "dv", 1)
                blk.vector(a9v)

                def a9pe(te):
                    P.wait(te, "dv")
                    P.wait(te, "sc")
                    te.matmul(pT[0:16, 0:128], st_hi[:, 0:16], cidb[:, :],
                              start=True, stop=True, is_transpose=True)
                    te.matmul(pT[0:16, 128:256], st_lo[:, 0:16], cidb[:, :],
                              start=True, stop=True, is_transpose=True)
                    te.matmul(pT[0:16, 256:384], st_hi[:, 16:32], cidb[:, :],
                              start=True, stop=True, is_transpose=True)
                    te.matmul(pT[0:16, 384:512], st_lo[:, 16:32], cidb[:, :],
                              start=True, stop=True, is_transpose=True)
                    P.inc(te.last, "pe", 1)
                blkT(a9pe)

                def a9v2(v):
                    P.wait(v, "pe")
                    P.flush(v, v.tensor_copy(lkT[:, :], pT[0:16, 0:128]), "dv")
                    P.flush(v, v.tensor_add(lkT[:, :], lkT[:, :], pT[0:16, 128:256]),
                            "dv")
                    P.flush(v, v.tensor_copy(lqT[:, :], pT[0:16, 256:384]), "dv")
                    P.flush(v, v.tensor_add(lqT[:, :], lqT[:, :], pT[0:16, 384:512]),
                            "dv")
                    P.flush(v, v.tensor_scalar(lkT[:, :], lkT[:, :], 1e-24, None,
                                               ALU.add), "dv")
                    P.flush(v, v.tensor_scalar(lqT[:, :], lqT[:, :], 1e-24, None,
                                               ALU.add), "dv")
                blk.vector(a9v2)

                if dbg and j == 0:
                    def dpre(v):
                        P.wait(v, "pe")
                        v.tensor_copy(dstage[0:128, 0:32], stat32[:, :])
                        v.tensor_copy(dstage[0:128, 32:64], st_hi[:, :])
                        v.tensor_copy(dstage[0:128, 64:96], st_lo[:, :])
                        v.tensor_copy(dstage[0:16, 96:224], lkT[:, :])
                        v.tensor_copy(dstage[0:16, 224:352], lqT[:, :])
                        v.tensor_copy(dstage[0:16, 352:480], pT[0:16, 0:128])
                        P.inc(v.tensor_copy(dstage[0:16, 480:608], pT[0:16, 128:256]),
                              "dv", 1)
                    blk.vector(dpre)

                    def dpre2(sy):
                        P.wait(sy, "dv")
                        P.inc(sy.dma_start(DBG.ap()[3], dstage[:, :]), "st", 16)
                    blk.sync(dpre2)

                def a11s(sc):
                    P.wait(sc, "dv")
                    z16, o16 = zcol[0:16, 0:1], onecol[0:16, 0:1]
                    P.flush(sc, sc.activation(nlaT[:, :], zaT[:, :], ACT.Ln, bias=z16,
                                               scale=o16), "sc")
                    P.flush(sc, sc.activation(nlbT[:, :], zbT[:, :], ACT.Ln, bias=z16,
                                              scale=o16), "sc")
                    P.flush(sc, sc.activation(lkT[:, :], lkT[:, :], ACT.Ln, bias=z16,
                                              scale=o16), "sc")
                    P.flush(sc, sc.activation(lqT[:, :], lqT[:, :], ACT.Ln, bias=z16,
                                              scale=o16), "sc")
                blk.scalar(a11s)

                if A1STAGE < 4:
                    continue

                def a12v(v):
                    P.wait(v, "sc")
                    P.flush(v, v.tensor_tensor_scan(Prow[:, 0:64], nlaT[:, 0:64],
                                               zero16[:, 0:64], 0.0, ALU.add, ALU.add), "dv")
                    P.flush(v, v.tensor_tensor_scan(Prow[:, 64:128], nlaT[:, 64:128],
                                               zero16[:, 64:128], 0.0, ALU.add, ALU.add),
                          "dv")
                    P.flush(v, v.scalar_tensor_tensor(r_c1n[:, :], lkT[:, :], -0.5,
                                                 Prow[:, :], ALU.mult, ALU.add), "dv")
                    P.flush(v, v.tensor_copy(c1n_hi[:, :], r_c1n[:, :]), "dv")
                    P.flush(v, v.tensor_sub(c1n_lo[:, :], r_c1n[:, :], c1n_hi[:, :]), "dv")
                    P.flush(v, v.tensor_sub(r_s1b[:, :], nlaT[:, :], nlbT[:, :]), "dv")
                    P.flush(v, v.tensor_sub(r_s1b[:, :], r_s1b[:, :], Prow[:, :]), "dv")
                    P.flush(v, v.scalar_tensor_tensor(r_s1b[:, :], lkT[:, :], -0.5,
                                                 r_s1b[:, :], ALU.mult, ALU.add), "dv")
                    P.flush(v, v.tensor_copy(s1b_hi[:, :], r_s1b[:, :]), "dv")
                    P.flush(v, v.tensor_sub(s1b_lo[:, :], r_s1b[:, :], s1b_hi[:, :]), "dv")
                    P.flush(v, v.scalar_tensor_tensor(r_s1m[:, :], lqT[:, :], -0.5,
                                                 Prow[:, :], ALU.mult, ALU.subtract),
                          "dv")
                    P.flush(v, v.tensor_copy(s1m_hi[:, :], r_s1m[:, :]), "dv")
                    P.flush(v, v.tensor_sub(s1m_lo[:, :], r_s1m[:, :], s1m_hi[:, :]), "dv")
                    P.flush(v, v.tensor_scalar(r_rcpre[:, 0:64], r_c1n[:, 0:64],
                                          Prow[:, 63:64], None, ALU.subtract), "dv")
                    P.flush(v, v.tensor_scalar(r_rcpre[:, 64:128], r_c1n[:, 64:128],
                                          Prow[:, 127:128], None, ALU.subtract), "dv")
                    P.flush(v, v.tensor_scalar(r_gcpre[:, 0:64], zero16[:, 0:64],
                                          Prow[:, 63:64], None, ALU.subtract), "dv")
                    P.flush(v, v.tensor_scalar(r_gcpre[:, 64:128], zero16[:, 64:128],
                                          Prow[:, 127:128], None, ALU.subtract), "dv")
                blk.vector(a12v)

                def a13s(sc):
                    P.wait(sc, "dv")
                    z16, o16, n16 = zcol[0:16, 0:1], onecol[0:16, 0:1], nonecol[0:16, 0:1]
                    P.flush(sc, sc.activation(r_rhsc[:, :], r_s1b[:, :], ACT.Exp,
                                        bias=z16, scale=o16), "sc")
                    P.flush(sc, sc.activation(r_g[:, :], r_s1m[:, :], ACT.Exp,
                                        bias=z16, scale=o16), "sc")
                    P.flush(sc, sc.activation(r_rc[:, :], r_rcpre[:, :], ACT.Exp,
                                        bias=z16, scale=o16), "sc")
                    P.flush(sc, sc.activation(r_sigb[:, :], nlbT[:, :], ACT.Exp,
                                        bias=z16, scale=n16), "sc")
                    P.flush(sc, sc.activation(r_gc[:, :], r_gcpre[:, :], ACT.Exp,
                                        bias=z16, scale=o16), "sc")
                blk.scalar(a13s)

                def a14pe(te):
                    P.wait(te, "sc")
                    for i, cls in enumerate([r_rhsc, r_g, r_rc, r_sigb, r_gc,
                                             c1n_hi, c1n_lo]):
                        te.matmul(pT[:, 512 + i * 16:512 + (i + 1) * 16], cls[:, :],
                                  cidb[0:16, 0:16], start=True, stop=True,
                                  is_transpose=True)
                    P.inc(te.last, "pe", 1)
                blkT(a14pe)

                def a14v(v):
                    P.wait(v, "pe")
                    P.flush(v, v.tensor_copy(backTb[:, :], pT[:, 512:592]), "dv")
                    P.flush(v, v.tensor_copy(backTc[:, :], pT[:, 592:608]), "dv")
                    P.flush(v, v.tensor_add(backTc[:, :], backTc[:, :], pT[:, 608:624]),
                            "dv")
                blk.vector(a14v)

                if A1STAGE < 5:
                    continue
                def a15v(v):
                    P.wait(v, "pe")
                    ps_k, ps_kt = pstep(Ktok), pstep(Ktb)
                    P.inc(v.tensor_copy(
                        bass.AP(Ktb, 0, [[ps_kt, 64], [128, H], [1, 64]]),
                        bass.AP(Ktok, 0, [[ps_k, 64], [64, H], [1, 64]])), "dv", 1)
                    ps_q, ps_qt = pstep(Qtok), pstep(Qtb)
                    P.inc(v.tensor_copy(
                        bass.AP(Qtb, 0, [[ps_qt, 64], [128, H], [1, 64]]),
                        bass.AP(Qtok, 0, [[ps_q, 64], [64, H], [1, 64]])), "dv", 1)
                blk.vector(a15v)

                def a15g(g):
                    P.wait(g, "dv")
                    ps_k, ps_kt = pstep(Ktok), pstep(Ktb)
                    P.inc(g.tensor_copy(
                        bass.AP(Ktb, 64 * ps_kt + 64, [[ps_kt, 64], [128, H], [1, 64]]),
                        bass.AP(Ktok, 64 * ps_k, [[ps_k, 64], [64, H], [1, 64]])), "gp", 1)
                    ps_q, ps_qt = pstep(Qtok), pstep(Qtb)
                    P.inc(g.tensor_copy(
                        bass.AP(Qtb, 64 * ps_qt + 64, [[ps_qt, 64], [128, H], [1, 64]]),
                        bass.AP(Qtok, 64 * ps_q, [[ps_q, 64], [64, H], [1, 64]])), "gp", 1)
                blk.gpsimd(a15g)

                for (tb, fb) in [(Ktb, Kfb), (Qtb, Qfb)]:
                    for w in range(2):
                        def a16pe(te, tb=tb, w=w):
                            P.wait(te, "gp")
                            P.wait(te, "sc")
                            P.wait(te, "dv")
                            for b in range(8):
                                hh = w * 8 + b
                                te.matmul(pT[:, b * 128:(b + 1) * 128],
                                          tb[:, hh * 128:(hh + 1) * 128],
                                          cidb[:, :], start=True, stop=True,
                                          is_transpose=True)
                            P.inc(te.last, "pe", 1)
                        blkT(a16pe)

                        def a16g(sc, fb=fb, w=w):
                            P.wait(sc, "pe")
                            P.inc(sc.activation(fb[:, w * 1024:(w + 1) * 1024], pT[:, :],
                                                ACT.Copy, bias=0.0,
                                                scale=onecol[:, 0:1]), "sc", 1)
                        blk.scalar(a16g)

                if A1STAGE < 6:
                    continue
                for half in range(2):
                    h0 = half * 8

                    def g1(te, h0=h0):
                        P.wait(te, "gp")
                        P.wait(te, "dv")
                        P.wait(te, "sc")
                        for hh in range(8):
                            ksl = Kfb[:, (h0 + hh) * 128:(h0 + hh + 1) * 128]
                            te.matmul(pA[:, hh * 128:(hh + 1) * 128], ksl, ksl,
                                      start=True, stop=True)
                            te.matmul(pC[:, hh * 128:(hh + 1) * 128], ksl,
                                      Qfb[:, (h0 + hh) * 128:(h0 + hh + 1) * 128],
                                      start=True, stop=True)
                            te.matmul(pD[:, hh * 64:(hh + 1) * 64], ksl,
                                      Sbf[:, (h0 + hh) * 64:(h0 + hh + 1) * 64],
                                      start=True, stop=True)
                        for bank in range(2):
                            for q in range(4):
                                hh = bank * 4 + q
                                oh = crep[:, (h0 + hh) * 128:(h0 + hh + 1) * 128]
                                te.matmul(pB[:, hh * 128:(hh + 1) * 128], oh,
                                          s1b_hi[:, :], start=(q == 0), stop=False)
                                te.matmul(pB[:, hh * 128:(hh + 1) * 128], oh,
                                          s1b_lo[:, :], start=False, stop=False)
                            te.matmul(pB[:, bank * 512:(bank + 1) * 512], cidb[:, :],
                                      maskB4[:, :], start=False, stop=True)
                        P.inc(te.last, "pe", 1)
                    blkT(g1)

                    def e1(v, h0=h0):
                        P.wait(v, "pe")
                        cb = bass.AP(backTc, h0,
                                     [[pstep(backTc), 128], [1, 8], [0, 128]])
                        P.inc(v.tensor_tensor(
                            Escr[:, :].rearrange("p (a b) -> p a b", a=8, b=128),
                            pB[:, :].rearrange("p (a b) -> p a b", a=8, b=128),
                            cb, ALU.add), "dv", 1)
                    blk.vector(e1)

                    def e1s(sc):
                        P.wait(sc, "dv")
                        P.inc(sc.activation(Epre[:, :], Escr[:, :], ACT.Exp, bias=zcol[:, 0:1], scale=onecol[:, 0:1]), "sc", 1)
                    blk.scalar(e1s)

                    def e1m(v, h0=h0):
                        P.wait(v, "sc")
                        P.inc(v.tensor_tensor(Cf1[:, h0 * 128:(h0 + 8) * 128],
                                              Epre[:, :], pA[:, :], ALU.mult), "dv", 1)
                        rb = bass.AP(backTb, R_RHSC + h0,
                                     [[pstep(backTb), 128], [1, 8], [0, 64]])
                        P.inc(v.tensor_tensor(
                            scrA[:, 0:512].rearrange("p (a b) -> p a b", a=8, b=64),
                            pD[:, 0:512].rearrange("p (a b) -> p a b", a=8, b=64),
                            rb, ALU.mult), "dv", 1)
                        sb = bass.AP(backTb, R_SIGB + h0,
                                     [[pstep(backTb), 128], [1, 8], [0, 64]])
                        P.inc(v.tensor_tensor(
                            scrA[:, 512:1024].rearrange("p (a b) -> p a b", a=8, b=64),
                            Vtok[:, h0 * 64:(h0 + 8) * 64].rearrange(
                                "p (a b) -> p a b", a=8, b=64),
                            sb, ALU.mult), "dv", 1)
                        P.inc(v.tensor_sub(Usb[:, h0 * 64:(h0 + 8) * 64],
                                           scrA[:, 512:1024], scrA[:, 0:512]), "dv", 1)
                    blk.vector(e1m)

                    if A1STAGE < 7:
                        continue
                    if half == 0:
                        dbgdump(4, Cf1[:, 0:1024], 1024, j)
                        dbgdump(5, Usb[:, 0:512], 512, j)
                    for k in [1, 2, 4, 8, 16]:
                        def sq1(te, k=k, h0=h0):
                            P.wait(te, "dv")
                            P.wait(te, "sc")
                            for b in range(8):
                                te.matmul(pT[:, b * 128:(b + 1) * 128],
                                          Cfs[k][:, (h0 + b) * 128:(h0 + b + 1) * 128],
                                          cidb[:, :], start=True, stop=True,
                                          is_transpose=True)
                            P.inc(te.last, "pe", 1)
                        blkT(sq1)

                        def sq2(sc):
                            P.wait(sc, "pe")
                            P.inc(sc.activation(Dsl[:, :], pT[:, :], ACT.Copy,
                                                bias=0.0, scale=onecol[:, 0:1]), "sc", 1)
                        blk.scalar(sq2)

                        def sq3(te, k=k, h0=h0):
                            P.wait(te, "sc")
                            P.wait(te, "dv")
                            for b in range(8):
                                te.matmul(pA[:, b * 128:(b + 1) * 128],
                                          Dsl[:, b * 128:(b + 1) * 128],
                                          Cfs[k][:, (h0 + b) * 128:(h0 + b + 1) * 128],
                                          start=True, stop=True)
                            P.inc(te.last, "pe", 1)
                        blkT(sq3)

                        def sq4(v, k=k, h0=h0):
                            P.wait(v, "pe")
                            P.inc(v.tensor_copy(Cfs[2 * k][:, h0 * 128:(h0 + 8) * 128],
                                                pA[:, :]), "dv", 1)
                        blk.vector(sq4)

                    for k in [32, 16, 8, 4, 2, 1]:
                        def ap1(te, k=k, h0=h0):
                            P.wait(te, "dv")
                            for b in range(8):
                                csl = Cfs[k][:, (h0 + b) * 128:(h0 + b + 1) * 128]
                                te.matmul(pD[:, b * 64:(b + 1) * 64], csl,
                                          Usb[:, (h0 + b) * 64:(h0 + b + 1) * 64],
                                          start=True, stop=True)
                            P.inc(te.last, "pe", 1)
                        blkT(ap1)

                        def ap2(v, k=k, h0=h0):
                            P.wait(v, "pe")
                            op = ALU.subtract if k == 1 else ALU.add
                            P.inc(v.tensor_tensor(Usb[:, h0 * 64:(h0 + 8) * 64],
                                                  Usb[:, h0 * 64:(h0 + 8) * 64],
                                                  pD[:, 0:512], op), "dv", 1)
                        blk.vector(ap2)

                    if A1STAGE < 8:
                        continue
                    if half == 0:
                        dbgdump(6, Usb[:, 0:512], 512, j)
                    def m1(te, h0=h0):
                        P.wait(te, "dv")
                        for bank in range(2):
                            for q in range(4):
                                hh = bank * 4 + q
                                oh = crep[:, (h0 + hh) * 128:(h0 + hh + 1) * 128]
                                te.matmul(pB[:, hh * 128:(hh + 1) * 128], oh,
                                          s1m_hi[:, :], start=(q == 0), stop=False)
                                te.matmul(pB[:, hh * 128:(hh + 1) * 128], oh,
                                          s1m_lo[:, :], start=False, stop=False)
                            te.matmul(pB[:, bank * 512:(bank + 1) * 512], cidb[:, :],
                                      maskM4[:, :], start=False, stop=True)
                        P.inc(te.last, "pe", 1)
                    blkT(m1)

                    def m2(v, h0=h0):
                        P.wait(v, "pe")
                        cb = bass.AP(backTc, h0,
                                     [[pstep(backTc), 128], [1, 8], [0, 128]])
                        P.inc(v.tensor_tensor(
                            Escr[:, :].rearrange("p (a b) -> p a b", a=8, b=128),
                            pB[:, :].rearrange("p (a b) -> p a b", a=8, b=128),
                            cb, ALU.add), "dv", 1)
                    blk.vector(m2)

                    def m3(sc):
                        P.wait(sc, "dv")
                        P.inc(sc.activation(Epre[:, :], Escr[:, :], ACT.Exp, bias=zcol[:, 0:1], scale=onecol[:, 0:1]), "sc", 1)
                    blk.scalar(m3)

                    def m4(v, h0=h0):
                        P.wait(v, "sc")
                        P.inc(v.tensor_tensor(Mts[:, h0 * 128:(h0 + 8) * 128],
                                              Epre[:, :], pC[:, :], ALU.mult), "dv", 1)
                        rc = bass.AP(backTb, R_RC + h0,
                                     [[pstep(backTb), 128], [1, 8], [0, 64]])
                        P.inc(v.tensor_tensor(
                            Usc[:, h0 * 64:(h0 + 8) * 64].rearrange(
                                "p (a b) -> p a b", a=8, b=64),
                            Usb[:, h0 * 64:(h0 + 8) * 64].rearrange(
                                "p (a b) -> p a b", a=8, b=64),
                            rc, ALU.mult), "dv", 1)
                    blk.vector(m4)

                    def o1(te, h0=h0):
                        P.wait(te, "dv")
                        for b in range(8):
                            msl = Mts[:, (h0 + b) * 128:(h0 + b + 1) * 128]
                            te.matmul(pD[:, b * 64:(b + 1) * 64], msl,
                                      Usb[:, (h0 + b) * 64:(h0 + b + 1) * 64],
                                      start=True, stop=True)
                        for b in range(8):
                            qsl = Qfb[:, (h0 + b) * 128:(h0 + b + 1) * 128]
                            te.matmul(pB[:, b * 64:(b + 1) * 64], qsl,
                                      Sbf[:, (h0 + b) * 64:(h0 + b + 1) * 64],
                                      start=True, stop=True)
                        P.inc(te.last, "pe", 1)
                    blkT(o1)

                    def o2(v, h0=h0):
                        P.wait(v, "pe")
                        gb = bass.AP(backTb, R_G + h0,
                                     [[pstep(backTb), 128], [1, 8], [0, 64]])
                        P.inc(v.tensor_tensor(
                            scrA[:, 0:512].rearrange("p (a b) -> p a b", a=8, b=64),
                            pB[:, 0:512].rearrange("p (a b) -> p a b", a=8, b=64),
                            gb, ALU.mult), "dv", 1)
                        P.inc(v.tensor_tensor(ytile[:, h0 * 64:(h0 + 8) * 64],
                                              scrA[:, 0:512], pD[:, 0:512], ALU.add),
                              "dv", 1)
                    blk.vector(o2)

                    def su1(te, h0=h0):
                        P.wait(te, "dv")
                        for b in range(8):
                            ksl = Ktb[:, (h0 + b) * 128:(h0 + b + 1) * 128]
                            te.matmul(pD[:, b * 64:(b + 1) * 64], ksl,
                                      Usc[:, (h0 + b) * 64:(h0 + b + 1) * 64],
                                      start=True, stop=True)
                        P.inc(te.last, "pe", 1)
                    blkT(su1)

                    def su2(v, h0=h0):
                        P.wait(v, "pe")
                        gc = bass.AP(backTb, R_GC + h0,
                                     [[pstep(backTb), 128], [1, 8], [0, 64]])
                        P.inc(v.tensor_tensor(
                            scrA[:, 0:512].rearrange("p (a b) -> p a b", a=8, b=64),
                            Ssl[:, h0 * 64:(h0 + 8) * 64].rearrange(
                                "p (a b) -> p a b", a=8, b=64),
                            gc, ALU.mult), "dv", 1)
                        P.inc(v.tensor_tensor(Ssl[:, h0 * 64:(h0 + 8) * 64],
                                              scrA[:, 0:512], pD[:, 0:512], ALU.add),
                              "dv", 1)
                    blk.vector(su2)

                dbgdump(7, ytile[:, :], D, j)
                def ys1(v, j=j):
                    P.flush(v, v.tensor_mul(scrA[:, :], ytile[:, :], ytile[:, :]),
                            "dv")
                    P.flush(v, v.tensor_reduce(ysq[:, 0:1], scrA[:, :], AX.X, ALU.add),
                            "dv")
                blk.vector(ys1)

                def ys1b(v):
                    P.flush(v, v.tensor_scalar(ysq[:, 0:1], ysq[:, 0:1], 1.0 / D, 1e-5,
                                               ALU.mult, ALU.add), "dv")
                blk.vector(ys1b)

                def ys2(sc, j=j):
                    P.wait(sc, "dv")
                    P.flush(sc, sc.activation(ysq[:, 0:1], ysq[:, 0:1], ACT.Ln,
                                               bias=zcol[:, 0:1], scale=onecol[:, 0:1]),
                            "sc")
                blk.scalar(ys2)

                def ys2b(v):
                    P.wait(v, "sc")
                    P.flush(v, v.tensor_scalar(ysq[:, 0:1], ysq[:, 0:1], -0.5, None,
                                               ALU.mult), "dv")
                blk.vector(ys2b)

                def ys2c(sc, j=j):
                    P.wait(sc, "dv")
                    P.flush(sc, sc.activation(RSTY[:, j:j + 1], ysq[:, 0:1], ACT.Exp,
                                               bias=zcol[:, 0:1], scale=onecol[:, 0:1]),
                            "sc")
                blk.scalar(ys2c)

                def ys3(sy, j=j):
                    P.wait(sy, "dv")
                    P.inc(sy.dma_start(YD.ap()[j], ytile[:, :]), "st", 16)
                blk.sync(ys3)

            def LW2(g):
                P.wait(g, "pe")
                for dst, src in [(Wk_s, WG), (Wq_s, WO)]:
                    sap = bass.AP(src, 0, [[D, 128], [128 * D, 8], [1, D]])
                    P.inc(g.dma_start(dst[:, :, :], sap), "lg", 16)
            blk.gpsimd(LW2)
            ld_w2 = P.cnt["lg"]

            def foldgn(v):
                v.wait_ge(P.sems["lg"], ld_w2)
                for p in range(8):
                    P.flush(v, v.tensor_scalar(Wq_s[:, p, :], Wq_s[:, p, :],
                                               gnsb[:, p:p + 1], None, ALU.mult), "dv")
            blk.vector(foldgn)

            act_tab(P, 18)

            for j in range(na2):
                xb = xbuf[j % 2]

                def b0(sy, j=j, xb=xb):
                    P.wait(sy, "dv")
                    P.wait(sy, "sc")
                    dma_x(P, sy, xb, X.ap(), j)
                    P.inc(sy.dma_start(ytile[:, :], YD.ap()[j]), "ld", 16)
                blk.sync(b0)
                ldj = P.cnt["ld"]

                def b1(v, j=j, xb=xb):
                    v.wait_ge(P.sems["ld"], ldj)
                    P.inc(v.tensor_scalar(xln[:, :], xb[:, :], LNm[:, j:j + 1],
                                          LNr[:, j:j + 1], ALU.subtract, ALU.mult),
                          "dv", 1)
                blk.vector(b1)

                def b2(te):
                    P.wait(te, "dv")
                    P.wait(te, "sc")
                    for p in range(8):
                        te.matmul(pT[:, p * 128:(p + 1) * 128],
                                  xln[:, p * 128:(p + 1) * 128],
                                  cidb[:, :], start=True, stop=True, is_transpose=True)
                    P.inc(te.last, "pe", 1)
                blkT(b2)

                def b3(sc):
                    P.wait(sc, "pe")
                    P.inc(sc.activation(xlnT[:, :], pT[:, :], ACT.Copy,
                                        bias=0.0, scale=onecol[:, 0:1]), "sc", 1)
                blk.scalar(b3)

                def b4(te):
                    P.wait(te, "lg")
                    P.wait(te, "sc")
                    for p in range(8):
                        te.matmul(pA[:, 0:512], xlnT[:, p * 128:(p + 1) * 128],
                                  Wk_s[:, p, 0:512], start=(p == 0), stop=(p == 7))
                        te.matmul(pA[:, 512:1024], xlnT[:, p * 128:(p + 1) * 128],
                                  Wk_s[:, p, 512:1024], start=(p == 0), stop=(p == 7))
                    P.inc(te.last, "pe", 1)
                blkT(b4)

                def b5(sc):
                    P.wait(sc, "pe")
                    P.inc(sc.activation(gate[:, :], pA[:, :], ACT.Silu, bias=zcol[:, 0:1], scale=onecol[:, 0:1]), "sc", 1)
                blk.scalar(b5)

                def b6(v):
                    P.wait(v, "sc")
                    P.inc(v.tensor_mul(xln[:, :], ytile[:, :], gate[:, :]), "dv", 1)
                blk.vector(b6)

                def b7(te):
                    P.wait(te, "dv")
                    P.wait(te, "sc")
                    for p in range(8):
                        te.matmul(pT[:, p * 128:(p + 1) * 128],
                                  xln[:, p * 128:(p + 1) * 128],
                                  cidb[:, :], start=True, stop=True, is_transpose=True)
                    P.inc(te.last, "pe", 1)
                blkT(b7)

                def b8(sc):
                    P.wait(sc, "pe")
                    P.inc(sc.activation(xlnT[:, :], pT[:, :], ACT.Copy,
                                        bias=0.0, scale=onecol[:, 0:1]), "sc", 1)
                blk.scalar(b8)

                def b9(te):
                    P.wait(te, "sc")
                    P.wait(te, "dv")
                    for p in range(8):
                        te.matmul(pC[:, 0:512], xlnT[:, p * 128:(p + 1) * 128],
                                  Wq_s[:, p, 0:512], start=(p == 0), stop=(p == 7))
                        te.matmul(pC[:, 512:1024], xlnT[:, p * 128:(p + 1) * 128],
                                  Wq_s[:, p, 512:1024], start=(p == 0), stop=(p == 7))
                    P.inc(te.last, "pe", 1)
                blkT(b9)

                def b10(v, j=j, xb=xb):
                    P.wait(v, "pe")
                    if dbg and j == 0:
                        P.inc(v.tensor_copy(dstage[:, 0:1024], pC[:, :]), "dv", 1)
                        P.inc(v.tensor_copy(dstage[:, 1024:1040], RSTY[:, 0:16]),
                              "dv", 1)
                    P.inc(v.scalar_tensor_tensor(x2t[:, :], pC[:, :], RSTY[:, j:j + 1],
                                                 xb[:, :], ALU.mult, ALU.add), "dv", 1)
                blk.vector(b10)
                if dbg and j == 0:
                    def db10(sy):
                        P.wait(sy, "dv")
                        P.inc(sy.dma_start(DBG.ap()[5], dstage[:, :]), "st", 16)
                    blk.sync(db10)

                def b11(sy, j=j):
                    P.wait(sy, "dv")
                    P.inc(sy.dma_start(X2D.ap()[j], x2t[:, :]), "st", 16)
                blk.sync(b11)

        _es.close()
        _esb = ExitStack()
        w1t_s = _esb.enter_context(nc.sbuf_tensor("w1t_s", [128, 8, HID], BF16))
        w2t_s = _esb.enter_context(nc.sbuf_tensor("w2t_s", [128, 32, D], BF16))
        x2b = _esb.enter_context(nc.sbuf_tensor("x2b", [128, D], F32))
        x2b2 = _esb.enter_context(nc.sbuf_tensor("x2b2", [128, D], F32))
        xqb = _esb.enter_context(nc.sbuf_tensor("xqb", [128, D], BF16))
        xqTb = _esb.enter_context(nc.sbuf_tensor("xqTb", [128, D], BF16))
        hbuf = _esb.enter_context(nc.sbuf_tensor("hbuf", [128, HID], F32))
        hqb = _esb.enter_context(nc.sbuf_tensor("hqb", [128, HID], BF16))
        hqT = _esb.enter_context(nc.sbuf_tensor("hqT", [128, HID], BF16))
        obuf = _esb.enter_context(nc.sbuf_tensor("obuf", [128, D], F32))
        bm = _esb.enter_context(nc.sbuf_tensor("bm", [128, 1], F32))
        bnm = _esb.enter_context(nc.sbuf_tensor("bnm", [128, 1], F32))
        bv = _esb.enter_context(nc.sbuf_tensor("bv", [128, 1], F32))
        bq = _esb.enter_context(nc.sbuf_tensor("bq", [128, 1], F32))
        chc = _esb.enter_context(nc.sbuf_tensor("chc", [128, 1], F32))
        c2c = _esb.enter_context(nc.sbuf_tensor("c2c", [128, 1], F32))
        bscr = _esb.enter_context(nc.sbuf_tensor("bscr", [128, 1024], F32))
        pH = _esb.enter_context(nc.psum_tensor("pH", [128, 1024], F32))
        pO = _esb.enter_context(nc.psum_tensor("pO", [128, 1024], F32))
        pTb = _esb.enter_context(nc.psum_tensor("pTb", [128, 1024], BF16))
        if True:
            def bl0(sy):
                P.wait(sy, "st")
                P.wait(sy, "dv")
                P.wait(sy, "sc")
                P.wait(sy, "pe")
                P.wait(sy, "gp")
                sap1 = bass.AP(W1T, 0, [[HID, 128], [128 * HID, 8], [1, HID]])
                P.inc(sy.dma_start(w1t_s[:, :, :], sap1), "ld", 16)
                sap2 = bass.AP(W2T, 0, [[D, 128], [128 * D, 32], [1, D]])
                P.inc(sy.dma_start(w2t_s[:, :, :], sap2), "ld", 16)
            blk.sync(bl0)

            act_tab(P, 6)

            for j in range(nbpre):
                def c0(sy, j=j):
                    P.wait(sy, "sc")
                    P.wait(sy, "dv")
                    if j == 0:
                        P.wait(sy, "st")
                    P.inc(sy.dma_start((x2b if j % 2 == 0 else x2b2)[:, :], X2D.ap()[j]),
                          "ld", 16)
                blk.sync(c0)
                ldj = P.cnt["ld"]

                def c1(v, j=j):
                    v.wait_ge(P.sems["ld"], ldj)
                    buf = x2b if j % 2 == 0 else x2b2
                    P.flush(v, v.tensor_reduce(bm[:, 0:1], buf[:, :], AX.X, ALU.add), "dv")
                    P.flush(v, v.tensor_scalar(bm[:, 0:1], bm[:, 0:1], 1.0 / D, None,
                                          ALU.mult), "dv")
                    P.flush(v, v.tensor_scalar(bnm[:, 0:1], bm[:, 0:1], -1.0, None, ALU.mult),
                          "dv")
                    v.tensor_copy(MEAN2[:, j:j + 1], bm[:, 0:1])
                    P.flush(v, v.tensor_scalar(bscr[:, :], buf[:, :], bm[:, 0:1], None,
                                          ALU.subtract), "dv")
                    P.flush(v, v.tensor_reduce(bq[:, 0:1], bscr[:, :], AX.X, ALU.max,
                                          apply_absolute_value=True), "dv")
                blk.vector(c1)

                def c2(sc, j=j):
                    P.wait(sc, "dv")
                    buf = x2b if j % 2 == 0 else x2b2
                    P.flush(sc, sc.activation(bscr[:, :], buf[:, :], ACT.Square,
                                               bias=bnm[:, 0:1], scale=onecol[:, 0:1],
                                               accum_out=bv[:, 0:1]), "sc")
                blk.scalar(c2)

                def c2v(v):
                    P.wait(v, "sc")
                    P.flush(v, v.tensor_scalar(bv[:, 0:1], bv[:, 0:1], 1.0 / D,
                                               1e-5, ALU.mult, ALU.add), "dv")
                blk.vector(c2v)

                def c2s(sc):
                    P.wait(sc, "dv")
                    P.flush(sc, sc.activation(bv[:, 0:1], bv[:, 0:1], ACT.Ln,
                                               bias=zcol[:, 0:1], scale=onecol[:, 0:1]),
                            "sc")
                blk.scalar(c2s)

                def c2v2(v):
                    P.wait(v, "sc")
                    P.flush(v, v.tensor_scalar(bv[:, 0:1], bv[:, 0:1], -0.5, None,
                                               ALU.mult), "dv")
                blk.vector(c2v2)

                def c2s2(sc):
                    P.wait(sc, "dv")
                    P.flush(sc, sc.activation(bv[:, 0:1], bv[:, 0:1], ACT.Exp,
                                               bias=zcol[:, 0:1], scale=onecol[:, 0:1]),
                            "sc")
                blk.scalar(c2s2)

                def c3(v, j=j):
                    P.wait(v, "sc")
                    P.flush(v, v.tensor_scalar(SCQ[:, j:j + 1], bq[:, 0:1], bv[:, 0:1], 1e-5,
                                          ALU.mult, ALU.max), "dv")
                    P.flush(v, v.reciprocal(bq[:, 0:1], SCQ[:, j:j + 1]), "dv")
                    P.flush(v, v.tensor_scalar(bq[:, 0:1], bq[:, 0:1], 127.0, None, ALU.mult),
                          "dv")
                    P.flush(v, v.tensor_mul(CC2[:, j:j + 1], bv[:, 0:1], bq[:, 0:1]), "dv")
                blk.vector(c3)

            act_tab(P, 10)
            ld_w = P.cnt["ld"]
            for j in range(nbmain):
                def d0(sy, j=j):
                    P.wait(sy, "dv")
                    P.inc(sy.dma_start((x2b if j % 2 == 0 else x2b2)[:, :], X2D.ap()[j]),
                          "ld", 16)
                blk.sync(d0)
                ldj = P.cnt["ld"]

                def d1(v, j=j):
                    v.wait_ge(P.sems["ld"], max(ldj, ld_w))
                    buf = x2b if j % 2 == 0 else x2b2
                    P.flush(v, v.tensor_scalar(bscr[:, :], buf[:, :], MEAN2[:, j:j + 1],
                                          CC2[:, j:j + 1], ALU.subtract, ALU.mult),
                          "dv")
                    P.flush(v, v.tensor_scalar(xqb[:, :], bscr[:, :], MAGIC, -MAGIC,
                                          ALU.add, ALU.add), "dv")
                    P.flush(v, v.tensor_scalar(chc[:, 0:1], SCQ[:, j:j + 1], s1col[:, 0:1],
                                          1.0 / 127.0, ALU.mult, ALU.mult), "dv")
                blk.vector(d1)

                def d2(te):
                    P.wait(te, "dv")
                    P.wait(te, "sc")
                    for p in range(8):
                        te.matmul(pTb[:, p * 128:(p + 1) * 128],
                                  xqb[:, p * 128:(p + 1) * 128],
                                  cidb[:, :], start=True, stop=True, is_transpose=True)
                    P.inc(te.last, "pe", 1)
                blkT(d2)

                def d3(sc):
                    P.wait(sc, "pe")
                    P.inc(sc.activation(xqTb[:, :], pTb[:, :], ACT.Copy,
                                        bias=0.0, scale=onecol[:, 0:1]), "sc", 1)
                blk.scalar(d3)
                ev_xqT = P.cnt["sc"]

                for q in range(4):
                    def d4(te, q=q):
                        P.wait(te, "sc", ev_xqT)
                        P.wait(te, "sc")
                        for p in range(8):
                            te.matmul(pH[:, 0:512], xqTb[:, p * 128:(p + 1) * 128],
                                      w1t_s[:, p, q * 1024:q * 1024 + 512],
                                      start=(p == 0), stop=(p == 7))
                            te.matmul(pH[:, 512:1024], xqTb[:, p * 128:(p + 1) * 128],
                                      w1t_s[:, p, q * 1024 + 512:(q + 1) * 1024],
                                      start=(p == 0), stop=(p == 7))
                        P.inc(te.last, "pe", 1)
                    blkT(d4)

                    def d5(sc, q=q):
                        P.wait(sc, "pe")
                        P.inc(sc.activation(hbuf[:, q * 1024:(q + 1) * 1024], pH[:, :],
                                            ACT.Gelu, bias=zcol[:, 0:1], scale=chc[:, 0:1]),
                              "sc", 1)
                    blk.scalar(d5)

                def d6(v, j=j):
                    P.wait(v, "sc")
                    P.flush(v, v.tensor_reduce(bq[:, 0:1], hbuf[:, :], AX.X, ALU.max,
                                          apply_absolute_value=True), "dv")
                    P.flush(v, v.tensor_scalar(bq[:, 0:1], bq[:, 0:1], 1e-5, None, ALU.max),
                          "dv")
                    P.flush(v, v.tensor_scalar(c2c[:, 0:1], bq[:, 0:1], s2col[:, 0:1],
                                          1.0 / 127.0, ALU.mult, ALU.mult), "dv")
                    P.flush(v, v.reciprocal(bq[:, 0:1], bq[:, 0:1]), "dv")
                    P.flush(v, v.tensor_scalar(bq[:, 0:1], bq[:, 0:1], 127.0, None, ALU.mult),
                          "dv")
                    P.flush(v, v.tensor_scalar(bscr[:, :], hbuf[:, 0:1024], bq[:, 0:1],
                                          None, ALU.mult), "dv")
                    P.flush(v, v.tensor_scalar(hqb[:, 0:1024], bscr[:, :], MAGIC, -MAGIC,
                                          ALU.add, ALU.add), "dv")
                blk.vector(d6)

                def d6g(g):
                    P.wait(g, "dv")
                    for q in range(1, 4):
                        g.tensor_scalar(bscr[:, :] if False else hqb[:, q * 1024:(q + 1) * 1024],
                                        hbuf[:, q * 1024:(q + 1) * 1024], bq[:, 0:1],
                                        None, ALU.mult)
                        g.tensor_scalar(hqb[:, q * 1024:(q + 1) * 1024],
                                        hqb[:, q * 1024:(q + 1) * 1024], MAGIC, -MAGIC,
                                        ALU.add, ALU.add)
                    P.inc(g.engine_nop(), "gp", 1)
                blk.gpsimd(d6g)

                for w in range(4):
                    def d7(te, w=w):
                        P.wait(te, "gp")
                        if w > 0:
                            P.wait(te, "dv")
                        for b in range(8):
                            te.matmul(pTb[:, b * 128:(b + 1) * 128],
                                      hqb[:, (w * 8 + b) * 128:(w * 8 + b + 1) * 128],
                                      cidb[:, :], start=True, stop=True,
                                      is_transpose=True)
                        P.inc(te.last, "pe", 1)
                    blkT(d7)

                    def d8(v, w=w):
                        P.wait(v, "pe")
                        P.inc(v.tensor_copy(hqT[:, w * 1024:(w + 1) * 1024], pTb[:, :]),
                              "dv", 1)
                    blk.vector(d8)

                def d9(te):
                    P.wait(te, "dv")
                    for p in range(32):
                        te.matmul(pO[:, 0:512], hqT[:, p * 128:(p + 1) * 128],
                                  w2t_s[:, p, 0:512], start=(p == 0), stop=(p == 31))
                        te.matmul(pO[:, 512:1024], hqT[:, p * 128:(p + 1) * 128],
                                  w2t_s[:, p, 512:1024], start=(p == 0), stop=(p == 31))
                    P.inc(te.last, "pe", 1)
                blkT(d9)

                def d10(v, j=j):
                    P.wait(v, "pe")
                    buf = x2b if j % 2 == 0 else x2b2
                    P.inc(v.scalar_tensor_tensor(obuf[:, :], pO[:, :], c2c[:, 0:1],
                                                 buf[:, :], ALU.mult, ALU.add), "dv", 1)
                blk.vector(d10)

                def d11(sy, j=j):
                    P.wait(sy, "dv")
                    dma_x(P, sy, obuf, OUT.ap(), j, store=True, sem="st")
                blk.sync(d11)

        _esb.close()
        def fin(sy):
            P.wait(sy, "st")
        blk.sync(fin)

    nc.finalize()
    return nc


_NC_CACHE = None


def host_inputs(inputs, core):
    ii = np.arange(128)
    tt = np.arange(128)
    maskb = np.where(tt[None, :] > ii[:, None], 0.0, -1e30).astype(np.float32)
    maskm = np.where(tt[None, :] >= ii[:, None], 0.0, -1e30).astype(np.float32)
    d = {
        "x": np.ascontiguousarray(inputs["L0_attn_x"][2 * core:2 * core + 2]),
        "wq": inputs["L0_attn_Wq"], "wk": inputs["L0_attn_Wk"],
        "wv": inputs["L0_attn_Wv"],
        "wab": np.ascontiguousarray(
            np.concatenate([inputs["L0_attn_Wa"], inputs["L0_attn_Wb"]], axis=1)),
        "bab": np.concatenate([inputs["L0_attn_ba"],
                               inputs["L0_attn_bb"]]).reshape(1, 32),
        "wg": inputs["L0_attn_Wg"], "wo": inputs["L0_attn_Wo"],
        "gn": np.ascontiguousarray(inputs["L0_attn_g_norm"].reshape(8, 128).T),
        "w1": inputs["L0_ffn_w1"], "w2": inputs["L0_ffn_w2"],
        "cidb": np.eye(128, dtype=BFNP),
        "cidf": np.eye(128, dtype=np.float32),
        "cmaskb": maskb.astype(BFNP), "cmaskm": maskm.astype(BFNP),
        "cones": np.ones((1, 128), dtype=np.float32),
        "cone128": np.ones((128, 1), dtype=np.float32),
        "crep": np.repeat(np.eye(16, dtype=BFNP), 128, axis=1) if False else np.kron(np.eye(16), np.ones((1, 128))).astype(BFNP),
        "cmaskb4": np.tile(maskb, (1, 4)).astype(BFNP),
        "cmaskm4": np.tile(maskm, (1, 4)).astype(BFNP),
        "babt": np.concatenate([inputs["L0_attn_ba"], inputs["L0_attn_bb"]]).reshape(32, 1),
    }
    return {k: np.ascontiguousarray(np.asarray(v)) for k, v in d.items()}


def kernel(**inputs):
    global _NC_CACHE
    if _NC_CACHE is None:
        _NC_CACHE = build_program()
    nc = _NC_CACHE
    in_maps = [host_inputs(inputs, c) for c in range(NCORES)]
    res = run_bass_kernel_spmd(nc, in_maps, core_ids=list(range(NCORES)))
    out = np.empty((16, SEQ, D), dtype=np.float32)
    for c in range(NCORES):
        out[2 * c:2 * c + 2] = np.asarray(res.results[c]["out"])
    return out


# version 09
# speedup vs baseline: 1.6860x; 1.2576x over previous
import numpy as np
from contextlib import ExitStack
import ml_dtypes
import concourse.bass as bass
import concourse.mybir as mybir
from concourse.bass_utils import run_bass_kernel_spmd

F32 = mybir.dt.float32
BF16 = mybir.dt.bfloat16
ALU = mybir.AluOpType
ACT = mybir.ActivationFunctionType
AX = mybir.AxisListType
BFNP = ml_dtypes.bfloat16

NCORES = 8
SEQ = 2048
D = 1024
H = 16
HD = 64
C = 64
NT = SEQ // C
HID = 4096
MAGIC = 12582912.0
A1STAGE = 99


class TEW:

    def __init__(self, te):
        object.__setattr__(self, "_te", te)
        object.__setattr__(self, "last", None)

    def matmul(self, *a, **k):
        inst = self._te.matmul(*a, **k)
        object.__setattr__(self, "last", inst)
        return inst

    def __getattr__(self, n):
        return getattr(self._te, n)


class Prog:
    def __init__(self, nc, blk):
        self.nc = nc
        self.blk = blk
        self.sems = {}
        self.cnt = {}
        for s in ["ld", "lg", "st", "pe", "dv", "sc", "gp"]:
            self.sems[s] = nc.alloc_semaphore("sem_" + s)
            self.cnt[s] = 0

    def inc(self, inst, name, n=1):
        inst.then_inc(self.sems[name], n)
        self.cnt[name] += n
        return self.cnt[name]

    def flush(self, eng, inst, sem):
        n = self.inc(inst, sem, 1)
        eng.wait_ge(self.sems[sem], n)

    def wait(self, eng, name, val=None):
        v = self.cnt[name] if val is None else val
        if v > 0:
            eng.wait_ge(self.sems[name], v)


def pstep(t):
    return t.ap().ap[0][0]


def dma_x(P, eng, sbuf, dram_ap3, j, store=False, sem="ld"):
    for s in range(2):
        sl = sbuf[64 * s:64 * s + 64, :]
        dr = dram_ap3[s, j * C:(j + 1) * C, :]
        if store:
            P.inc(eng.dma_start(dr, sl), sem, 16)
        else:
            P.inc(eng.dma_start(sl, dr), sem, 16)


def act_tab(P, idx):
    def f(sc):
        sc.add_instruction(mybir.InstLoadActFuncSet(
            name=P.nc.get_next_instruction_name(),
            act_func_set_id=idx, ins=[], outs=[]))
    P.blk.scalar(f)


def build_program(nq=2, na1=NT, na2=NT, nbpre=NT, nbmain=NT, dbg=False):
    nc = bass.Bass("TRN2", target_bir_lowering=False, detect_race_conditions=False)

    D_IN = {}
    def din(name, shape, dt=F32):
        D_IN[name] = nc.dram_tensor(name, shape, dt, kind="ExternalInput")
        return D_IN[name]

    X = din("x", [2, SEQ, D])
    WQ = din("wq", [D, D]); WK = din("wk", [D, D]); WV = din("wv", [D, D])
    WAB = din("wab", [D, 2 * H]); BAB = din("bab", [1, 2 * H])
    BABT = din("babt", [32, 1])
    WG = din("wg", [D, D]); WO = din("wo", [D, D]); GN = din("gn", [128, 8])
    W1 = din("w1", [HID, D]); W2 = din("w2", [D, HID])
    CIDB = din("cidb", [128, 128], BF16); CIDF = din("cidf", [128, 128])
    CMB = din("cmaskb", [128, 128], BF16); CMM = din("cmaskm", [128, 128], BF16)
    CON1 = din("cones", [1, 128]); CON128 = din("cone128", [128, 1])
    CREP = din("crep", [16, H * 128], BF16)
    CMB4 = din("cmaskb4", [128, 512], BF16)
    CMM4 = din("cmaskm4", [128, 512], BF16)

    OUT = nc.dram_tensor("out", [2, SEQ, D], F32, kind="ExternalOutput")
    dkind = "ExternalOutput" if dbg else "Internal"
    DBG = nc.dram_tensor("dbgx", [8, 128, 2048], F32, kind=dkind) if dbg else None
    YD = nc.dram_tensor("ydram", [NT, 128, D], BF16, kind=dkind)
    X2D = nc.dram_tensor("x2dram", [NT, 128, D], F32, kind=dkind)
    W1T = nc.dram_tensor("w1tT", [D, HID], BF16)
    W2T = nc.dram_tensor("w2tT", [HID, D], BF16)

    with nc.Block() as blk:
        def blkT(f):
            blk.tensor(lambda te: f(TEW(te)))

        P = Prog(nc, blk)

        _esp = ExitStack()
        cidb = _esp.enter_context(nc.sbuf_tensor("cidb_s", [128, 128], BF16))
        cidf = _esp.enter_context(nc.sbuf_tensor("cidf_s", [128, 128], F32))
        maskB = _esp.enter_context(nc.sbuf_tensor("maskB_s", [128, 128], BF16))
        maskM = _esp.enter_context(nc.sbuf_tensor("maskM_s", [128, 128], BF16))
        ones1 = _esp.enter_context(nc.sbuf_tensor("ones1_s", [1, 128], F32))
        one128 = _esp.enter_context(nc.sbuf_tensor("one128_s", [128, 1], F32))
        gnsb = _esp.enter_context(nc.sbuf_tensor("gn_s", [128, 8], F32))
        babsb = _esp.enter_context(nc.sbuf_tensor("bab_s", [1, 2 * H], F32))
        crep = _esp.enter_context(nc.sbuf_tensor("crep_s", [16, H * 128], BF16))
        maskB4 = _esp.enter_context(nc.sbuf_tensor("maskB4_s", [128, 512], BF16))
        maskM4 = _esp.enter_context(nc.sbuf_tensor("maskM4_s", [128, 512], BF16))
        nbab_a = _esp.enter_context(nc.sbuf_tensor("nbab_a", [16, 1], F32))
        nbab_b = _esp.enter_context(nc.sbuf_tensor("nbab_b", [16, 1], F32))
        xbuf = [_esp.enter_context(nc.sbuf_tensor(f"xbuf{i}", [128, D], F32)) for i in range(2)]
        LNm = _esp.enter_context(nc.sbuf_tensor("LNm", [128, NT], F32))
        LNr = _esp.enter_context(nc.sbuf_tensor("LNr", [128, NT], F32))
        RSTY = _esp.enter_context(nc.sbuf_tensor("RSTY", [128, NT], F32))
        MEAN2 = _esp.enter_context(nc.sbuf_tensor("MEAN2", [128, NT], F32))
        CC2 = _esp.enter_context(nc.sbuf_tensor("CC2", [128, NT], F32))
        SCQ = _esp.enter_context(nc.sbuf_tensor("SCQ", [128, NT], F32))
        s1col = _esp.enter_context(nc.sbuf_tensor("s1col", [128, 1], F32))
        s2col = _esp.enter_context(nc.sbuf_tensor("s2col", [128, 1], F32))
        zcol = _esp.enter_context(nc.sbuf_tensor("zcol", [128, 1], F32))
        onecol = _esp.enter_context(nc.sbuf_tensor("onecol", [128, 1], F32))
        nonecol = _esp.enter_context(nc.sbuf_tensor("nonecol", [128, 1], F32))

        def L0(sy):
            for dst, src in [(cidb, CIDB), (cidf, CIDF), (maskB, CMB), (maskM, CMM),
                             (ones1, CON1), (one128, CON128), (gnsb, GN), (babsb, BAB),
                         (crep, CREP), (maskB4, CMB4), (maskM4, CMM4)]:
                P.inc(sy.dma_start(dst[:, :], src.ap()), "ld", 16)
            P.inc(sy.dma_start(nbab_a[:, 0:1], BABT.ap()[0:16, :]), "ld", 16)
            P.inc(sy.dma_start(nbab_b[:, 0:1], BABT.ap()[16:32, :]), "ld", 16)
            dma_x(P, sy, xbuf[0], X.ap(), 0)
        blk.sync(L0)

        def L0g(g):
            P.wait(g, "ld")
            g.tensor_scalar(nbab_a[:, 0:1], nbab_a[:, 0:1], -1.0, None, ALU.mult)
            g.tensor_scalar(nbab_b[:, 0:1], nbab_b[:, 0:1], -1.0, None, ALU.mult)
            g.memset(zcol[:, 0:1], 0.0)
            g.memset(onecol[:, 0:1], 1.0)
            P.inc(g.memset(nonecol[:, 0:1], -1.0), "gp", 1)
        blk.gpsimd(L0g)

        act_tab(P, 6)

        _es0 = ExitStack()
        wb0 = _es0.enter_context(nc.sbuf_tensor("wb0", [128, HID], F32))
        wb1 = _es0.enter_context(nc.sbuf_tensor("wb1", [128, HID], F32))
        wpart = _es0.enter_context(nc.sbuf_tensor("wpart", [128, 32], F32))
        wsum = _es0.enter_context(nc.sbuf_tensor("wsum", [128, 1], F32))
        wcmp = _es0.enter_context(nc.sbuf_tensor("wcmp", [128, HID], F32))
        terb = _es0.enter_context(nc.sbuf_tensor("terb", [128, HID], BF16))
        tslab = _es0.enter_context(nc.sbuf_tensor("tslab", [128, 1024], BF16))
        halfs = _es0.enter_context(nc.sbuf_tensor("halfs", [128, 1], F32))
        nhalfs = _es0.enter_context(nc.sbuf_tensor("nhalfs", [128, 1], F32))
        stot = _es0.enter_context(nc.sbuf_tensor("stot", [1, 2], F32))
        wabs = _es0.enter_context(nc.sbuf_tensor("wabs", [128, HID], F32))
        pq0 = _es0.enter_context(nc.psum_tensor("pq0", [128, 512], F32))
        pqt = _es0.enter_context(nc.psum_tensor("pqt", [128, 1024], BF16))
        if True:
            for wi, (WSRC, WDST, NTW, FW, WDc, scol) in enumerate([
                    (W1, W1T, 32, D, HID, s1col), (W2, W2T, 8, HID, D, s2col)][:nq]):
                wap = WSRC.ap().rearrange("(t p) f -> t p f", p=128)
                bufs = [wb0, wb1]

                base_ld = P.cnt["ld"]
                def p1ld(sy, base=base_ld):
                    for t in range(NTW):
                        if t >= 2:
                            P.wait(sy, "sc", 0)
                        P.inc(sy.dma_start(bufs[t % 2][:, 0:FW], wap[t]), "ld", 16)
                sc_base = P.cnt["sc"]
                def p1(sy):
                    for t in range(NTW):
                        if t >= 2:
                            sy.wait_ge(P.sems["sc"], sc_base + (t - 1))
                        P.inc(sy.dma_start(bufs[t % 2][:, 0:FW], wap[t]), "ld", 16)
                blk.sync(p1)

                def p1s(sc, base=base_ld):
                    P.wait(sc, "gp")
                    for t in range(NTW):
                        sc.wait_ge(P.sems["ld"], base + 16 * (t + 1))
                        P.flush(sc, sc.activation(wabs[:, 0:FW], bufs[t % 2][:, 0:FW],
                                                   ACT.Abs,
                                                   accum_out=wpart[:, t:t + 1],
                                                   bias=zcol[:, 0:1],
                                                   scale=onecol[:, 0:1]), "sc")
                blk.scalar(p1s)

                def p1v(v):
                    P.wait(v, "sc")
                    P.flush(v, v.tensor_reduce(wsum[:, 0:1], wpart[:, 0:NTW], AX.X,
                                               ALU.add), "dv")
                blk.vector(p1v)

                def p1pe(te):
                    P.wait(te, "dv")
                    te.matmul(pq0[0:1, 0:1], one128[:, 0:1], wsum[:, 0:1],
                              start=True, stop=True)
                    P.inc(te.last, "pe", 1)
                blkT(p1pe)

                def p1v2(v, wi=wi):
                    P.wait(v, "pe")
                    P.flush(v, v.tensor_scalar(stot[0:1, wi:wi + 1], pq0[0:1, 0:1],
                                                1.0 / (NTW * 128 * FW), 1e-5,
                                                ALU.mult, ALU.max), "dv")
                blk.vector(p1v2)

                def p1pe2(te, wi=wi):
                    P.wait(te, "dv")
                    te.matmul(pq0[:, 1:2], ones1[0:1, :], stot[0:1, wi:wi + 1],
                              start=True, stop=True)
                    P.inc(te.last, "pe", 1)
                blkT(p1pe2)

                def p1v3(v, scol=scol):
                    P.wait(v, "pe")
                    P.flush(v, v.tensor_scalar(halfs[:, 0:1], pq0[:, 1:2], 0.5, None,
                                                ALU.mult), "dv")
                    P.flush(v, v.tensor_scalar(nhalfs[:, 0:1], pq0[:, 1:2], -0.5, None,
                                               ALU.mult), "dv")
                    P.flush(v, v.tensor_scalar(scol[:, 0:1], pq0[:, 1:2], -1.0, None,
                                               ALU.mult), "dv")
                blk.vector(p1v3)
                ev_scales = P.cnt["dv"]

                base2 = P.cnt["ld"]
                nblk = FW // 128
                for t in range(NTW):
                    def p2ld(sy, t=t):
                        if t >= 2:
                            P.wait(sy, "dv")
                        P.inc(sy.dma_start(bufs[t % 2][:, 0:FW], wap[t]), "ld", 16)
                    blk.sync(p2ld)
                    ld_t = P.cnt["ld"]

                    def p2v(v, t=t, ld_t=ld_t):
                        v.wait_ge(P.sems["ld"], ld_t)
                        if t == 0:
                            P.wait(v, "dv", ev_scales)
                        if t >= 1:
                            P.wait(v, "gp")
                        P.flush(v, v.tensor_scalar(wcmp[:, 0:FW], bufs[t % 2][:, 0:FW],
                                                    halfs[:, 0:1], None, ALU.is_gt),
                                "dv")
                        P.inc(v.scalar_tensor_tensor(terb[:, 0:FW], bufs[t % 2][:, 0:FW],
                                                     nhalfs[:, 0:1], wcmp[:, 0:FW],
                                                     ALU.is_lt, ALU.subtract), "dv", 1)
                    blk.vector(p2v)
                    ev_ter = P.cnt["dv"]

                    for w0 in range(0, nblk, 8):
                        def p2pe(te, w0=w0, ev=ev_ter):
                            P.wait(te, "dv", ev)
                            P.wait(te, "sc")
                            for b in range(8):
                                te.matmul(pqt[:, b * 128:(b + 1) * 128],
                                          terb[:, (w0 + b) * 128:(w0 + b + 1) * 128],
                                          cidb[:, :], start=True, stop=True,
                                          is_transpose=True)
                            P.inc(te.last, "pe", 1)
                        blkT(p2pe)
                        ev_tr = P.cnt["pe"]

                        def p2gp(sc, ev=ev_tr):
                            P.wait(sc, "pe", ev)
                            P.wait(sc, "st")
                            P.inc(sc.activation(tslab[:, :], pqt[:, :], ACT.Copy,
                                                bias=0.0, scale=onecol[:, 0:1]), "sc", 1)
                        blk.scalar(p2gp)
                        ev_ev = P.cnt["sc"]

                        def p2st(sy, t=t, w0=w0, ev=ev_ev, WDST=WDST, WDc=WDc):
                            P.wait(sy, "sc", ev)
                            off = (w0 * 128) * WDc + t * 128
                            dst = bass.AP(WDST, off, [[WDc, 128], [128 * WDc, 8], [1, 128]])
                            P.inc(sy.dma_start(
                                dst, tslab[:, :].rearrange("p (b f) -> p b f", b=8)),
                                "st", 16)
                        blk.sync(p2st)
        _es0.close()
        quant_st_done = P.cnt["st"]

        _es = ExitStack()
        Wk_s = _es.enter_context(nc.sbuf_tensor("Wk_s", [128, 8, D], BF16))
        Wq_s = _es.enter_context(nc.sbuf_tensor("Wq_s", [128, 8, D], BF16))
        Wv_s = _es.enter_context(nc.sbuf_tensor("Wv_s", [128, 8, D], BF16))
        Wab_s = _es.enter_context(nc.sbuf_tensor("Wab_s", [128, 8, 2 * H], BF16))
        xln = _es.enter_context(nc.sbuf_tensor("xln", [128, D], BF16))
        xlnT = _es.enter_context(nc.sbuf_tensor("xlnT", [128, D], BF16))
        Ktok = _es.enter_context(nc.sbuf_tensor("Ktok", [128, D], BF16))
        Qtok = _es.enter_context(nc.sbuf_tensor("Qtok", [128, D], BF16))
        Vtok = _es.enter_context(nc.sbuf_tensor("Vtok", [128, D], BF16))
        Ktb = _es.enter_context(nc.sbuf_tensor("Ktb", [128, H * 128], BF16))
        Qtb = _es.enter_context(nc.sbuf_tensor("Qtb", [128, H * 128], BF16))
        Kfb = _es.enter_context(nc.sbuf_tensor("Kfb", [128, H * 128], BF16))
        Qfb = _es.enter_context(nc.sbuf_tensor("Qfb", [128, H * 128], BF16))
        Cf1 = _es.enter_context(nc.sbuf_tensor("Cf1", [128, H * 128], BF16))
        Cf2 = _es.enter_context(nc.sbuf_tensor("Cf2", [128, H * 128], BF16))
        Cf4 = _es.enter_context(nc.sbuf_tensor("Cf4", [128, H * 128], BF16))
        Cf8 = _es.enter_context(nc.sbuf_tensor("Cf8", [128, H * 128], BF16))
        Cf16 = _es.enter_context(nc.sbuf_tensor("Cf16", [128, H * 128], BF16))
        Cf32 = _es.enter_context(nc.sbuf_tensor("Cf32", [128, H * 128], BF16))
        Dsl = _es.enter_context(nc.sbuf_tensor("Dsl", [128, 8 * 128], BF16))
        Mts = _es.enter_context(nc.sbuf_tensor("Mts", [128, H * 128], BF16))
        Usb = _es.enter_context(nc.sbuf_tensor("Usb", [128, H * HD], BF16))
        Usc = _es.enter_context(nc.sbuf_tensor("Usc", [128, H * HD], BF16))
        Ssl = _es.enter_context(nc.sbuf_tensor("Ssl", [128, H * HD], F32))
        Sbf = _es.enter_context(nc.sbuf_tensor("Sbf", [128, H * HD], BF16))
        ytile = _es.enter_context(nc.sbuf_tensor("ytile", [128, D], BF16))
        Escr = _es.enter_context(nc.sbuf_tensor("Escr", [128, 1024], F32))
        Epre = _es.enter_context(nc.sbuf_tensor("Epre", [128, 1024], BF16))
        scrA = _es.enter_context(nc.sbuf_tensor("scrA", [128, 1024], F32))
        stat32 = _es.enter_context(nc.sbuf_tensor("stat32", [128, 32], F32))
        st_hi = _es.enter_context(nc.sbuf_tensor("st_hi", [128, 32], BF16))
        st_lo = _es.enter_context(nc.sbuf_tensor("st_lo", [128, 32], BF16))
        Prow = _es.enter_context(nc.sbuf_tensor("Prow", [16, 128], F32))
        zero16 = _es.enter_context(nc.sbuf_tensor("zero16", [16, 128], F32))
        zaT = _es.enter_context(nc.sbuf_tensor("zaT", [16, 128], F32))
        zbT = _es.enter_context(nc.sbuf_tensor("zbT", [16, 128], F32))
        nlaT = _es.enter_context(nc.sbuf_tensor("nlaT", [16, 128], F32))
        nlbT = _es.enter_context(nc.sbuf_tensor("nlbT", [16, 128], F32))
        lkT = _es.enter_context(nc.sbuf_tensor("lkT", [16, 128], F32))
        lqT = _es.enter_context(nc.sbuf_tensor("lqT", [16, 128], F32))
        r_c1n = _es.enter_context(nc.sbuf_tensor("r_c1n", [16, 128], F32))
        r_s1b = _es.enter_context(nc.sbuf_tensor("r_s1b", [16, 128], F32))
        r_s1m = _es.enter_context(nc.sbuf_tensor("r_s1m", [16, 128], F32))
        r_rhsc = _es.enter_context(nc.sbuf_tensor("r_rhsc", [16, 128], BF16))
        r_g = _es.enter_context(nc.sbuf_tensor("r_g", [16, 128], BF16))
        r_rc = _es.enter_context(nc.sbuf_tensor("r_rc", [16, 128], BF16))
        r_sigb = _es.enter_context(nc.sbuf_tensor("r_sigb", [16, 128], BF16))
        r_gc = _es.enter_context(nc.sbuf_tensor("r_gc", [16, 128], BF16))
        r_gcpre = _es.enter_context(nc.sbuf_tensor("r_gcpre", [16, 128], F32))
        r_rcpre = _es.enter_context(nc.sbuf_tensor("r_rcpre", [16, 128], F32))
        c1n_hi = _es.enter_context(nc.sbuf_tensor("c1n_hi", [16, 128], BF16))
        c1n_lo = _es.enter_context(nc.sbuf_tensor("c1n_lo", [16, 128], BF16))
        s1b_hi = _es.enter_context(nc.sbuf_tensor("s1b_hi", [16, 128], BF16))
        s1b_lo = _es.enter_context(nc.sbuf_tensor("s1b_lo", [16, 128], BF16))
        s1m_hi = _es.enter_context(nc.sbuf_tensor("s1m_hi", [16, 128], BF16))
        s1m_lo = _es.enter_context(nc.sbuf_tensor("s1m_lo", [16, 128], BF16))
        tsplit = _es.enter_context(nc.sbuf_tensor("tsplit", [16, 128], F32))
        backTb = _es.enter_context(nc.sbuf_tensor("backTb", [128, 80], BF16))
        backTc = _es.enter_context(nc.sbuf_tensor("backTc", [128, 16], F32))
        ysq = _es.enter_context(nc.sbuf_tensor("ysq", [128, 1], F32))
        mcol = _es.enter_context(nc.sbuf_tensor("mcol", [128, 1], F32))
        vcol = _es.enter_context(nc.sbuf_tensor("vcol", [128, 1], F32))
        negm = _es.enter_context(nc.sbuf_tensor("negm", [128, 1], F32))
        gate = _es.enter_context(nc.sbuf_tensor("gate", [128, D], BF16))
        x2t = _es.enter_context(nc.sbuf_tensor("x2t", [128, D], F32))
        dstage = _es.enter_context(nc.sbuf_tensor("dstage", [128, 2048], F32)) if dbg else None
        pA = _es.enter_context(nc.psum_tensor("pA", [128, 1024], F32))
        pB = _es.enter_context(nc.psum_tensor("pB", [128, 1024], F32))
        pC = _es.enter_context(nc.psum_tensor("pC", [128, 1024], F32))
        pD = _es.enter_context(nc.psum_tensor("pD", [128, 512], F32))
        pT = _es.enter_context(nc.psum_tensor("pT", [128, 1024], BF16))
        if True:
            Cfs = {1: Cf1, 2: Cf2, 4: Cf4, 8: Cf8, 16: Cf16, 32: Cf32}

            def LW(g):
                P.wait(g, "st")
                P.wait(g, "dv")
                P.wait(g, "sc")
                P.wait(g, "pe")
                for dst, src, cols in [(Wk_s, WK, D), (Wq_s, WQ, D), (Wv_s, WV, D),
                                       (Wab_s, WAB, 2 * H)]:
                    sap = bass.AP(src, 0, [[cols, 128], [128 * cols, 8], [1, cols]])
                    P.inc(g.dma_start(dst[:, :, :], sap), "lg", 16)
                g.memset(Ktb[:, :], 0.0)
                g.memset(Qtb[:, :], 0.0)
                g.memset(Ssl[:, :], 0.0)
                g.memset(Sbf[:, :], 0.0)
                P.inc(g.memset(zero16[:, :], 0.0), "gp", 1)
            blk.gpsimd(LW)

            def dbgdump(slot, src_ap, cols, j):
                if not dbg or j != 0:
                    return

                def dv(v):
                    P.wait(v, "pe")
                    P.wait(v, "sc")
                    P.wait(v, "gp")
                    P.inc(v.tensor_copy(dstage[:, 0:cols], src_ap), "dv", 1)
                blk.vector(dv)

                def dsy(sy):
                    P.wait(sy, "dv")
                    P.inc(sy.dma_start(DBG.ap()[slot, :, 0:cols], dstage[:, 0:cols]),
                          "st", 16)
                blk.sync(dsy)

            R_RHSC, R_G, R_RC, R_SIGB, R_GC = 0, 16, 32, 48, 64

            for j in range(na1):
                xb = xbuf[j % 2]
                if j + 1 < NT:
                    def pref(sy, j=j):
                        P.wait(sy, "dv")
                        dma_x(P, sy, xbuf[(j + 1) % 2], X.ap(), j + 1)
                    blk.sync(pref)

                ld_need = P.cnt["ld"]

                def a1(v, j=j, xb=xb):
                    v.wait_ge(P.sems["ld"], ld_need)
                    if j == 0:
                        P.wait(v, "gp")
                    P.flush(v, v.tensor_reduce(mcol[:, 0:1], xb[:, :], AX.X, ALU.add),
                            "dv")
                    P.flush(v, v.tensor_scalar(mcol[:, 0:1], mcol[:, 0:1], 1.0 / D,
                                               None, ALU.mult), "dv")
                    P.flush(v, v.tensor_scalar(negm[:, 0:1], mcol[:, 0:1], -1.0, None,
                                               ALU.mult), "dv")
                blk.vector(a1)

                def a2s(sc, xb=xb):
                    P.wait(sc, "dv")
                    P.flush(sc, sc.activation(scrA[:, :], xb[:, :], ACT.Square,
                                               bias=negm[:, 0:1], scale=onecol[:, 0:1],
                                               accum_out=vcol[:, 0:1]), "sc")
                blk.scalar(a2s)

                def a2v(v):
                    P.wait(v, "sc")
                    P.flush(v, v.tensor_scalar(vcol[:, 0:1], vcol[:, 0:1], 1.0 / D, 1e-5,
                                               ALU.mult, ALU.add), "dv")
                blk.vector(a2v)

                def a2s2(sc):
                    P.wait(sc, "dv")
                    P.flush(sc, sc.activation(vcol[:, 0:1], vcol[:, 0:1], ACT.Ln,
                                              bias=zcol[:, 0:1], scale=onecol[:, 0:1]),
                            "sc")
                blk.scalar(a2s2)

                def a2v2(v):
                    P.wait(v, "sc")
                    P.flush(v, v.tensor_scalar(vcol[:, 0:1], vcol[:, 0:1], -0.5, None,
                                               ALU.mult), "dv")
                blk.vector(a2v2)

                def a2s3(sc):
                    P.wait(sc, "dv")
                    P.flush(sc, sc.activation(vcol[:, 0:1], vcol[:, 0:1], ACT.Exp,
                                              bias=zcol[:, 0:1], scale=onecol[:, 0:1]),
                            "sc")
                blk.scalar(a2s3)

                def a3v(v, j=j, xb=xb):
                    P.wait(v, "sc")
                    P.inc(v.tensor_scalar(xln[:, :], xb[:, :], mcol[:, 0:1], vcol[:, 0:1],
                                          ALU.subtract, ALU.mult), "dv", 1)
                    v.tensor_copy(LNm[:, j:j + 1], mcol[:, 0:1])
                    P.inc(v.tensor_copy(LNr[:, j:j + 1], vcol[:, 0:1]), "dv", 1)
                blk.vector(a3v)

                dbgdump(0, xln[:, :], D, j)
                def a4pe(te):
                    P.wait(te, "dv")
                    P.wait(te, "sc")
                    for p in range(8):
                        te.matmul(pT[:, p * 128:(p + 1) * 128], xln[:, p * 128:(p + 1) * 128],
                                  cidb[:, :], start=True, stop=True, is_transpose=True)
                    P.inc(te.last, "pe", 1)
                blkT(a4pe)

                def a4gp(sc):
                    P.wait(sc, "pe")
                    P.inc(sc.activation(xlnT[:, :], pT[:, :], ACT.Copy,
                                        bias=0.0, scale=onecol[:, 0:1]), "sc", 1)
                blk.scalar(a4gp)

                if A1STAGE < 2:
                    continue
                def a5pe(te):
                    P.wait(te, "sc")
                    P.wait(te, "gp")
                    P.wait(te, "lg")
                    P.wait(te, "dv")
                    P.wait(te, "sc")
                    for p in range(8):
                        for ps, Wsb in [(pA, Wk_s), (pB, Wq_s), (pC, Wv_s)]:
                            te.matmul(ps[:, 0:512], xlnT[:, p * 128:(p + 1) * 128],
                                      Wsb[:, p, 0:512], start=(p == 0), stop=(p == 7))
                            te.matmul(ps[:, 512:1024], xlnT[:, p * 128:(p + 1) * 128],
                                      Wsb[:, p, 512:1024], start=(p == 0), stop=(p == 7))
                        te.matmul(pD[0:16, 0:128], Wab_s[:, p, 0:16],
                                  xlnT[:, p * 128:(p + 1) * 128],
                                  start=(p == 0), stop=False)
                        te.matmul(pD[0:16, 128:256], Wab_s[:, p, 16:32],
                                  xlnT[:, p * 128:(p + 1) * 128],
                                  start=False, stop=(p == 7))
                    P.inc(te.last, "pe", 1)
                blkT(a5pe)
                ev_proj = P.cnt["pe"]

                def a6v(v):
                    P.wait(v, "pe", ev_proj)
                    P.inc(v.tensor_copy(Ktok[:, :], pA[:, :]), "dv", 1)
                blk.vector(a6v)

                def a6g(v):
                    P.wait(v, "pe", ev_proj)
                    P.inc(v.tensor_copy(Qtok[:, :], pB[:, :]), "dv", 1)
                    P.inc(v.tensor_copy(Vtok[:, :], pC[:, :]), "dv", 1)
                blk.vector(a6g)

                def a6s(sc):
                    P.wait(sc, "pe", ev_proj)
                    P.wait(sc, "dv")
                    P.inc(sc.activation(scrA[:, :], pA[:, :], ACT.Square, bias=zcol[:, 0:1], scale=onecol[:, 0:1]), "sc", 1)
                blk.scalar(a6s)

                def a7v(v):
                    P.wait(v, "sc")
                    P.flush(v, v.tensor_reduce(
                        stat32[:, 0:16].rearrange("p (a b) -> p a b", a=16, b=1),
                        scrA[:, :].rearrange("p (a b) -> p a b", a=16, b=64),
                        AX.X, ALU.add), "dv")
                blk.vector(a7v)

                def a7s(sc):
                    P.wait(sc, "dv")
                    P.inc(sc.activation(scrA[:, :], pB[:, :], ACT.Square, bias=zcol[:, 0:1], scale=onecol[:, 0:1]), "sc", 1)
                blk.scalar(a7s)

                def a8v(v):
                    P.wait(v, "sc")
                    P.flush(v, v.tensor_reduce(
                        stat32[:, 16:32].rearrange("p (a b) -> p a b", a=16, b=1),
                        scrA[:, :].rearrange("p (a b) -> p a b", a=16, b=64),
                        AX.X, ALU.add), "dv")
                    P.inc(v.tensor_copy(Sbf[:, :], Ssl[:, :]), "dv", 1)
                blk.vector(a8v)

                dbgdump(1, Ktok[:, :], D, j)
                dbgdump(2, Vtok[:, :], D, j)
                if A1STAGE < 3:
                    continue
                def a10s(sc):
                    P.wait(sc, "pe")
                    P.wait(sc, "dv")
                    P.inc(sc.activation(zaT[:, :], pD[0:16, 0:128], ACT.Exp,
                                        bias=nbab_a[:, 0:1], scale=nonecol[0:16, 0:1]),
                          "sc", 1)
                    P.inc(sc.activation(zbT[:, :], pD[0:16, 128:256], ACT.Exp,
                                        bias=nbab_b[:, 0:1], scale=nonecol[0:16, 0:1]),
                          "sc", 1)
                blk.scalar(a10s)

                def a9v(v):
                    P.wait(v, "sc")
                    P.flush(v, v.tensor_copy(st_hi[:, :], stat32[:, :]), "dv")
                    P.flush(v, v.tensor_sub(st_lo[:, :], stat32[:, :], st_hi[:, :]),
                            "dv")
                    P.inc(v.tensor_scalar(zaT[:, :], zaT[:, :], 1.0, None, ALU.add),
                          "dv", 1)
                    P.inc(v.tensor_scalar(zbT[:, :], zbT[:, :], 1.0, None, ALU.add),
                          "dv", 1)
                blk.vector(a9v)

                def a9pe(te):
                    P.wait(te, "dv")
                    P.wait(te, "sc")
                    te.matmul(pT[0:16, 0:128], st_hi[:, 0:16], cidb[:, :],
                              start=True, stop=True, is_transpose=True)
                    te.matmul(pT[0:16, 128:256], st_lo[:, 0:16], cidb[:, :],
                              start=True, stop=True, is_transpose=True)
                    te.matmul(pT[0:16, 256:384], st_hi[:, 16:32], cidb[:, :],
                              start=True, stop=True, is_transpose=True)
                    te.matmul(pT[0:16, 384:512], st_lo[:, 16:32], cidb[:, :],
                              start=True, stop=True, is_transpose=True)
                    P.inc(te.last, "pe", 1)
                blkT(a9pe)

                def a9v2(v):
                    P.wait(v, "pe")
                    P.flush(v, v.tensor_copy(lkT[:, :], pT[0:16, 0:128]), "dv")
                    P.flush(v, v.tensor_add(lkT[:, :], lkT[:, :], pT[0:16, 128:256]),
                            "dv")
                    P.flush(v, v.tensor_copy(lqT[:, :], pT[0:16, 256:384]), "dv")
                    P.flush(v, v.tensor_add(lqT[:, :], lqT[:, :], pT[0:16, 384:512]),
                            "dv")
                    P.flush(v, v.tensor_scalar(lkT[:, :], lkT[:, :], 1e-24, None,
                                               ALU.add), "dv")
                    P.flush(v, v.tensor_scalar(lqT[:, :], lqT[:, :], 1e-24, None,
                                               ALU.add), "dv")
                blk.vector(a9v2)

                if dbg and j == 0:
                    def dpre(v):
                        P.wait(v, "pe")
                        v.tensor_copy(dstage[0:128, 0:32], stat32[:, :])
                        v.tensor_copy(dstage[0:128, 32:64], st_hi[:, :])
                        v.tensor_copy(dstage[0:128, 64:96], st_lo[:, :])
                        v.tensor_copy(dstage[0:16, 96:224], lkT[:, :])
                        v.tensor_copy(dstage[0:16, 224:352], lqT[:, :])
                        v.tensor_copy(dstage[0:16, 352:480], pT[0:16, 0:128])
                        P.inc(v.tensor_copy(dstage[0:16, 480:608], pT[0:16, 128:256]),
                              "dv", 1)
                    blk.vector(dpre)

                    def dpre2(sy):
                        P.wait(sy, "dv")
                        P.inc(sy.dma_start(DBG.ap()[3], dstage[:, :]), "st", 16)
                    blk.sync(dpre2)

                def a11s(sc):
                    P.wait(sc, "dv")
                    z16, o16 = zcol[0:16, 0:1], onecol[0:16, 0:1]
                    P.flush(sc, sc.activation(nlaT[:, :], zaT[:, :], ACT.Ln, bias=z16,
                                               scale=o16), "sc")
                    P.flush(sc, sc.activation(nlbT[:, :], zbT[:, :], ACT.Ln, bias=z16,
                                              scale=o16), "sc")
                    P.flush(sc, sc.activation(lkT[:, :], lkT[:, :], ACT.Ln, bias=z16,
                                              scale=o16), "sc")
                    P.flush(sc, sc.activation(lqT[:, :], lqT[:, :], ACT.Ln, bias=z16,
                                              scale=o16), "sc")
                blk.scalar(a11s)

                if A1STAGE < 4:
                    continue

                def a12v(v):
                    P.wait(v, "sc")
                    P.flush(v, v.tensor_tensor_scan(Prow[:, 0:64], nlaT[:, 0:64],
                                               zero16[:, 0:64], 0.0, ALU.add, ALU.add), "dv")
                    P.flush(v, v.tensor_tensor_scan(Prow[:, 64:128], nlaT[:, 64:128],
                                               zero16[:, 64:128], 0.0, ALU.add, ALU.add),
                          "dv")
                    P.flush(v, v.scalar_tensor_tensor(r_c1n[:, :], lkT[:, :], -0.5,
                                                 Prow[:, :], ALU.mult, ALU.add), "dv")
                    P.flush(v, v.tensor_copy(c1n_hi[:, :], r_c1n[:, :]), "dv")
                    P.flush(v, v.tensor_sub(c1n_lo[:, :], r_c1n[:, :], c1n_hi[:, :]), "dv")
                    P.flush(v, v.tensor_sub(r_s1b[:, :], nlaT[:, :], nlbT[:, :]), "dv")
                    P.flush(v, v.tensor_sub(r_s1b[:, :], r_s1b[:, :], Prow[:, :]), "dv")
                    P.flush(v, v.scalar_tensor_tensor(r_s1b[:, :], lkT[:, :], -0.5,
                                                 r_s1b[:, :], ALU.mult, ALU.add), "dv")
                    P.flush(v, v.tensor_copy(s1b_hi[:, :], r_s1b[:, :]), "dv")
                    P.flush(v, v.tensor_sub(s1b_lo[:, :], r_s1b[:, :], s1b_hi[:, :]), "dv")
                    P.flush(v, v.scalar_tensor_tensor(r_s1m[:, :], lqT[:, :], -0.5,
                                                 Prow[:, :], ALU.mult, ALU.subtract),
                          "dv")
                    P.flush(v, v.tensor_copy(s1m_hi[:, :], r_s1m[:, :]), "dv")
                    P.flush(v, v.tensor_sub(s1m_lo[:, :], r_s1m[:, :], s1m_hi[:, :]), "dv")
                    P.flush(v, v.tensor_scalar(r_rcpre[:, 0:64], r_c1n[:, 0:64],
                                          Prow[:, 63:64], None, ALU.subtract), "dv")
                    P.flush(v, v.tensor_scalar(r_rcpre[:, 64:128], r_c1n[:, 64:128],
                                          Prow[:, 127:128], None, ALU.subtract), "dv")
                    P.flush(v, v.tensor_scalar(r_gcpre[:, 0:64], zero16[:, 0:64],
                                          Prow[:, 63:64], None, ALU.subtract), "dv")
                    P.flush(v, v.tensor_scalar(r_gcpre[:, 64:128], zero16[:, 64:128],
                                          Prow[:, 127:128], None, ALU.subtract), "dv")
                blk.vector(a12v)

                def a13s(sc):
                    P.wait(sc, "dv")
                    z16, o16, n16 = zcol[0:16, 0:1], onecol[0:16, 0:1], nonecol[0:16, 0:1]
                    P.flush(sc, sc.activation(r_rhsc[:, :], r_s1b[:, :], ACT.Exp,
                                        bias=z16, scale=o16), "sc")
                    P.flush(sc, sc.activation(r_g[:, :], r_s1m[:, :], ACT.Exp,
                                        bias=z16, scale=o16), "sc")
                    P.flush(sc, sc.activation(r_rc[:, :], r_rcpre[:, :], ACT.Exp,
                                        bias=z16, scale=o16), "sc")
                    P.flush(sc, sc.activation(r_sigb[:, :], nlbT[:, :], ACT.Exp,
                                        bias=z16, scale=n16), "sc")
                    P.flush(sc, sc.activation(r_gc[:, :], r_gcpre[:, :], ACT.Exp,
                                        bias=z16, scale=o16), "sc")
                blk.scalar(a13s)

                def a14pe(te):
                    P.wait(te, "sc")
                    for i, cls in enumerate([r_rhsc, r_g, r_rc, r_sigb, r_gc,
                                             c1n_hi, c1n_lo]):
                        te.matmul(pT[:, 512 + i * 16:512 + (i + 1) * 16], cls[:, :],
                                  cidb[0:16, 0:16], start=True, stop=True,
                                  is_transpose=True)
                    P.inc(te.last, "pe", 1)
                blkT(a14pe)

                def a14v(v):
                    P.wait(v, "pe")
                    P.flush(v, v.tensor_copy(backTb[:, :], pT[:, 512:592]), "dv")
                    P.flush(v, v.tensor_copy(backTc[:, :], pT[:, 592:608]), "dv")
                    P.flush(v, v.tensor_add(backTc[:, :], backTc[:, :], pT[:, 608:624]),
                            "dv")
                blk.vector(a14v)

                if A1STAGE < 5:
                    continue
                def a15v(v):
                    P.wait(v, "pe")
                    ps_k, ps_kt = pstep(Ktok), pstep(Ktb)
                    P.inc(v.tensor_copy(
                        bass.AP(Ktb, 0, [[ps_kt, 64], [128, H], [1, 64]]),
                        bass.AP(Ktok, 0, [[ps_k, 64], [64, H], [1, 64]])), "dv", 1)
                    ps_q, ps_qt = pstep(Qtok), pstep(Qtb)
                    P.inc(v.tensor_copy(
                        bass.AP(Qtb, 0, [[ps_qt, 64], [128, H], [1, 64]]),
                        bass.AP(Qtok, 0, [[ps_q, 64], [64, H], [1, 64]])), "dv", 1)
                blk.vector(a15v)

                def a15g(g):
                    P.wait(g, "dv")
                    ps_k, ps_kt = pstep(Ktok), pstep(Ktb)
                    P.inc(g.tensor_copy(
                        bass.AP(Ktb, 64 * ps_kt + 64, [[ps_kt, 64], [128, H], [1, 64]]),
                        bass.AP(Ktok, 64 * ps_k, [[ps_k, 64], [64, H], [1, 64]])), "gp", 1)
                    ps_q, ps_qt = pstep(Qtok), pstep(Qtb)
                    P.inc(g.tensor_copy(
                        bass.AP(Qtb, 64 * ps_qt + 64, [[ps_qt, 64], [128, H], [1, 64]]),
                        bass.AP(Qtok, 64 * ps_q, [[ps_q, 64], [64, H], [1, 64]])), "gp", 1)
                blk.gpsimd(a15g)

                for (tb, fb) in [(Ktb, Kfb), (Qtb, Qfb)]:
                    for w in range(2):
                        def a16pe(te, tb=tb, w=w):
                            P.wait(te, "gp")
                            P.wait(te, "sc")
                            P.wait(te, "dv")
                            for b in range(8):
                                hh = w * 8 + b
                                te.matmul(pT[:, b * 128:(b + 1) * 128],
                                          tb[:, hh * 128:(hh + 1) * 128],
                                          cidb[:, :], start=True, stop=True,
                                          is_transpose=True)
                            P.inc(te.last, "pe", 1)
                        blkT(a16pe)

                        def a16g(sc, fb=fb, w=w):
                            P.wait(sc, "pe")
                            P.inc(sc.activation(fb[:, w * 1024:(w + 1) * 1024], pT[:, :],
                                                ACT.Copy, bias=0.0,
                                                scale=onecol[:, 0:1]), "sc", 1)
                        blk.scalar(a16g)

                if A1STAGE < 6:
                    continue
                for half in range(2):
                    h0 = half * 8

                    def g1(te, h0=h0):
                        P.wait(te, "gp")
                        P.wait(te, "dv")
                        P.wait(te, "sc")
                        for hh in range(8):
                            ksl = Kfb[:, (h0 + hh) * 128:(h0 + hh + 1) * 128]
                            te.matmul(pA[:, hh * 128:(hh + 1) * 128], ksl, ksl,
                                      start=True, stop=True)
                            te.matmul(pC[:, hh * 128:(hh + 1) * 128], ksl,
                                      Qfb[:, (h0 + hh) * 128:(h0 + hh + 1) * 128],
                                      start=True, stop=True)
                            te.matmul(pD[:, hh * 64:(hh + 1) * 64], ksl,
                                      Sbf[:, (h0 + hh) * 64:(h0 + hh + 1) * 64],
                                      start=True, stop=True)
                        for bank in range(2):
                            for q in range(4):
                                hh = bank * 4 + q
                                oh = crep[:, (h0 + hh) * 128:(h0 + hh + 1) * 128]
                                te.matmul(pB[:, hh * 128:(hh + 1) * 128], oh,
                                          s1b_hi[:, :], start=(q == 0), stop=False)
                                te.matmul(pB[:, hh * 128:(hh + 1) * 128], oh,
                                          s1b_lo[:, :], start=False, stop=False)
                            te.matmul(pB[:, bank * 512:(bank + 1) * 512], cidb[:, :],
                                      maskB4[:, :], start=False, stop=True)
                        P.inc(te.last, "pe", 1)
                    blkT(g1)

                    def e1(v, h0=h0):
                        P.wait(v, "pe")
                        cb = bass.AP(backTc, h0,
                                     [[pstep(backTc), 128], [1, 8], [0, 128]])
                        P.inc(v.tensor_tensor(
                            Escr[:, :].rearrange("p (a b) -> p a b", a=8, b=128),
                            pB[:, :].rearrange("p (a b) -> p a b", a=8, b=128),
                            cb, ALU.add), "dv", 1)
                    blk.vector(e1)

                    def e1s(sc):
                        P.wait(sc, "dv")
                        P.inc(sc.activation(Epre[:, :], Escr[:, :], ACT.Exp, bias=zcol[:, 0:1], scale=onecol[:, 0:1]), "sc", 1)
                    blk.scalar(e1s)

                    def e1m(v, h0=h0):
                        P.wait(v, "sc")
                        P.inc(v.tensor_tensor(Cf1[:, h0 * 128:(h0 + 8) * 128],
                                              Epre[:, :], pA[:, :], ALU.mult), "dv", 1)
                        rb = bass.AP(backTb, R_RHSC + h0,
                                     [[pstep(backTb), 128], [1, 8], [0, 64]])
                        P.inc(v.tensor_tensor(
                            scrA[:, 0:512].rearrange("p (a b) -> p a b", a=8, b=64),
                            pD[:, 0:512].rearrange("p (a b) -> p a b", a=8, b=64),
                            rb, ALU.mult), "dv", 1)
                        sb = bass.AP(backTb, R_SIGB + h0,
                                     [[pstep(backTb), 128], [1, 8], [0, 64]])
                        P.inc(v.tensor_tensor(
                            scrA[:, 512:1024].rearrange("p (a b) -> p a b", a=8, b=64),
                            Vtok[:, h0 * 64:(h0 + 8) * 64].rearrange(
                                "p (a b) -> p a b", a=8, b=64),
                            sb, ALU.mult), "dv", 1)
                        P.inc(v.tensor_sub(Usb[:, h0 * 64:(h0 + 8) * 64],
                                           scrA[:, 512:1024], scrA[:, 0:512]), "dv", 1)
                    blk.vector(e1m)

                    if A1STAGE < 7:
                        continue
                    if half == 0:
                        dbgdump(4, Cf1[:, 0:1024], 1024, j)
                        dbgdump(5, Usb[:, 0:512], 512, j)
                    for k in [1, 2, 4, 8, 16]:
                        def sq1(te, k=k, h0=h0):
                            P.wait(te, "dv")
                            P.wait(te, "sc")
                            for b in range(8):
                                te.matmul(pT[:, b * 128:(b + 1) * 128],
                                          Cfs[k][:, (h0 + b) * 128:(h0 + b + 1) * 128],
                                          cidb[:, :], start=True, stop=True,
                                          is_transpose=True)
                            P.inc(te.last, "pe", 1)
                        blkT(sq1)

                        def sq2(sc):
                            P.wait(sc, "pe")
                            P.inc(sc.activation(Dsl[:, :], pT[:, :], ACT.Copy,
                                                bias=0.0, scale=onecol[:, 0:1]), "sc", 1)
                        blk.scalar(sq2)

                        def sq3(te, k=k, h0=h0):
                            P.wait(te, "sc")
                            P.wait(te, "dv")
                            for b in range(8):
                                te.matmul(pA[:, b * 128:(b + 1) * 128],
                                          Dsl[:, b * 128:(b + 1) * 128],
                                          Cfs[k][:, (h0 + b) * 128:(h0 + b + 1) * 128],
                                          start=True, stop=True)
                            P.inc(te.last, "pe", 1)
                        blkT(sq3)

                        def sq4(v, k=k, h0=h0):
                            P.wait(v, "pe")
                            P.inc(v.tensor_copy(Cfs[2 * k][:, h0 * 128:(h0 + 8) * 128],
                                                pA[:, :]), "dv", 1)
                        blk.vector(sq4)

                    for k in [32, 16, 8, 4, 2, 1]:
                        def ap1(te, k=k, h0=h0):
                            P.wait(te, "dv")
                            for b in range(8):
                                csl = Cfs[k][:, (h0 + b) * 128:(h0 + b + 1) * 128]
                                te.matmul(pD[:, b * 64:(b + 1) * 64], csl,
                                          Usb[:, (h0 + b) * 64:(h0 + b + 1) * 64],
                                          start=True, stop=True)
                            P.inc(te.last, "pe", 1)
                        blkT(ap1)

                        def ap2(v, k=k, h0=h0):
                            P.wait(v, "pe")
                            op = ALU.subtract if k == 1 else ALU.add
                            P.inc(v.tensor_tensor(Usb[:, h0 * 64:(h0 + 8) * 64],
                                                  Usb[:, h0 * 64:(h0 + 8) * 64],
                                                  pD[:, 0:512], op), "dv", 1)
                        blk.vector(ap2)

                    if A1STAGE < 8:
                        continue
                    if half == 0:
                        dbgdump(6, Usb[:, 0:512], 512, j)
                    def m1(te, h0=h0):
                        P.wait(te, "dv")
                        for bank in range(2):
                            for q in range(4):
                                hh = bank * 4 + q
                                oh = crep[:, (h0 + hh) * 128:(h0 + hh + 1) * 128]
                                te.matmul(pB[:, hh * 128:(hh + 1) * 128], oh,
                                          s1m_hi[:, :], start=(q == 0), stop=False)
                                te.matmul(pB[:, hh * 128:(hh + 1) * 128], oh,
                                          s1m_lo[:, :], start=False, stop=False)
                            te.matmul(pB[:, bank * 512:(bank + 1) * 512], cidb[:, :],
                                      maskM4[:, :], start=False, stop=True)
                        P.inc(te.last, "pe", 1)
                    blkT(m1)

                    def m2(v, h0=h0):
                        P.wait(v, "pe")
                        cb = bass.AP(backTc, h0,
                                     [[pstep(backTc), 128], [1, 8], [0, 128]])
                        P.inc(v.tensor_tensor(
                            Escr[:, :].rearrange("p (a b) -> p a b", a=8, b=128),
                            pB[:, :].rearrange("p (a b) -> p a b", a=8, b=128),
                            cb, ALU.add), "dv", 1)
                    blk.vector(m2)

                    def m3(sc):
                        P.wait(sc, "dv")
                        P.inc(sc.activation(Epre[:, :], Escr[:, :], ACT.Exp, bias=zcol[:, 0:1], scale=onecol[:, 0:1]), "sc", 1)
                    blk.scalar(m3)

                    def m4(v, h0=h0):
                        P.wait(v, "sc")
                        P.inc(v.tensor_tensor(Mts[:, h0 * 128:(h0 + 8) * 128],
                                              Epre[:, :], pC[:, :], ALU.mult), "dv", 1)
                        rc = bass.AP(backTb, R_RC + h0,
                                     [[pstep(backTb), 128], [1, 8], [0, 64]])
                        P.inc(v.tensor_tensor(
                            Usc[:, h0 * 64:(h0 + 8) * 64].rearrange(
                                "p (a b) -> p a b", a=8, b=64),
                            Usb[:, h0 * 64:(h0 + 8) * 64].rearrange(
                                "p (a b) -> p a b", a=8, b=64),
                            rc, ALU.mult), "dv", 1)
                    blk.vector(m4)

                    def o1(te, h0=h0):
                        P.wait(te, "dv")
                        for b in range(8):
                            msl = Mts[:, (h0 + b) * 128:(h0 + b + 1) * 128]
                            te.matmul(pD[:, b * 64:(b + 1) * 64], msl,
                                      Usb[:, (h0 + b) * 64:(h0 + b + 1) * 64],
                                      start=True, stop=True)
                        for b in range(8):
                            qsl = Qfb[:, (h0 + b) * 128:(h0 + b + 1) * 128]
                            te.matmul(pB[:, b * 64:(b + 1) * 64], qsl,
                                      Sbf[:, (h0 + b) * 64:(h0 + b + 1) * 64],
                                      start=True, stop=True)
                        P.inc(te.last, "pe", 1)
                    blkT(o1)

                    def o2(v, h0=h0):
                        P.wait(v, "pe")
                        gb = bass.AP(backTb, R_G + h0,
                                     [[pstep(backTb), 128], [1, 8], [0, 64]])
                        P.inc(v.tensor_tensor(
                            scrA[:, 0:512].rearrange("p (a b) -> p a b", a=8, b=64),
                            pB[:, 0:512].rearrange("p (a b) -> p a b", a=8, b=64),
                            gb, ALU.mult), "dv", 1)
                        P.inc(v.tensor_tensor(ytile[:, h0 * 64:(h0 + 8) * 64],
                                              scrA[:, 0:512], pD[:, 0:512], ALU.add),
                              "dv", 1)
                    blk.vector(o2)

                    def su1(te, h0=h0):
                        P.wait(te, "dv")
                        for b in range(8):
                            ksl = Ktb[:, (h0 + b) * 128:(h0 + b + 1) * 128]
                            te.matmul(pD[:, b * 64:(b + 1) * 64], ksl,
                                      Usc[:, (h0 + b) * 64:(h0 + b + 1) * 64],
                                      start=True, stop=True)
                        P.inc(te.last, "pe", 1)
                    blkT(su1)

                    def su2(v, h0=h0):
                        P.wait(v, "pe")
                        gc = bass.AP(backTb, R_GC + h0,
                                     [[pstep(backTb), 128], [1, 8], [0, 64]])
                        P.inc(v.tensor_tensor(
                            scrA[:, 0:512].rearrange("p (a b) -> p a b", a=8, b=64),
                            Ssl[:, h0 * 64:(h0 + 8) * 64].rearrange(
                                "p (a b) -> p a b", a=8, b=64),
                            gc, ALU.mult), "dv", 1)
                        P.inc(v.tensor_tensor(Ssl[:, h0 * 64:(h0 + 8) * 64],
                                              scrA[:, 0:512], pD[:, 0:512], ALU.add),
                              "dv", 1)
                    blk.vector(su2)

                dbgdump(7, ytile[:, :], D, j)
                def ys1(v, j=j):
                    P.flush(v, v.tensor_mul(scrA[:, :], ytile[:, :], ytile[:, :]),
                            "dv")
                    P.flush(v, v.tensor_reduce(ysq[:, 0:1], scrA[:, :], AX.X, ALU.add),
                            "dv")
                blk.vector(ys1)

                def ys1b(v):
                    P.flush(v, v.tensor_scalar(ysq[:, 0:1], ysq[:, 0:1], 1.0 / D, 1e-5,
                                               ALU.mult, ALU.add), "dv")
                blk.vector(ys1b)

                def ys2(sc, j=j):
                    P.wait(sc, "dv")
                    P.flush(sc, sc.activation(ysq[:, 0:1], ysq[:, 0:1], ACT.Ln,
                                               bias=zcol[:, 0:1], scale=onecol[:, 0:1]),
                            "sc")
                blk.scalar(ys2)

                def ys2b(v):
                    P.wait(v, "sc")
                    P.flush(v, v.tensor_scalar(ysq[:, 0:1], ysq[:, 0:1], -0.5, None,
                                               ALU.mult), "dv")
                blk.vector(ys2b)

                def ys2c(sc, j=j):
                    P.wait(sc, "dv")
                    P.flush(sc, sc.activation(RSTY[:, j:j + 1], ysq[:, 0:1], ACT.Exp,
                                               bias=zcol[:, 0:1], scale=onecol[:, 0:1]),
                            "sc")
                blk.scalar(ys2c)

                def ys3(sy, j=j):
                    P.wait(sy, "dv")
                    P.inc(sy.dma_start(YD.ap()[j], ytile[:, :]), "st", 16)
                blk.sync(ys3)

            def LW2(g):
                P.wait(g, "pe")
                for dst, src in [(Wk_s, WG), (Wq_s, WO)]:
                    sap = bass.AP(src, 0, [[D, 128], [128 * D, 8], [1, D]])
                    P.inc(g.dma_start(dst[:, :, :], sap), "lg", 16)
            blk.gpsimd(LW2)
            ld_w2 = P.cnt["lg"]

            def foldgn(v):
                v.wait_ge(P.sems["lg"], ld_w2)
                for p in range(8):
                    P.flush(v, v.tensor_scalar(Wq_s[:, p, :], Wq_s[:, p, :],
                                               gnsb[:, p:p + 1], None, ALU.mult), "dv")
            blk.vector(foldgn)

            act_tab(P, 18)

            for j in range(na2):
                xb = xbuf[j % 2]

                def b0(sy, j=j, xb=xb):
                    P.wait(sy, "dv")
                    P.wait(sy, "sc")
                    dma_x(P, sy, xb, X.ap(), j)
                    P.inc(sy.dma_start(ytile[:, :], YD.ap()[j]), "ld", 16)
                blk.sync(b0)
                ldj = P.cnt["ld"]

                def b1(v, j=j, xb=xb):
                    v.wait_ge(P.sems["ld"], ldj)
                    P.inc(v.tensor_scalar(xln[:, :], xb[:, :], LNm[:, j:j + 1],
                                          LNr[:, j:j + 1], ALU.subtract, ALU.mult),
                          "dv", 1)
                blk.vector(b1)

                def b2(te):
                    P.wait(te, "dv")
                    P.wait(te, "sc")
                    for p in range(8):
                        te.matmul(pT[:, p * 128:(p + 1) * 128],
                                  xln[:, p * 128:(p + 1) * 128],
                                  cidb[:, :], start=True, stop=True, is_transpose=True)
                    P.inc(te.last, "pe", 1)
                blkT(b2)

                def b3(sc):
                    P.wait(sc, "pe")
                    P.inc(sc.activation(xlnT[:, :], pT[:, :], ACT.Copy,
                                        bias=0.0, scale=onecol[:, 0:1]), "sc", 1)
                blk.scalar(b3)

                def b4(te):
                    P.wait(te, "lg")
                    P.wait(te, "sc")
                    for p in range(8):
                        te.matmul(pA[:, 0:512], xlnT[:, p * 128:(p + 1) * 128],
                                  Wk_s[:, p, 0:512], start=(p == 0), stop=(p == 7))
                        te.matmul(pA[:, 512:1024], xlnT[:, p * 128:(p + 1) * 128],
                                  Wk_s[:, p, 512:1024], start=(p == 0), stop=(p == 7))
                    P.inc(te.last, "pe", 1)
                blkT(b4)

                def b5(sc):
                    P.wait(sc, "pe")
                    P.inc(sc.activation(gate[:, :], pA[:, :], ACT.Silu, bias=zcol[:, 0:1], scale=onecol[:, 0:1]), "sc", 1)
                blk.scalar(b5)

                def b6(v):
                    P.wait(v, "sc")
                    P.inc(v.tensor_mul(xln[:, :], ytile[:, :], gate[:, :]), "dv", 1)
                blk.vector(b6)

                def b7(te):
                    P.wait(te, "dv")
                    P.wait(te, "sc")
                    for p in range(8):
                        te.matmul(pT[:, p * 128:(p + 1) * 128],
                                  xln[:, p * 128:(p + 1) * 128],
                                  cidb[:, :], start=True, stop=True, is_transpose=True)
                    P.inc(te.last, "pe", 1)
                blkT(b7)

                def b8(sc):
                    P.wait(sc, "pe")
                    P.inc(sc.activation(xlnT[:, :], pT[:, :], ACT.Copy,
                                        bias=0.0, scale=onecol[:, 0:1]), "sc", 1)
                blk.scalar(b8)

                def b9(te):
                    P.wait(te, "sc")
                    P.wait(te, "dv")
                    for p in range(8):
                        te.matmul(pC[:, 0:512], xlnT[:, p * 128:(p + 1) * 128],
                                  Wq_s[:, p, 0:512], start=(p == 0), stop=(p == 7))
                        te.matmul(pC[:, 512:1024], xlnT[:, p * 128:(p + 1) * 128],
                                  Wq_s[:, p, 512:1024], start=(p == 0), stop=(p == 7))
                    P.inc(te.last, "pe", 1)
                blkT(b9)

                def b10(v, j=j, xb=xb):
                    P.wait(v, "pe")
                    if dbg and j == 0:
                        P.inc(v.tensor_copy(dstage[:, 0:1024], pC[:, :]), "dv", 1)
                        P.inc(v.tensor_copy(dstage[:, 1024:1040], RSTY[:, 0:16]),
                              "dv", 1)
                    P.inc(v.scalar_tensor_tensor(x2t[:, :], pC[:, :], RSTY[:, j:j + 1],
                                                 xb[:, :], ALU.mult, ALU.add), "dv", 1)
                blk.vector(b10)
                if dbg and j == 0:
                    def db10(sy):
                        P.wait(sy, "dv")
                        P.inc(sy.dma_start(DBG.ap()[5], dstage[:, :]), "st", 16)
                    blk.sync(db10)

                def b11(sy, j=j):
                    P.wait(sy, "dv")
                    P.inc(sy.dma_start(X2D.ap()[j], x2t[:, :]), "st", 16)
                blk.sync(b11)

        _es.close()
        _esb = ExitStack()
        w1t_s = _esb.enter_context(nc.sbuf_tensor("w1t_s", [128, 8, HID], BF16))
        w2t_s = _esb.enter_context(nc.sbuf_tensor("w2t_s", [128, 32, D], BF16))
        x2b = _esb.enter_context(nc.sbuf_tensor("x2b", [128, D], F32))
        x2b2 = _esb.enter_context(nc.sbuf_tensor("x2b2", [128, D], F32))
        xqb = _esb.enter_context(nc.sbuf_tensor("xqb", [128, D], BF16))
        xqTb = _esb.enter_context(nc.sbuf_tensor("xqTb", [128, D], BF16))
        hbuf = _esb.enter_context(nc.sbuf_tensor("hbuf", [128, HID], F32))
        hqb = _esb.enter_context(nc.sbuf_tensor("hqb", [128, HID], BF16))
        hqT = _esb.enter_context(nc.sbuf_tensor("hqT", [128, HID], BF16))
        obuf = _esb.enter_context(nc.sbuf_tensor("obuf", [128, D], F32))
        bm = _esb.enter_context(nc.sbuf_tensor("bm", [128, 1], F32))
        bnm = _esb.enter_context(nc.sbuf_tensor("bnm", [128, 1], F32))
        bv = _esb.enter_context(nc.sbuf_tensor("bv", [128, 1], F32))
        bq = _esb.enter_context(nc.sbuf_tensor("bq", [128, 1], F32))
        chc = _esb.enter_context(nc.sbuf_tensor("chc", [128, 1], F32))
        c2c = _esb.enter_context(nc.sbuf_tensor("c2c", [128, 1], F32))
        bscr = _esb.enter_context(nc.sbuf_tensor("bscr", [128, 1024], F32))
        pH = _esb.enter_context(nc.psum_tensor("pH", [128, 1024], F32))
        pO = _esb.enter_context(nc.psum_tensor("pO", [128, 1024], F32))
        pTb = _esb.enter_context(nc.psum_tensor("pTb", [128, 1024], BF16))
        if True:
            def bl0(sy):
                P.wait(sy, "st")
                P.wait(sy, "dv")
                P.wait(sy, "sc")
                P.wait(sy, "pe")
                P.wait(sy, "gp")
                sap1 = bass.AP(W1T, 0, [[HID, 128], [128 * HID, 8], [1, HID]])
                P.inc(sy.dma_start(w1t_s[:, :, :], sap1), "ld", 16)
                sap2 = bass.AP(W2T, 0, [[D, 128], [128 * D, 32], [1, D]])
                P.inc(sy.dma_start(w2t_s[:, :, :], sap2), "ld", 16)
            blk.sync(bl0)

            act_tab(P, 6)

            for j in range(nbpre):
                def c0(sy, j=j):
                    P.wait(sy, "sc")
                    P.wait(sy, "dv")
                    if j == 0:
                        P.wait(sy, "st")
                    P.inc(sy.dma_start((x2b if j % 2 == 0 else x2b2)[:, :], X2D.ap()[j]),
                          "ld", 16)
                blk.sync(c0)
                ldj = P.cnt["ld"]

                def c1(v, j=j):
                    v.wait_ge(P.sems["ld"], ldj)
                    buf = x2b if j % 2 == 0 else x2b2
                    P.flush(v, v.tensor_reduce(bm[:, 0:1], buf[:, :], AX.X, ALU.add), "dv")
                    P.flush(v, v.tensor_scalar(bm[:, 0:1], bm[:, 0:1], 1.0 / D, None,
                                          ALU.mult), "dv")
                    P.flush(v, v.tensor_scalar(bnm[:, 0:1], bm[:, 0:1], -1.0, None, ALU.mult),
                          "dv")
                    v.tensor_copy(MEAN2[:, j:j + 1], bm[:, 0:1])
                    P.flush(v, v.tensor_scalar(bscr[:, :], buf[:, :], bm[:, 0:1], None,
                                          ALU.subtract), "dv")
                    P.flush(v, v.tensor_reduce(bq[:, 0:1], bscr[:, :], AX.X, ALU.max,
                                          apply_absolute_value=True), "dv")
                blk.vector(c1)

                def c2(sc, j=j):
                    P.wait(sc, "dv")
                    buf = x2b if j % 2 == 0 else x2b2
                    P.flush(sc, sc.activation(bscr[:, :], buf[:, :], ACT.Square,
                                               bias=bnm[:, 0:1], scale=onecol[:, 0:1],
                                               accum_out=bv[:, 0:1]), "sc")
                blk.scalar(c2)

                def c2v(v):
                    P.wait(v, "sc")
                    P.flush(v, v.tensor_scalar(bv[:, 0:1], bv[:, 0:1], 1.0 / D,
                                               1e-5, ALU.mult, ALU.add), "dv")
                blk.vector(c2v)

                def c2s(sc):
                    P.wait(sc, "dv")
                    P.flush(sc, sc.activation(bv[:, 0:1], bv[:, 0:1], ACT.Ln,
                                               bias=zcol[:, 0:1], scale=onecol[:, 0:1]),
                            "sc")
                blk.scalar(c2s)

                def c2v2(v):
                    P.wait(v, "sc")
                    P.flush(v, v.tensor_scalar(bv[:, 0:1], bv[:, 0:1], -0.5, None,
                                               ALU.mult), "dv")
                blk.vector(c2v2)

                def c2s2(sc):
                    P.wait(sc, "dv")
                    P.flush(sc, sc.activation(bv[:, 0:1], bv[:, 0:1], ACT.Exp,
                                               bias=zcol[:, 0:1], scale=onecol[:, 0:1]),
                            "sc")
                blk.scalar(c2s2)

                def c3(v, j=j):
                    P.wait(v, "sc")
                    P.flush(v, v.tensor_scalar(SCQ[:, j:j + 1], bq[:, 0:1], bv[:, 0:1], 1e-5,
                                          ALU.mult, ALU.max), "dv")
                    P.flush(v, v.reciprocal(bq[:, 0:1], SCQ[:, j:j + 1]), "dv")
                    P.flush(v, v.tensor_scalar(bq[:, 0:1], bq[:, 0:1], 127.0, None, ALU.mult),
                          "dv")
                    P.flush(v, v.tensor_mul(CC2[:, j:j + 1], bv[:, 0:1], bq[:, 0:1]), "dv")
                blk.vector(c3)

            act_tab(P, 10)
            ld_w = P.cnt["ld"]
            for j in range(nbmain):
                def d0(sy, j=j):
                    P.wait(sy, "dv")
                    P.inc(sy.dma_start((x2b if j % 2 == 0 else x2b2)[:, :], X2D.ap()[j]),
                          "ld", 16)
                blk.sync(d0)
                ldj = P.cnt["ld"]

                def d1(v, j=j):
                    v.wait_ge(P.sems["ld"], max(ldj, ld_w))
                    buf = x2b if j % 2 == 0 else x2b2
                    P.flush(v, v.tensor_scalar(bscr[:, :], buf[:, :], MEAN2[:, j:j + 1],
                                          CC2[:, j:j + 1], ALU.subtract, ALU.mult),
                          "dv")
                    P.flush(v, v.tensor_scalar(xqb[:, :], bscr[:, :], MAGIC, -MAGIC,
                                          ALU.add, ALU.add), "dv")
                    P.flush(v, v.tensor_scalar(chc[:, 0:1], SCQ[:, j:j + 1], s1col[:, 0:1],
                                          1.0 / 127.0, ALU.mult, ALU.mult), "dv")
                blk.vector(d1)

                def d2(te):
                    P.wait(te, "dv")
                    P.wait(te, "sc")
                    for p in range(8):
                        te.matmul(pTb[:, p * 128:(p + 1) * 128],
                                  xqb[:, p * 128:(p + 1) * 128],
                                  cidb[:, :], start=True, stop=True, is_transpose=True)
                    P.inc(te.last, "pe", 1)
                blkT(d2)

                def d3(sc):
                    P.wait(sc, "pe")
                    P.inc(sc.activation(xqTb[:, :], pTb[:, :], ACT.Copy,
                                        bias=0.0, scale=onecol[:, 0:1]), "sc", 1)
                blk.scalar(d3)
                ev_xqT = P.cnt["sc"]

                for q in range(4):
                    def d4(te, q=q):
                        P.wait(te, "sc", ev_xqT)
                        P.wait(te, "sc")
                        for p in range(8):
                            te.matmul(pH[:, 0:512], xqTb[:, p * 128:(p + 1) * 128],
                                      w1t_s[:, p, q * 1024:q * 1024 + 512],
                                      start=(p == 0), stop=(p == 7))
                            te.matmul(pH[:, 512:1024], xqTb[:, p * 128:(p + 1) * 128],
                                      w1t_s[:, p, q * 1024 + 512:(q + 1) * 1024],
                                      start=(p == 0), stop=(p == 7))
                        P.inc(te.last, "pe", 1)
                    blkT(d4)

                    def d5(sc, q=q):
                        P.wait(sc, "pe")
                        P.inc(sc.activation(hbuf[:, q * 1024:(q + 1) * 1024], pH[:, :],
                                            ACT.Gelu, bias=zcol[:, 0:1], scale=chc[:, 0:1]),
                              "sc", 1)
                    blk.scalar(d5)

                def d6(v, j=j):
                    P.wait(v, "sc")
                    P.flush(v, v.tensor_reduce(bq[:, 0:1], hbuf[:, :], AX.X, ALU.max,
                                          apply_absolute_value=True), "dv")
                    P.flush(v, v.tensor_scalar(bq[:, 0:1], bq[:, 0:1], 1e-5, None, ALU.max),
                          "dv")
                    P.flush(v, v.tensor_scalar(c2c[:, 0:1], bq[:, 0:1], s2col[:, 0:1],
                                          1.0 / 127.0, ALU.mult, ALU.mult), "dv")
                    P.flush(v, v.reciprocal(bq[:, 0:1], bq[:, 0:1]), "dv")
                    P.flush(v, v.tensor_scalar(bq[:, 0:1], bq[:, 0:1], 127.0, None, ALU.mult),
                          "dv")
                blk.vector(d6)

                qev = []

                def d6b(v):
                    for q in range(4):
                        v.tensor_scalar(bscr[:, :], hbuf[:, q * 1024:(q + 1) * 1024],
                                        bq[:, 0:1], None, ALU.mult)
                        P.inc(v.tensor_scalar(hqb[:, q * 1024:(q + 1) * 1024], bscr[:, :],
                                              MAGIC, -MAGIC, ALU.add, ALU.add), "dv", 1)
                        qev.append(P.cnt["dv"])
                blk.vector(d6b)

                ev_d8 = []
                for w in range(4):
                    def d7(te, w=w):
                        te.wait_ge(P.sems["dv"], ev_d8[w - 1] if w > 0 else qev[0])
                        for b in range(8):
                            te.matmul(pTb[:, b * 128:(b + 1) * 128],
                                      hqb[:, (w * 8 + b) * 128:(w * 8 + b + 1) * 128],
                                      cidb[:, :], start=True, stop=True,
                                      is_transpose=True)
                        P.inc(te.last, "pe", 1)
                    blkT(d7)

                    def d8(v, w=w):
                        P.wait(v, "pe")
                        P.inc(v.tensor_copy(hqT[:, w * 1024:(w + 1) * 1024], pTb[:, :]),
                              "dv", 1)
                        ev_d8.append(P.cnt["dv"])
                    blk.vector(d8)

                def d9(te):
                    P.wait(te, "dv")
                    for p in range(32):
                        te.matmul(pO[:, 0:512], hqT[:, p * 128:(p + 1) * 128],
                                  w2t_s[:, p, 0:512], start=(p == 0), stop=(p == 31))
                        te.matmul(pO[:, 512:1024], hqT[:, p * 128:(p + 1) * 128],
                                  w2t_s[:, p, 512:1024], start=(p == 0), stop=(p == 31))
                    P.inc(te.last, "pe", 1)
                blkT(d9)

                def d10(v, j=j):
                    P.wait(v, "pe")
                    buf = x2b if j % 2 == 0 else x2b2
                    P.inc(v.scalar_tensor_tensor(obuf[:, :], pO[:, :], c2c[:, 0:1],
                                                 buf[:, :], ALU.mult, ALU.add), "dv", 1)
                blk.vector(d10)

                def d11(sy, j=j):
                    P.wait(sy, "dv")
                    dma_x(P, sy, obuf, OUT.ap(), j, store=True, sem="st")
                blk.sync(d11)

        _esb.close()
        def fin(sy):
            P.wait(sy, "st")
        blk.sync(fin)

    nc.finalize()
    return nc


_NC_CACHE = None


def host_inputs(inputs, core):
    ii = np.arange(128)
    tt = np.arange(128)
    maskb = np.where(tt[None, :] > ii[:, None], 0.0, -1e30).astype(np.float32)
    maskm = np.where(tt[None, :] >= ii[:, None], 0.0, -1e30).astype(np.float32)
    d = {
        "x": np.ascontiguousarray(inputs["L0_attn_x"][2 * core:2 * core + 2]),
        "wq": inputs["L0_attn_Wq"], "wk": inputs["L0_attn_Wk"],
        "wv": inputs["L0_attn_Wv"],
        "wab": np.ascontiguousarray(
            np.concatenate([inputs["L0_attn_Wa"], inputs["L0_attn_Wb"]], axis=1)),
        "bab": np.concatenate([inputs["L0_attn_ba"],
                               inputs["L0_attn_bb"]]).reshape(1, 32),
        "wg": inputs["L0_attn_Wg"], "wo": inputs["L0_attn_Wo"],
        "gn": np.ascontiguousarray(inputs["L0_attn_g_norm"].reshape(8, 128).T),
        "w1": inputs["L0_ffn_w1"], "w2": inputs["L0_ffn_w2"],
        "cidb": np.eye(128, dtype=BFNP),
        "cidf": np.eye(128, dtype=np.float32),
        "cmaskb": maskb.astype(BFNP), "cmaskm": maskm.astype(BFNP),
        "cones": np.ones((1, 128), dtype=np.float32),
        "cone128": np.ones((128, 1), dtype=np.float32),
        "crep": np.repeat(np.eye(16, dtype=BFNP), 128, axis=1) if False else np.kron(np.eye(16), np.ones((1, 128))).astype(BFNP),
        "cmaskb4": np.tile(maskb, (1, 4)).astype(BFNP),
        "cmaskm4": np.tile(maskm, (1, 4)).astype(BFNP),
        "babt": np.concatenate([inputs["L0_attn_ba"], inputs["L0_attn_bb"]]).reshape(32, 1),
    }
    return {k: np.ascontiguousarray(np.asarray(v)) for k, v in d.items()}


def kernel(**inputs):
    global _NC_CACHE
    if _NC_CACHE is None:
        _NC_CACHE = build_program()
    nc = _NC_CACHE
    in_maps = [host_inputs(inputs, c) for c in range(NCORES)]
    res = run_bass_kernel_spmd(nc, in_maps, core_ids=list(range(NCORES)))
    out = np.empty((16, SEQ, D), dtype=np.float32)
    for c in range(NCORES):
        out[2 * c:2 * c + 2] = np.asarray(res.results[c]["out"])
    return out


# version 10
# speedup vs baseline: 1.6956x; 1.0057x over previous
import numpy as np
from contextlib import ExitStack
import ml_dtypes
import concourse.bass as bass
import concourse.mybir as mybir
from concourse.bass_utils import run_bass_kernel_spmd

F32 = mybir.dt.float32
BF16 = mybir.dt.bfloat16
ALU = mybir.AluOpType
ACT = mybir.ActivationFunctionType
AX = mybir.AxisListType
BFNP = ml_dtypes.bfloat16

NCORES = 8
SEQ = 2048
D = 1024
H = 16
HD = 64
C = 64
NT = SEQ // C
HID = 4096
MAGIC = 12582912.0
A1STAGE = 99


class TEW:

    def __init__(self, te):
        object.__setattr__(self, "_te", te)
        object.__setattr__(self, "last", None)

    def matmul(self, *a, **k):
        inst = self._te.matmul(*a, **k)
        object.__setattr__(self, "last", inst)
        return inst

    def __getattr__(self, n):
        return getattr(self._te, n)


class Prog:
    def __init__(self, nc, blk):
        self.nc = nc
        self.blk = blk
        self.sems = {}
        self.cnt = {}
        for s in ["ld", "lg", "st", "pe", "dv", "sc", "gp"]:
            self.sems[s] = nc.alloc_semaphore("sem_" + s)
            self.cnt[s] = 0

    def inc(self, inst, name, n=1):
        inst.then_inc(self.sems[name], n)
        self.cnt[name] += n
        return self.cnt[name]

    def flush(self, eng, inst, sem):
        n = self.inc(inst, sem, 1)
        eng.wait_ge(self.sems[sem], n)

    def wait(self, eng, name, val=None):
        v = self.cnt[name] if val is None else val
        if v > 0:
            eng.wait_ge(self.sems[name], v)


def pstep(t):
    return t.ap().ap[0][0]


def dma_x(P, eng, sbuf, dram_ap3, j, store=False, sem="ld"):
    for s in range(2):
        sl = sbuf[64 * s:64 * s + 64, :]
        dr = dram_ap3[s, j * C:(j + 1) * C, :]
        if store:
            P.inc(eng.dma_start(dr, sl), sem, 16)
        else:
            P.inc(eng.dma_start(sl, dr), sem, 16)


def act_tab(P, idx):
    def f(sc):
        sc.add_instruction(mybir.InstLoadActFuncSet(
            name=P.nc.get_next_instruction_name(),
            act_func_set_id=idx, ins=[], outs=[]))
    P.blk.scalar(f)


def build_program(nq=2, na1=NT, na2=NT, nbpre=NT, nbmain=NT, dbg=False):
    nc = bass.Bass("TRN2", target_bir_lowering=False, detect_race_conditions=False)

    D_IN = {}
    def din(name, shape, dt=F32):
        D_IN[name] = nc.dram_tensor(name, shape, dt, kind="ExternalInput")
        return D_IN[name]

    X = din("x", [2, SEQ, D])
    WQ = din("wq", [D, D]); WK = din("wk", [D, D]); WV = din("wv", [D, D])
    WAB = din("wab", [D, 2 * H]); BAB = din("bab", [1, 2 * H])
    BABT = din("babt", [32, 1])
    WG = din("wg", [D, D]); WO = din("wo", [D, D]); GN = din("gn", [128, 8])
    W1 = din("w1", [HID, D]); W2 = din("w2", [D, HID])
    CIDB = din("cidb", [128, 128], BF16); CIDF = din("cidf", [128, 128])
    CMB = din("cmaskb", [128, 128], BF16); CMM = din("cmaskm", [128, 128], BF16)
    CON1 = din("cones", [1, 128]); CON128 = din("cone128", [128, 1])
    CREP = din("crep", [16, H * 128], BF16)
    CMB4 = din("cmaskb4", [128, 512], BF16)
    CMM4 = din("cmaskm4", [128, 512], BF16)

    OUT = nc.dram_tensor("out", [2, SEQ, D], F32, kind="ExternalOutput")
    dkind = "ExternalOutput" if dbg else "Internal"
    DBG = nc.dram_tensor("dbgx", [8, 128, 2048], F32, kind=dkind) if dbg else None
    YD = nc.dram_tensor("ydram", [NT, 128, D], BF16, kind=dkind)
    X2D = nc.dram_tensor("x2dram", [NT, 128, D], F32, kind=dkind)
    W1T = nc.dram_tensor("w1tT", [D, HID], BF16)
    W2T = nc.dram_tensor("w2tT", [HID, D], BF16)

    with nc.Block() as blk:
        def blkT(f):
            blk.tensor(lambda te: f(TEW(te)))

        P = Prog(nc, blk)

        _esp = ExitStack()
        cidb = _esp.enter_context(nc.sbuf_tensor("cidb_s", [128, 128], BF16))
        cidf = _esp.enter_context(nc.sbuf_tensor("cidf_s", [128, 128], F32))
        maskB = _esp.enter_context(nc.sbuf_tensor("maskB_s", [128, 128], BF16))
        maskM = _esp.enter_context(nc.sbuf_tensor("maskM_s", [128, 128], BF16))
        ones1 = _esp.enter_context(nc.sbuf_tensor("ones1_s", [1, 128], F32))
        one128 = _esp.enter_context(nc.sbuf_tensor("one128_s", [128, 1], F32))
        gnsb = _esp.enter_context(nc.sbuf_tensor("gn_s", [128, 8], F32))
        babsb = _esp.enter_context(nc.sbuf_tensor("bab_s", [1, 2 * H], F32))
        crep = _esp.enter_context(nc.sbuf_tensor("crep_s", [16, H * 128], BF16))
        maskB4 = _esp.enter_context(nc.sbuf_tensor("maskB4_s", [128, 512], BF16))
        maskM4 = _esp.enter_context(nc.sbuf_tensor("maskM4_s", [128, 512], BF16))
        nbab_a = _esp.enter_context(nc.sbuf_tensor("nbab_a", [16, 1], F32))
        nbab_b = _esp.enter_context(nc.sbuf_tensor("nbab_b", [16, 1], F32))
        xbuf = [_esp.enter_context(nc.sbuf_tensor(f"xbuf{i}", [128, D], F32)) for i in range(2)]
        LNm = _esp.enter_context(nc.sbuf_tensor("LNm", [128, NT], F32))
        LNr = _esp.enter_context(nc.sbuf_tensor("LNr", [128, NT], F32))
        RSTY = _esp.enter_context(nc.sbuf_tensor("RSTY", [128, NT], F32))
        YSQ = _esp.enter_context(nc.sbuf_tensor("YSQ", [128, NT], F32))
        MEAN2 = _esp.enter_context(nc.sbuf_tensor("MEAN2", [128, NT], F32))
        CC2 = _esp.enter_context(nc.sbuf_tensor("CC2", [128, NT], F32))
        SCQ = _esp.enter_context(nc.sbuf_tensor("SCQ", [128, NT], F32))
        s1col = _esp.enter_context(nc.sbuf_tensor("s1col", [128, 1], F32))
        s2col = _esp.enter_context(nc.sbuf_tensor("s2col", [128, 1], F32))
        zcol = _esp.enter_context(nc.sbuf_tensor("zcol", [128, 1], F32))
        onecol = _esp.enter_context(nc.sbuf_tensor("onecol", [128, 1], F32))
        nonecol = _esp.enter_context(nc.sbuf_tensor("nonecol", [128, 1], F32))

        def L0(sy):
            for dst, src in [(cidb, CIDB), (cidf, CIDF), (maskB, CMB), (maskM, CMM),
                             (ones1, CON1), (one128, CON128), (gnsb, GN), (babsb, BAB),
                         (crep, CREP), (maskB4, CMB4), (maskM4, CMM4)]:
                P.inc(sy.dma_start(dst[:, :], src.ap()), "ld", 16)
            P.inc(sy.dma_start(nbab_a[:, 0:1], BABT.ap()[0:16, :]), "ld", 16)
            P.inc(sy.dma_start(nbab_b[:, 0:1], BABT.ap()[16:32, :]), "ld", 16)
            dma_x(P, sy, xbuf[0], X.ap(), 0)
        blk.sync(L0)

        def L0g(g):
            P.wait(g, "ld")
            g.tensor_scalar(nbab_a[:, 0:1], nbab_a[:, 0:1], -1.0, None, ALU.mult)
            g.tensor_scalar(nbab_b[:, 0:1], nbab_b[:, 0:1], -1.0, None, ALU.mult)
            g.memset(zcol[:, 0:1], 0.0)
            g.memset(onecol[:, 0:1], 1.0)
            P.inc(g.memset(nonecol[:, 0:1], -1.0), "gp", 1)
        blk.gpsimd(L0g)

        act_tab(P, 6)

        _es0 = ExitStack()
        wb0 = _es0.enter_context(nc.sbuf_tensor("wb0", [128, HID], F32))
        wb1 = _es0.enter_context(nc.sbuf_tensor("wb1", [128, HID], F32))
        wpart = _es0.enter_context(nc.sbuf_tensor("wpart", [128, 32], F32))
        wsum = _es0.enter_context(nc.sbuf_tensor("wsum", [128, 1], F32))
        wcmp = _es0.enter_context(nc.sbuf_tensor("wcmp", [128, HID], F32))
        terb = _es0.enter_context(nc.sbuf_tensor("terb", [128, HID], BF16))
        tslab = _es0.enter_context(nc.sbuf_tensor("tslab", [128, 1024], BF16))
        halfs = _es0.enter_context(nc.sbuf_tensor("halfs", [128, 1], F32))
        nhalfs = _es0.enter_context(nc.sbuf_tensor("nhalfs", [128, 1], F32))
        stot = _es0.enter_context(nc.sbuf_tensor("stot", [1, 2], F32))
        wabs = _es0.enter_context(nc.sbuf_tensor("wabs", [128, HID], F32))
        pq0 = _es0.enter_context(nc.psum_tensor("pq0", [128, 512], F32))
        pqt = _es0.enter_context(nc.psum_tensor("pqt", [128, 1024], BF16))
        if True:
            for wi, (WSRC, WDST, NTW, FW, WDc, scol) in enumerate([
                    (W1, W1T, 32, D, HID, s1col), (W2, W2T, 8, HID, D, s2col)][:nq]):
                wap = WSRC.ap().rearrange("(t p) f -> t p f", p=128)
                bufs = [wb0, wb1]

                base_ld = P.cnt["ld"]
                def p1ld(sy, base=base_ld):
                    for t in range(NTW):
                        if t >= 2:
                            P.wait(sy, "sc", 0)
                        P.inc(sy.dma_start(bufs[t % 2][:, 0:FW], wap[t]), "ld", 16)
                sc_base = P.cnt["sc"]
                def p1(sy):
                    for t in range(NTW):
                        if t >= 2:
                            sy.wait_ge(P.sems["sc"], sc_base + (t - 1))
                        P.inc(sy.dma_start(bufs[t % 2][:, 0:FW], wap[t]), "ld", 16)
                blk.sync(p1)

                def p1s(sc, base=base_ld):
                    P.wait(sc, "gp")
                    for t in range(NTW):
                        sc.wait_ge(P.sems["ld"], base + 16 * (t + 1))
                        P.flush(sc, sc.activation(wabs[:, 0:FW], bufs[t % 2][:, 0:FW],
                                                   ACT.Abs,
                                                   accum_out=wpart[:, t:t + 1],
                                                   bias=zcol[:, 0:1],
                                                   scale=onecol[:, 0:1]), "sc")
                blk.scalar(p1s)

                def p1v(v):
                    P.wait(v, "sc")
                    P.flush(v, v.tensor_reduce(wsum[:, 0:1], wpart[:, 0:NTW], AX.X,
                                               ALU.add), "dv")
                blk.vector(p1v)

                def p1pe(te):
                    P.wait(te, "dv")
                    te.matmul(pq0[0:1, 0:1], one128[:, 0:1], wsum[:, 0:1],
                              start=True, stop=True)
                    P.inc(te.last, "pe", 1)
                blkT(p1pe)

                def p1v2(v, wi=wi):
                    P.wait(v, "pe")
                    P.flush(v, v.tensor_scalar(stot[0:1, wi:wi + 1], pq0[0:1, 0:1],
                                                1.0 / (NTW * 128 * FW), 1e-5,
                                                ALU.mult, ALU.max), "dv")
                blk.vector(p1v2)

                def p1pe2(te, wi=wi):
                    P.wait(te, "dv")
                    te.matmul(pq0[:, 1:2], ones1[0:1, :], stot[0:1, wi:wi + 1],
                              start=True, stop=True)
                    P.inc(te.last, "pe", 1)
                blkT(p1pe2)

                def p1v3(v, scol=scol):
                    P.wait(v, "pe")
                    P.flush(v, v.tensor_scalar(halfs[:, 0:1], pq0[:, 1:2], 0.5, None,
                                                ALU.mult), "dv")
                    P.flush(v, v.tensor_scalar(nhalfs[:, 0:1], pq0[:, 1:2], -0.5, None,
                                               ALU.mult), "dv")
                    P.flush(v, v.tensor_scalar(scol[:, 0:1], pq0[:, 1:2], -1.0, None,
                                               ALU.mult), "dv")
                blk.vector(p1v3)
                ev_scales = P.cnt["dv"]

                base2 = P.cnt["ld"]
                nblk = FW // 128
                for t in range(NTW):
                    def p2ld(sy, t=t):
                        if t >= 2:
                            P.wait(sy, "dv")
                        P.inc(sy.dma_start(bufs[t % 2][:, 0:FW], wap[t]), "ld", 16)
                    blk.sync(p2ld)
                    ld_t = P.cnt["ld"]

                    def p2v(v, t=t, ld_t=ld_t):
                        v.wait_ge(P.sems["ld"], ld_t)
                        if t == 0:
                            P.wait(v, "dv", ev_scales)
                        if t >= 1:
                            P.wait(v, "gp")
                        P.flush(v, v.tensor_scalar(wcmp[:, 0:FW], bufs[t % 2][:, 0:FW],
                                                    halfs[:, 0:1], None, ALU.is_gt),
                                "dv")
                        P.inc(v.scalar_tensor_tensor(terb[:, 0:FW], bufs[t % 2][:, 0:FW],
                                                     nhalfs[:, 0:1], wcmp[:, 0:FW],
                                                     ALU.is_lt, ALU.subtract), "dv", 1)
                    blk.vector(p2v)
                    ev_ter = P.cnt["dv"]

                    for w0 in range(0, nblk, 8):
                        def p2pe(te, w0=w0, ev=ev_ter):
                            P.wait(te, "dv", ev)
                            P.wait(te, "sc")
                            for b in range(8):
                                te.matmul(pqt[:, b * 128:(b + 1) * 128],
                                          terb[:, (w0 + b) * 128:(w0 + b + 1) * 128],
                                          cidb[:, :], start=True, stop=True,
                                          is_transpose=True)
                            P.inc(te.last, "pe", 1)
                        blkT(p2pe)
                        ev_tr = P.cnt["pe"]

                        def p2gp(sc, ev=ev_tr):
                            P.wait(sc, "pe", ev)
                            P.wait(sc, "st")
                            P.inc(sc.activation(tslab[:, :], pqt[:, :], ACT.Copy,
                                                bias=0.0, scale=onecol[:, 0:1]), "sc", 1)
                        blk.scalar(p2gp)
                        ev_ev = P.cnt["sc"]

                        def p2st(sy, t=t, w0=w0, ev=ev_ev, WDST=WDST, WDc=WDc):
                            P.wait(sy, "sc", ev)
                            off = (w0 * 128) * WDc + t * 128
                            dst = bass.AP(WDST, off, [[WDc, 128], [128 * WDc, 8], [1, 128]])
                            P.inc(sy.dma_start(
                                dst, tslab[:, :].rearrange("p (b f) -> p b f", b=8)),
                                "st", 16)
                        blk.sync(p2st)
        _es0.close()
        quant_st_done = P.cnt["st"]

        _es = ExitStack()
        Wk_s = _es.enter_context(nc.sbuf_tensor("Wk_s", [128, 8, D], BF16))
        Wq_s = _es.enter_context(nc.sbuf_tensor("Wq_s", [128, 8, D], BF16))
        Wv_s = _es.enter_context(nc.sbuf_tensor("Wv_s", [128, 8, D], BF16))
        Wab_s = _es.enter_context(nc.sbuf_tensor("Wab_s", [128, 8, 2 * H], BF16))
        xln = _es.enter_context(nc.sbuf_tensor("xln", [128, D], BF16))
        xlnT = _es.enter_context(nc.sbuf_tensor("xlnT", [128, D], BF16))
        Ktok = _es.enter_context(nc.sbuf_tensor("Ktok", [128, D], BF16))
        Qtok = _es.enter_context(nc.sbuf_tensor("Qtok", [128, D], BF16))
        Vtok = _es.enter_context(nc.sbuf_tensor("Vtok", [128, D], BF16))
        Ktb = _es.enter_context(nc.sbuf_tensor("Ktb", [128, H * 128], BF16))
        Qtb = _es.enter_context(nc.sbuf_tensor("Qtb", [128, H * 128], BF16))
        Kfb = _es.enter_context(nc.sbuf_tensor("Kfb", [128, H * 128], BF16))
        Qfb = _es.enter_context(nc.sbuf_tensor("Qfb", [128, H * 128], BF16))
        Cf1 = _es.enter_context(nc.sbuf_tensor("Cf1", [128, H * 128], BF16))
        Cf2 = _es.enter_context(nc.sbuf_tensor("Cf2", [128, H * 128], BF16))
        Cf4 = _es.enter_context(nc.sbuf_tensor("Cf4", [128, H * 128], BF16))
        Cf8 = _es.enter_context(nc.sbuf_tensor("Cf8", [128, H * 128], BF16))
        Cf16 = _es.enter_context(nc.sbuf_tensor("Cf16", [128, H * 128], BF16))
        Cf32 = _es.enter_context(nc.sbuf_tensor("Cf32", [128, H * 128], BF16))
        Dsl = _es.enter_context(nc.sbuf_tensor("Dsl", [128, 8 * 128], BF16))
        Mts = _es.enter_context(nc.sbuf_tensor("Mts", [128, H * 128], BF16))
        Usb = _es.enter_context(nc.sbuf_tensor("Usb", [128, H * HD], BF16))
        Usc = _es.enter_context(nc.sbuf_tensor("Usc", [128, H * HD], BF16))
        Ssl = _es.enter_context(nc.sbuf_tensor("Ssl", [128, H * HD], F32))
        Sbf = _es.enter_context(nc.sbuf_tensor("Sbf", [128, H * HD], BF16))
        ytile = _es.enter_context(nc.sbuf_tensor("ytile", [128, D], BF16))
        Escr = _es.enter_context(nc.sbuf_tensor("Escr", [128, 1024], F32))
        Epre = _es.enter_context(nc.sbuf_tensor("Epre", [128, 1024], BF16))
        scrA = _es.enter_context(nc.sbuf_tensor("scrA", [128, 1024], F32))
        stat32 = _es.enter_context(nc.sbuf_tensor("stat32", [128, 32], F32))
        st_hi = _es.enter_context(nc.sbuf_tensor("st_hi", [128, 32], BF16))
        st_lo = _es.enter_context(nc.sbuf_tensor("st_lo", [128, 32], BF16))
        Prow = _es.enter_context(nc.sbuf_tensor("Prow", [16, 128], F32))
        zero16 = _es.enter_context(nc.sbuf_tensor("zero16", [16, 128], F32))
        zaT = _es.enter_context(nc.sbuf_tensor("zaT", [16, 128], F32))
        zbT = _es.enter_context(nc.sbuf_tensor("zbT", [16, 128], F32))
        nlaT = _es.enter_context(nc.sbuf_tensor("nlaT", [16, 128], F32))
        nlbT = _es.enter_context(nc.sbuf_tensor("nlbT", [16, 128], F32))
        lkT = _es.enter_context(nc.sbuf_tensor("lkT", [16, 128], F32))
        lqT = _es.enter_context(nc.sbuf_tensor("lqT", [16, 128], F32))
        r_c1n = _es.enter_context(nc.sbuf_tensor("r_c1n", [16, 128], F32))
        r_s1b = _es.enter_context(nc.sbuf_tensor("r_s1b", [16, 128], F32))
        r_s1m = _es.enter_context(nc.sbuf_tensor("r_s1m", [16, 128], F32))
        r_rhsc = _es.enter_context(nc.sbuf_tensor("r_rhsc", [16, 128], BF16))
        r_g = _es.enter_context(nc.sbuf_tensor("r_g", [16, 128], BF16))
        r_rc = _es.enter_context(nc.sbuf_tensor("r_rc", [16, 128], BF16))
        r_sigb = _es.enter_context(nc.sbuf_tensor("r_sigb", [16, 128], BF16))
        r_gc = _es.enter_context(nc.sbuf_tensor("r_gc", [16, 128], BF16))
        r_gcpre = _es.enter_context(nc.sbuf_tensor("r_gcpre", [16, 128], F32))
        r_rcpre = _es.enter_context(nc.sbuf_tensor("r_rcpre", [16, 128], F32))
        c1n_hi = _es.enter_context(nc.sbuf_tensor("c1n_hi", [16, 128], BF16))
        c1n_lo = _es.enter_context(nc.sbuf_tensor("c1n_lo", [16, 128], BF16))
        s1b_hi = _es.enter_context(nc.sbuf_tensor("s1b_hi", [16, 128], BF16))
        s1b_lo = _es.enter_context(nc.sbuf_tensor("s1b_lo", [16, 128], BF16))
        s1m_hi = _es.enter_context(nc.sbuf_tensor("s1m_hi", [16, 128], BF16))
        s1m_lo = _es.enter_context(nc.sbuf_tensor("s1m_lo", [16, 128], BF16))
        tsplit = _es.enter_context(nc.sbuf_tensor("tsplit", [16, 128], F32))
        backTb = _es.enter_context(nc.sbuf_tensor("backTb", [128, 80], BF16))
        backTc = _es.enter_context(nc.sbuf_tensor("backTc", [128, 16], F32))
        ysq = _es.enter_context(nc.sbuf_tensor("ysq", [128, 1], F32))
        mcol = _es.enter_context(nc.sbuf_tensor("mcol", [128, 1], F32))
        vcol = _es.enter_context(nc.sbuf_tensor("vcol", [128, 1], F32))
        negm = _es.enter_context(nc.sbuf_tensor("negm", [128, 1], F32))
        gate = _es.enter_context(nc.sbuf_tensor("gate", [128, D], BF16))
        x2t = _es.enter_context(nc.sbuf_tensor("x2t", [128, D], F32))
        dstage = _es.enter_context(nc.sbuf_tensor("dstage", [128, 2048], F32)) if dbg else None
        pA = _es.enter_context(nc.psum_tensor("pA", [128, 1024], F32))
        pB = _es.enter_context(nc.psum_tensor("pB", [128, 1024], F32))
        pC = _es.enter_context(nc.psum_tensor("pC", [128, 1024], F32))
        pD = _es.enter_context(nc.psum_tensor("pD", [128, 512], F32))
        pT = _es.enter_context(nc.psum_tensor("pT", [128, 1024], BF16))
        if True:
            Cfs = {1: Cf1, 2: Cf2, 4: Cf4, 8: Cf8, 16: Cf16, 32: Cf32}

            def LW(g):
                P.wait(g, "st")
                P.wait(g, "dv")
                P.wait(g, "sc")
                P.wait(g, "pe")
                for dst, src, cols in [(Wk_s, WK, D), (Wq_s, WQ, D), (Wv_s, WV, D),
                                       (Wab_s, WAB, 2 * H)]:
                    sap = bass.AP(src, 0, [[cols, 128], [128 * cols, 8], [1, cols]])
                    P.inc(g.dma_start(dst[:, :, :], sap), "lg", 16)
                g.memset(Ktb[:, :], 0.0)
                g.memset(Qtb[:, :], 0.0)
                g.memset(Ssl[:, :], 0.0)
                g.memset(Sbf[:, :], 0.0)
                P.inc(g.memset(zero16[:, :], 0.0), "gp", 1)
            blk.gpsimd(LW)

            def dbgdump(slot, src_ap, cols, j):
                if not dbg or j != 0:
                    return

                def dv(v):
                    P.wait(v, "pe")
                    P.wait(v, "sc")
                    P.wait(v, "gp")
                    P.inc(v.tensor_copy(dstage[:, 0:cols], src_ap), "dv", 1)
                blk.vector(dv)

                def dsy(sy):
                    P.wait(sy, "dv")
                    P.inc(sy.dma_start(DBG.ap()[slot, :, 0:cols], dstage[:, 0:cols]),
                          "st", 16)
                blk.sync(dsy)

            R_RHSC, R_G, R_RC, R_SIGB, R_GC = 0, 16, 32, 48, 64

            for j in range(na1):
                xb = xbuf[j % 2]
                if j + 1 < NT:
                    def pref(sy, j=j):
                        P.wait(sy, "dv")
                        dma_x(P, sy, xbuf[(j + 1) % 2], X.ap(), j + 1)
                    blk.sync(pref)

                ld_need = P.cnt["ld"]

                def a1(v, j=j, xb=xb):
                    v.wait_ge(P.sems["ld"], ld_need)
                    if j == 0:
                        P.wait(v, "gp")
                    P.flush(v, v.tensor_reduce(mcol[:, 0:1], xb[:, :], AX.X, ALU.add),
                            "dv")
                    P.flush(v, v.tensor_scalar(mcol[:, 0:1], mcol[:, 0:1], 1.0 / D,
                                               None, ALU.mult), "dv")
                    P.flush(v, v.tensor_scalar(negm[:, 0:1], mcol[:, 0:1], -1.0, None,
                                               ALU.mult), "dv")
                blk.vector(a1)

                def a2s(sc, xb=xb):
                    P.wait(sc, "dv")
                    P.flush(sc, sc.activation(scrA[:, :], xb[:, :], ACT.Square,
                                               bias=negm[:, 0:1], scale=onecol[:, 0:1],
                                               accum_out=vcol[:, 0:1]), "sc")
                blk.scalar(a2s)

                def a2v(v):
                    P.wait(v, "sc")
                    P.flush(v, v.tensor_scalar(vcol[:, 0:1], vcol[:, 0:1], 1.0 / D, 1e-5,
                                               ALU.mult, ALU.add), "dv")
                blk.vector(a2v)

                def a2s2(sc):
                    P.wait(sc, "dv")
                    P.flush(sc, sc.activation(vcol[:, 0:1], vcol[:, 0:1], ACT.Ln,
                                              bias=zcol[:, 0:1], scale=onecol[:, 0:1]),
                            "sc")
                blk.scalar(a2s2)

                def a2v2(v):
                    P.wait(v, "sc")
                    P.flush(v, v.tensor_scalar(vcol[:, 0:1], vcol[:, 0:1], -0.5, None,
                                               ALU.mult), "dv")
                blk.vector(a2v2)

                def a2s3(sc):
                    P.wait(sc, "dv")
                    P.flush(sc, sc.activation(vcol[:, 0:1], vcol[:, 0:1], ACT.Exp,
                                              bias=zcol[:, 0:1], scale=onecol[:, 0:1]),
                            "sc")
                blk.scalar(a2s3)

                def a3v(v, j=j, xb=xb):
                    P.wait(v, "sc")
                    P.inc(v.tensor_scalar(xln[:, :], xb[:, :], mcol[:, 0:1], vcol[:, 0:1],
                                          ALU.subtract, ALU.mult), "dv", 1)
                    v.tensor_copy(LNm[:, j:j + 1], mcol[:, 0:1])
                    P.inc(v.tensor_copy(LNr[:, j:j + 1], vcol[:, 0:1]), "dv", 1)
                blk.vector(a3v)

                dbgdump(0, xln[:, :], D, j)
                def a4pe(te):
                    P.wait(te, "dv")
                    P.wait(te, "sc")
                    for p in range(8):
                        te.matmul(pT[:, p * 128:(p + 1) * 128], xln[:, p * 128:(p + 1) * 128],
                                  cidb[:, :], start=True, stop=True, is_transpose=True)
                    P.inc(te.last, "pe", 1)
                blkT(a4pe)

                def a4gp(sc):
                    P.wait(sc, "pe")
                    P.inc(sc.activation(xlnT[:, :], pT[:, :], ACT.Copy,
                                        bias=0.0, scale=onecol[:, 0:1]), "sc", 1)
                blk.scalar(a4gp)

                if A1STAGE < 2:
                    continue
                def a5pe(te):
                    P.wait(te, "sc")
                    P.wait(te, "gp")
                    P.wait(te, "lg")
                    P.wait(te, "dv")
                    P.wait(te, "sc")
                    for p in range(8):
                        for ps, Wsb in [(pA, Wk_s), (pB, Wq_s), (pC, Wv_s)]:
                            te.matmul(ps[:, 0:512], xlnT[:, p * 128:(p + 1) * 128],
                                      Wsb[:, p, 0:512], start=(p == 0), stop=(p == 7))
                            te.matmul(ps[:, 512:1024], xlnT[:, p * 128:(p + 1) * 128],
                                      Wsb[:, p, 512:1024], start=(p == 0), stop=(p == 7))
                        te.matmul(pD[0:16, 0:128], Wab_s[:, p, 0:16],
                                  xlnT[:, p * 128:(p + 1) * 128],
                                  start=(p == 0), stop=False)
                        te.matmul(pD[0:16, 128:256], Wab_s[:, p, 16:32],
                                  xlnT[:, p * 128:(p + 1) * 128],
                                  start=False, stop=(p == 7))
                    P.inc(te.last, "pe", 1)
                blkT(a5pe)
                ev_proj = P.cnt["pe"]

                def a6v(v):
                    P.wait(v, "pe", ev_proj)
                    P.inc(v.tensor_copy(Ktok[:, :], pA[:, :]), "dv", 1)
                blk.vector(a6v)

                def a6g(v):
                    P.wait(v, "pe", ev_proj)
                    P.inc(v.tensor_copy(Qtok[:, :], pB[:, :]), "dv", 1)
                    P.inc(v.tensor_copy(Vtok[:, :], pC[:, :]), "dv", 1)
                blk.vector(a6g)

                def a6s(sc):
                    P.wait(sc, "pe", ev_proj)
                    P.wait(sc, "dv")
                    P.inc(sc.activation(scrA[:, :], pA[:, :], ACT.Square, bias=zcol[:, 0:1], scale=onecol[:, 0:1]), "sc", 1)
                blk.scalar(a6s)

                def a7v(v):
                    P.wait(v, "sc")
                    P.flush(v, v.tensor_reduce(
                        stat32[:, 0:16].rearrange("p (a b) -> p a b", a=16, b=1),
                        scrA[:, :].rearrange("p (a b) -> p a b", a=16, b=64),
                        AX.X, ALU.add), "dv")
                blk.vector(a7v)

                def a7s(sc):
                    P.wait(sc, "dv")
                    P.inc(sc.activation(scrA[:, :], pB[:, :], ACT.Square, bias=zcol[:, 0:1], scale=onecol[:, 0:1]), "sc", 1)
                blk.scalar(a7s)

                def a8v(v):
                    P.wait(v, "sc")
                    P.flush(v, v.tensor_reduce(
                        stat32[:, 16:32].rearrange("p (a b) -> p a b", a=16, b=1),
                        scrA[:, :].rearrange("p (a b) -> p a b", a=16, b=64),
                        AX.X, ALU.add), "dv")
                    P.inc(v.tensor_copy(Sbf[:, :], Ssl[:, :]), "dv", 1)
                blk.vector(a8v)

                dbgdump(1, Ktok[:, :], D, j)
                dbgdump(2, Vtok[:, :], D, j)
                if A1STAGE < 3:
                    continue
                def a10s(sc):
                    P.wait(sc, "pe")
                    P.wait(sc, "dv")
                    P.inc(sc.activation(zaT[:, :], pD[0:16, 0:128], ACT.Exp,
                                        bias=nbab_a[:, 0:1], scale=nonecol[0:16, 0:1]),
                          "sc", 1)
                    P.inc(sc.activation(zbT[:, :], pD[0:16, 128:256], ACT.Exp,
                                        bias=nbab_b[:, 0:1], scale=nonecol[0:16, 0:1]),
                          "sc", 1)
                blk.scalar(a10s)

                def a9v(v):
                    P.wait(v, "sc")
                    P.flush(v, v.tensor_copy(st_hi[:, :], stat32[:, :]), "dv")
                    P.flush(v, v.tensor_sub(st_lo[:, :], stat32[:, :], st_hi[:, :]),
                            "dv")
                    P.inc(v.tensor_scalar(zaT[:, :], zaT[:, :], 1.0, None, ALU.add),
                          "dv", 1)
                    P.inc(v.tensor_scalar(zbT[:, :], zbT[:, :], 1.0, None, ALU.add),
                          "dv", 1)
                blk.vector(a9v)

                def a9pe(te):
                    P.wait(te, "dv")
                    P.wait(te, "sc")
                    te.matmul(pT[0:16, 0:128], st_hi[:, 0:16], cidb[:, :],
                              start=True, stop=True, is_transpose=True)
                    te.matmul(pT[0:16, 128:256], st_lo[:, 0:16], cidb[:, :],
                              start=True, stop=True, is_transpose=True)
                    te.matmul(pT[0:16, 256:384], st_hi[:, 16:32], cidb[:, :],
                              start=True, stop=True, is_transpose=True)
                    te.matmul(pT[0:16, 384:512], st_lo[:, 16:32], cidb[:, :],
                              start=True, stop=True, is_transpose=True)
                    P.inc(te.last, "pe", 1)
                blkT(a9pe)

                def a9v2(v):
                    P.wait(v, "pe")
                    P.flush(v, v.tensor_copy(lkT[:, :], pT[0:16, 0:128]), "dv")
                    P.flush(v, v.tensor_add(lkT[:, :], lkT[:, :], pT[0:16, 128:256]),
                            "dv")
                    P.flush(v, v.tensor_copy(lqT[:, :], pT[0:16, 256:384]), "dv")
                    P.flush(v, v.tensor_add(lqT[:, :], lqT[:, :], pT[0:16, 384:512]),
                            "dv")
                    P.flush(v, v.tensor_scalar(lkT[:, :], lkT[:, :], 1e-24, None,
                                               ALU.add), "dv")
                    P.flush(v, v.tensor_scalar(lqT[:, :], lqT[:, :], 1e-24, None,
                                               ALU.add), "dv")
                blk.vector(a9v2)

                if dbg and j == 0:
                    def dpre(v):
                        P.wait(v, "pe")
                        v.tensor_copy(dstage[0:128, 0:32], stat32[:, :])
                        v.tensor_copy(dstage[0:128, 32:64], st_hi[:, :])
                        v.tensor_copy(dstage[0:128, 64:96], st_lo[:, :])
                        v.tensor_copy(dstage[0:16, 96:224], lkT[:, :])
                        v.tensor_copy(dstage[0:16, 224:352], lqT[:, :])
                        v.tensor_copy(dstage[0:16, 352:480], pT[0:16, 0:128])
                        P.inc(v.tensor_copy(dstage[0:16, 480:608], pT[0:16, 128:256]),
                              "dv", 1)
                    blk.vector(dpre)

                    def dpre2(sy):
                        P.wait(sy, "dv")
                        P.inc(sy.dma_start(DBG.ap()[3], dstage[:, :]), "st", 16)
                    blk.sync(dpre2)

                def a11s(sc):
                    P.wait(sc, "dv")
                    z16, o16 = zcol[0:16, 0:1], onecol[0:16, 0:1]
                    P.flush(sc, sc.activation(nlaT[:, :], zaT[:, :], ACT.Ln, bias=z16,
                                               scale=o16), "sc")
                    P.flush(sc, sc.activation(nlbT[:, :], zbT[:, :], ACT.Ln, bias=z16,
                                              scale=o16), "sc")
                    P.flush(sc, sc.activation(lkT[:, :], lkT[:, :], ACT.Ln, bias=z16,
                                              scale=o16), "sc")
                    P.flush(sc, sc.activation(lqT[:, :], lqT[:, :], ACT.Ln, bias=z16,
                                              scale=o16), "sc")
                blk.scalar(a11s)

                if A1STAGE < 4:
                    continue

                def a12v(v):
                    P.wait(v, "sc")
                    P.flush(v, v.tensor_tensor_scan(Prow[:, 0:64], nlaT[:, 0:64],
                                               zero16[:, 0:64], 0.0, ALU.add, ALU.add), "dv")
                    P.flush(v, v.tensor_tensor_scan(Prow[:, 64:128], nlaT[:, 64:128],
                                               zero16[:, 64:128], 0.0, ALU.add, ALU.add),
                          "dv")
                    P.flush(v, v.scalar_tensor_tensor(r_c1n[:, :], lkT[:, :], -0.5,
                                                 Prow[:, :], ALU.mult, ALU.add), "dv")
                    P.flush(v, v.tensor_copy(c1n_hi[:, :], r_c1n[:, :]), "dv")
                    P.flush(v, v.tensor_sub(c1n_lo[:, :], r_c1n[:, :], c1n_hi[:, :]), "dv")
                    P.flush(v, v.tensor_sub(r_s1b[:, :], nlaT[:, :], nlbT[:, :]), "dv")
                    P.flush(v, v.tensor_sub(r_s1b[:, :], r_s1b[:, :], Prow[:, :]), "dv")
                    P.flush(v, v.scalar_tensor_tensor(r_s1b[:, :], lkT[:, :], -0.5,
                                                 r_s1b[:, :], ALU.mult, ALU.add), "dv")
                    P.flush(v, v.tensor_copy(s1b_hi[:, :], r_s1b[:, :]), "dv")
                    P.flush(v, v.tensor_sub(s1b_lo[:, :], r_s1b[:, :], s1b_hi[:, :]), "dv")
                    P.flush(v, v.scalar_tensor_tensor(r_s1m[:, :], lqT[:, :], -0.5,
                                                 Prow[:, :], ALU.mult, ALU.subtract),
                          "dv")
                    P.flush(v, v.tensor_copy(s1m_hi[:, :], r_s1m[:, :]), "dv")
                    P.flush(v, v.tensor_sub(s1m_lo[:, :], r_s1m[:, :], s1m_hi[:, :]), "dv")
                    P.flush(v, v.tensor_scalar(r_rcpre[:, 0:64], r_c1n[:, 0:64],
                                          Prow[:, 63:64], None, ALU.subtract), "dv")
                    P.flush(v, v.tensor_scalar(r_rcpre[:, 64:128], r_c1n[:, 64:128],
                                          Prow[:, 127:128], None, ALU.subtract), "dv")
                    P.flush(v, v.tensor_scalar(r_gcpre[:, 0:64], zero16[:, 0:64],
                                          Prow[:, 63:64], None, ALU.subtract), "dv")
                    P.flush(v, v.tensor_scalar(r_gcpre[:, 64:128], zero16[:, 64:128],
                                          Prow[:, 127:128], None, ALU.subtract), "dv")
                blk.vector(a12v)

                def a13s(sc):
                    P.wait(sc, "dv")
                    z16, o16, n16 = zcol[0:16, 0:1], onecol[0:16, 0:1], nonecol[0:16, 0:1]
                    P.flush(sc, sc.activation(r_rhsc[:, :], r_s1b[:, :], ACT.Exp,
                                        bias=z16, scale=o16), "sc")
                    P.flush(sc, sc.activation(r_g[:, :], r_s1m[:, :], ACT.Exp,
                                        bias=z16, scale=o16), "sc")
                    P.flush(sc, sc.activation(r_rc[:, :], r_rcpre[:, :], ACT.Exp,
                                        bias=z16, scale=o16), "sc")
                    P.flush(sc, sc.activation(r_sigb[:, :], nlbT[:, :], ACT.Exp,
                                        bias=z16, scale=n16), "sc")
                    P.flush(sc, sc.activation(r_gc[:, :], r_gcpre[:, :], ACT.Exp,
                                        bias=z16, scale=o16), "sc")
                blk.scalar(a13s)

                def a14pe(te):
                    P.wait(te, "sc")
                    for i, cls in enumerate([r_rhsc, r_g, r_rc, r_sigb, r_gc,
                                             c1n_hi, c1n_lo]):
                        te.matmul(pT[:, 512 + i * 16:512 + (i + 1) * 16], cls[:, :],
                                  cidb[0:16, 0:16], start=True, stop=True,
                                  is_transpose=True)
                    P.inc(te.last, "pe", 1)
                blkT(a14pe)

                def a14v(v):
                    P.wait(v, "pe")
                    P.flush(v, v.tensor_copy(backTb[:, :], pT[:, 512:592]), "dv")
                    P.flush(v, v.tensor_copy(backTc[:, :], pT[:, 592:608]), "dv")
                    P.flush(v, v.tensor_add(backTc[:, :], backTc[:, :], pT[:, 608:624]),
                            "dv")
                blk.vector(a14v)

                if A1STAGE < 5:
                    continue
                def a15v(v):
                    P.wait(v, "pe")
                    ps_k, ps_kt = pstep(Ktok), pstep(Ktb)
                    P.inc(v.tensor_copy(
                        bass.AP(Ktb, 0, [[ps_kt, 64], [128, H], [1, 64]]),
                        bass.AP(Ktok, 0, [[ps_k, 64], [64, H], [1, 64]])), "dv", 1)
                    ps_q, ps_qt = pstep(Qtok), pstep(Qtb)
                    P.inc(v.tensor_copy(
                        bass.AP(Qtb, 0, [[ps_qt, 64], [128, H], [1, 64]]),
                        bass.AP(Qtok, 0, [[ps_q, 64], [64, H], [1, 64]])), "dv", 1)
                blk.vector(a15v)

                def a15g(g):
                    P.wait(g, "dv")
                    ps_k, ps_kt = pstep(Ktok), pstep(Ktb)
                    P.inc(g.tensor_copy(
                        bass.AP(Ktb, 64 * ps_kt + 64, [[ps_kt, 64], [128, H], [1, 64]]),
                        bass.AP(Ktok, 64 * ps_k, [[ps_k, 64], [64, H], [1, 64]])), "gp", 1)
                    ps_q, ps_qt = pstep(Qtok), pstep(Qtb)
                    P.inc(g.tensor_copy(
                        bass.AP(Qtb, 64 * ps_qt + 64, [[ps_qt, 64], [128, H], [1, 64]]),
                        bass.AP(Qtok, 64 * ps_q, [[ps_q, 64], [64, H], [1, 64]])), "gp", 1)
                blk.gpsimd(a15g)

                for (tb, fb) in [(Ktb, Kfb), (Qtb, Qfb)]:
                    for w in range(2):
                        def a16pe(te, tb=tb, w=w):
                            P.wait(te, "gp")
                            P.wait(te, "sc")
                            P.wait(te, "dv")
                            for b in range(8):
                                hh = w * 8 + b
                                te.matmul(pT[:, b * 128:(b + 1) * 128],
                                          tb[:, hh * 128:(hh + 1) * 128],
                                          cidb[:, :], start=True, stop=True,
                                          is_transpose=True)
                            P.inc(te.last, "pe", 1)
                        blkT(a16pe)

                        def a16g(sc, fb=fb, w=w):
                            P.wait(sc, "pe")
                            P.inc(sc.activation(fb[:, w * 1024:(w + 1) * 1024], pT[:, :],
                                                ACT.Copy, bias=0.0,
                                                scale=onecol[:, 0:1]), "sc", 1)
                        blk.scalar(a16g)

                if A1STAGE < 6:
                    continue
                for half in range(2):
                    h0 = half * 8

                    def g1(te, h0=h0):
                        P.wait(te, "gp")
                        P.wait(te, "dv")
                        P.wait(te, "sc")
                        for hh in range(8):
                            ksl = Kfb[:, (h0 + hh) * 128:(h0 + hh + 1) * 128]
                            te.matmul(pA[:, hh * 128:(hh + 1) * 128], ksl, ksl,
                                      start=True, stop=True)
                            te.matmul(pC[:, hh * 128:(hh + 1) * 128], ksl,
                                      Qfb[:, (h0 + hh) * 128:(h0 + hh + 1) * 128],
                                      start=True, stop=True)
                            te.matmul(pD[:, hh * 64:(hh + 1) * 64], ksl,
                                      Sbf[:, (h0 + hh) * 64:(h0 + hh + 1) * 64],
                                      start=True, stop=True)
                        for bank in range(2):
                            for q in range(4):
                                hh = bank * 4 + q
                                oh = crep[:, (h0 + hh) * 128:(h0 + hh + 1) * 128]
                                te.matmul(pB[:, hh * 128:(hh + 1) * 128], oh,
                                          s1b_hi[:, :], start=(q == 0), stop=False)
                                te.matmul(pB[:, hh * 128:(hh + 1) * 128], oh,
                                          s1b_lo[:, :], start=False, stop=False)
                            te.matmul(pB[:, bank * 512:(bank + 1) * 512], cidb[:, :],
                                      maskB4[:, :], start=False, stop=True)
                        P.inc(te.last, "pe", 1)
                    blkT(g1)

                    def e1(v, h0=h0):
                        P.wait(v, "pe")
                        cb = bass.AP(backTc, h0,
                                     [[pstep(backTc), 128], [1, 8], [0, 128]])
                        P.inc(v.tensor_tensor(
                            Escr[:, :].rearrange("p (a b) -> p a b", a=8, b=128),
                            pB[:, :].rearrange("p (a b) -> p a b", a=8, b=128),
                            cb, ALU.add), "dv", 1)
                    blk.vector(e1)

                    def e1s(sc):
                        P.wait(sc, "dv")
                        P.inc(sc.activation(Epre[:, :], Escr[:, :], ACT.Exp, bias=zcol[:, 0:1], scale=onecol[:, 0:1]), "sc", 1)
                    blk.scalar(e1s)

                    def e1m(v, h0=h0):
                        P.wait(v, "sc")
                        P.inc(v.tensor_tensor(Cf1[:, h0 * 128:(h0 + 8) * 128],
                                              Epre[:, :], pA[:, :], ALU.mult), "dv", 1)
                        rb = bass.AP(backTb, R_RHSC + h0,
                                     [[pstep(backTb), 128], [1, 8], [0, 64]])
                        P.inc(v.tensor_tensor(
                            scrA[:, 0:512].rearrange("p (a b) -> p a b", a=8, b=64),
                            pD[:, 0:512].rearrange("p (a b) -> p a b", a=8, b=64),
                            rb, ALU.mult), "dv", 1)
                        sb = bass.AP(backTb, R_SIGB + h0,
                                     [[pstep(backTb), 128], [1, 8], [0, 64]])
                        P.inc(v.tensor_tensor(
                            scrA[:, 512:1024].rearrange("p (a b) -> p a b", a=8, b=64),
                            Vtok[:, h0 * 64:(h0 + 8) * 64].rearrange(
                                "p (a b) -> p a b", a=8, b=64),
                            sb, ALU.mult), "dv", 1)
                        P.inc(v.tensor_sub(Usb[:, h0 * 64:(h0 + 8) * 64],
                                           scrA[:, 512:1024], scrA[:, 0:512]), "dv", 1)
                    blk.vector(e1m)

                    if A1STAGE < 7:
                        continue
                    if half == 0:
                        dbgdump(4, Cf1[:, 0:1024], 1024, j)
                        dbgdump(5, Usb[:, 0:512], 512, j)
                    def ap1(te, k, h0=h0):
                        P.wait(te, "dv")
                        for b in range(8):
                            csl = Cfs[k][:, (h0 + b) * 128:(h0 + b + 1) * 128]
                            te.matmul(pD[:, b * 64:(b + 1) * 64], csl,
                                      Usb[:, (h0 + b) * 64:(h0 + b + 1) * 64],
                                      start=True, stop=True)
                        P.inc(te.last, "pe", 1)

                    def ap2(v, k, h0=h0):
                        P.wait(v, "pe")
                        op = ALU.subtract if k == 1 else ALU.add
                        P.inc(v.tensor_tensor(Usb[:, h0 * 64:(h0 + 8) * 64],
                                              Usb[:, h0 * 64:(h0 + 8) * 64],
                                              pD[:, 0:512], op), "dv", 1)

                    for k in [1, 2, 4, 8, 16]:
                        def sq1(te, k=k, h0=h0):
                            P.wait(te, "dv")
                            P.wait(te, "sc")
                            for b in range(8):
                                te.matmul(pT[:, b * 128:(b + 1) * 128],
                                          Cfs[k][:, (h0 + b) * 128:(h0 + b + 1) * 128],
                                          cidb[:, :], start=True, stop=True,
                                          is_transpose=True)
                            P.inc(te.last, "pe", 1)
                        blkT(sq1)

                        def sq2(sc):
                            P.wait(sc, "pe")
                            P.inc(sc.activation(Dsl[:, :], pT[:, :], ACT.Copy,
                                                bias=0.0, scale=onecol[:, 0:1]), "sc", 1)
                        blk.scalar(sq2)

                        def sq3(te, k=k, h0=h0):
                            P.wait(te, "sc")
                            P.wait(te, "dv")
                            for b in range(8):
                                te.matmul(pA[:, b * 128:(b + 1) * 128],
                                          Dsl[:, b * 128:(b + 1) * 128],
                                          Cfs[k][:, (h0 + b) * 128:(h0 + b + 1) * 128],
                                          start=True, stop=True)
                            P.inc(te.last, "pe", 1)
                        blkT(sq3)

                        def sq4(v, k=k, h0=h0):
                            P.wait(v, "pe")
                            P.inc(v.tensor_copy(Cfs[2 * k][:, h0 * 128:(h0 + 8) * 128],
                                                pA[:, :]), "dv", 1)
                        blk.vector(sq4)

                        blkT(lambda te, k=k: ap1(te, 2 * k))
                        blk.vector(lambda v, k=k: ap2(v, 2 * k))

                    blkT(lambda te: ap1(te, 1))
                    blk.vector(lambda v: ap2(v, 1))

                    if A1STAGE < 8:
                        continue
                    if half == 0:
                        dbgdump(6, Usb[:, 0:512], 512, j)
                    def m1(te, h0=h0):
                        P.wait(te, "dv")
                        for bank in range(2):
                            for q in range(4):
                                hh = bank * 4 + q
                                oh = crep[:, (h0 + hh) * 128:(h0 + hh + 1) * 128]
                                te.matmul(pB[:, hh * 128:(hh + 1) * 128], oh,
                                          s1m_hi[:, :], start=(q == 0), stop=False)
                                te.matmul(pB[:, hh * 128:(hh + 1) * 128], oh,
                                          s1m_lo[:, :], start=False, stop=False)
                            te.matmul(pB[:, bank * 512:(bank + 1) * 512], cidb[:, :],
                                      maskM4[:, :], start=False, stop=True)
                        P.inc(te.last, "pe", 1)
                    blkT(m1)

                    def m2(v, h0=h0):
                        P.wait(v, "pe")
                        cb = bass.AP(backTc, h0,
                                     [[pstep(backTc), 128], [1, 8], [0, 128]])
                        P.inc(v.tensor_tensor(
                            Escr[:, :].rearrange("p (a b) -> p a b", a=8, b=128),
                            pB[:, :].rearrange("p (a b) -> p a b", a=8, b=128),
                            cb, ALU.add), "dv", 1)
                    blk.vector(m2)

                    def m3(sc):
                        P.wait(sc, "dv")
                        P.inc(sc.activation(Epre[:, :], Escr[:, :], ACT.Exp, bias=zcol[:, 0:1], scale=onecol[:, 0:1]), "sc", 1)
                    blk.scalar(m3)

                    def m4(v, h0=h0):
                        P.wait(v, "sc")
                        P.inc(v.tensor_tensor(Mts[:, h0 * 128:(h0 + 8) * 128],
                                              Epre[:, :], pC[:, :], ALU.mult), "dv", 1)
                        rc = bass.AP(backTb, R_RC + h0,
                                     [[pstep(backTb), 128], [1, 8], [0, 64]])
                        P.inc(v.tensor_tensor(
                            Usc[:, h0 * 64:(h0 + 8) * 64].rearrange(
                                "p (a b) -> p a b", a=8, b=64),
                            Usb[:, h0 * 64:(h0 + 8) * 64].rearrange(
                                "p (a b) -> p a b", a=8, b=64),
                            rc, ALU.mult), "dv", 1)
                    blk.vector(m4)

                    def o1(te, h0=h0):
                        P.wait(te, "dv")
                        for b in range(8):
                            msl = Mts[:, (h0 + b) * 128:(h0 + b + 1) * 128]
                            te.matmul(pD[:, b * 64:(b + 1) * 64], msl,
                                      Usb[:, (h0 + b) * 64:(h0 + b + 1) * 64],
                                      start=True, stop=True)
                        for b in range(8):
                            qsl = Qfb[:, (h0 + b) * 128:(h0 + b + 1) * 128]
                            te.matmul(pB[:, b * 64:(b + 1) * 64], qsl,
                                      Sbf[:, (h0 + b) * 64:(h0 + b + 1) * 64],
                                      start=True, stop=True)
                        P.inc(te.last, "pe", 1)
                    blkT(o1)

                    def o2(v, h0=h0):
                        P.wait(v, "pe")
                        gb = bass.AP(backTb, R_G + h0,
                                     [[pstep(backTb), 128], [1, 8], [0, 64]])
                        P.inc(v.tensor_tensor(
                            scrA[:, 0:512].rearrange("p (a b) -> p a b", a=8, b=64),
                            pB[:, 0:512].rearrange("p (a b) -> p a b", a=8, b=64),
                            gb, ALU.mult), "dv", 1)
                        P.inc(v.tensor_tensor(ytile[:, h0 * 64:(h0 + 8) * 64],
                                              scrA[:, 0:512], pD[:, 0:512], ALU.add),
                              "dv", 1)
                    blk.vector(o2)

                    def su1(te, h0=h0):
                        P.wait(te, "dv")
                        for b in range(8):
                            ksl = Ktb[:, (h0 + b) * 128:(h0 + b + 1) * 128]
                            te.matmul(pD[:, b * 64:(b + 1) * 64], ksl,
                                      Usc[:, (h0 + b) * 64:(h0 + b + 1) * 64],
                                      start=True, stop=True)
                        P.inc(te.last, "pe", 1)
                    blkT(su1)

                    def su2(v, h0=h0):
                        P.wait(v, "pe")
                        gc = bass.AP(backTb, R_GC + h0,
                                     [[pstep(backTb), 128], [1, 8], [0, 64]])
                        P.inc(v.tensor_tensor(
                            scrA[:, 0:512].rearrange("p (a b) -> p a b", a=8, b=64),
                            Ssl[:, h0 * 64:(h0 + 8) * 64].rearrange(
                                "p (a b) -> p a b", a=8, b=64),
                            gc, ALU.mult), "dv", 1)
                        P.inc(v.tensor_tensor(Ssl[:, h0 * 64:(h0 + 8) * 64],
                                              scrA[:, 0:512], pD[:, 0:512], ALU.add),
                              "dv", 1)
                    blk.vector(su2)

                dbgdump(7, ytile[:, :], D, j)
                def ys1(v, j=j):
                    P.flush(v, v.tensor_mul(scrA[:, :], ytile[:, :], ytile[:, :]),
                            "dv")
                    P.inc(v.tensor_reduce(YSQ[:, j:j + 1], scrA[:, :], AX.X, ALU.add),
                          "dv", 1)
                blk.vector(ys1)

                def ys3(sy, j=j):
                    P.wait(sy, "dv")
                    P.inc(sy.dma_start(YD.ap()[j], ytile[:, :]), "st", 16)
                blk.sync(ys3)

            def LW2(g):
                P.wait(g, "pe")
                for dst, src in [(Wk_s, WG), (Wq_s, WO)]:
                    sap = bass.AP(src, 0, [[D, 128], [128 * D, 8], [1, D]])
                    P.inc(g.dma_start(dst[:, :, :], sap), "lg", 16)
            blk.gpsimd(LW2)
            ld_w2 = P.cnt["lg"]

            def foldgn(v):
                v.wait_ge(P.sems["lg"], ld_w2)
                for p in range(8):
                    P.flush(v, v.tensor_scalar(Wq_s[:, p, :], Wq_s[:, p, :],
                                               gnsb[:, p:p + 1], None, ALU.mult), "dv")
            blk.vector(foldgn)

            def ysb1(v):
                P.wait(v, "dv")
                P.flush(v, v.tensor_scalar(RSTY[:, 0:NT], YSQ[:, 0:NT], 1.0 / D, 1e-5,
                                           ALU.mult, ALU.add), "dv")
            blk.vector(ysb1)

            def ysb2(sc):
                P.wait(sc, "dv")
                P.inc(sc.activation(RSTY[:, 0:NT], RSTY[:, 0:NT], ACT.Ln,
                                    bias=zcol[:, 0:1], scale=onecol[:, 0:1]), "sc", 1)
            blk.scalar(ysb2)

            def ysb3(v):
                P.wait(v, "sc")
                P.flush(v, v.tensor_scalar(RSTY[:, 0:NT], RSTY[:, 0:NT], -0.5, None,
                                           ALU.mult), "dv")
            blk.vector(ysb3)

            def ysb4(sc):
                P.wait(sc, "dv")
                P.inc(sc.activation(RSTY[:, 0:NT], RSTY[:, 0:NT], ACT.Exp,
                                    bias=zcol[:, 0:1], scale=onecol[:, 0:1]), "sc", 1)
            blk.scalar(ysb4)

            act_tab(P, 18)

            for j in range(na2):
                xb = xbuf[j % 2]

                def b0(sy, j=j, xb=xb):
                    P.wait(sy, "dv")
                    P.wait(sy, "sc")
                    dma_x(P, sy, xb, X.ap(), j)
                    P.inc(sy.dma_start(ytile[:, :], YD.ap()[j]), "ld", 16)
                blk.sync(b0)
                ldj = P.cnt["ld"]

                def b1(v, j=j, xb=xb):
                    v.wait_ge(P.sems["ld"], ldj)
                    P.inc(v.tensor_scalar(xln[:, :], xb[:, :], LNm[:, j:j + 1],
                                          LNr[:, j:j + 1], ALU.subtract, ALU.mult),
                          "dv", 1)
                blk.vector(b1)

                def b2(te):
                    P.wait(te, "dv")
                    P.wait(te, "sc")
                    for p in range(8):
                        te.matmul(pT[:, p * 128:(p + 1) * 128],
                                  xln[:, p * 128:(p + 1) * 128],
                                  cidb[:, :], start=True, stop=True, is_transpose=True)
                    P.inc(te.last, "pe", 1)
                blkT(b2)

                def b3(sc):
                    P.wait(sc, "pe")
                    P.inc(sc.activation(xlnT[:, :], pT[:, :], ACT.Copy,
                                        bias=0.0, scale=onecol[:, 0:1]), "sc", 1)
                blk.scalar(b3)

                def b4(te):
                    P.wait(te, "lg")
                    P.wait(te, "sc")
                    for p in range(8):
                        te.matmul(pA[:, 0:512], xlnT[:, p * 128:(p + 1) * 128],
                                  Wk_s[:, p, 0:512], start=(p == 0), stop=(p == 7))
                        te.matmul(pA[:, 512:1024], xlnT[:, p * 128:(p + 1) * 128],
                                  Wk_s[:, p, 512:1024], start=(p == 0), stop=(p == 7))
                    P.inc(te.last, "pe", 1)
                blkT(b4)

                def b5(sc):
                    P.wait(sc, "pe")
                    P.inc(sc.activation(gate[:, :], pA[:, :], ACT.Silu, bias=zcol[:, 0:1], scale=onecol[:, 0:1]), "sc", 1)
                blk.scalar(b5)

                def b6(v):
                    P.wait(v, "sc")
                    P.inc(v.tensor_mul(xln[:, :], ytile[:, :], gate[:, :]), "dv", 1)
                blk.vector(b6)

                def b7(te):
                    P.wait(te, "dv")
                    P.wait(te, "sc")
                    for p in range(8):
                        te.matmul(pT[:, p * 128:(p + 1) * 128],
                                  xln[:, p * 128:(p + 1) * 128],
                                  cidb[:, :], start=True, stop=True, is_transpose=True)
                    P.inc(te.last, "pe", 1)
                blkT(b7)

                def b8(sc):
                    P.wait(sc, "pe")
                    P.inc(sc.activation(xlnT[:, :], pT[:, :], ACT.Copy,
                                        bias=0.0, scale=onecol[:, 0:1]), "sc", 1)
                blk.scalar(b8)

                def b9(te):
                    P.wait(te, "sc")
                    P.wait(te, "dv")
                    for p in range(8):
                        te.matmul(pC[:, 0:512], xlnT[:, p * 128:(p + 1) * 128],
                                  Wq_s[:, p, 0:512], start=(p == 0), stop=(p == 7))
                        te.matmul(pC[:, 512:1024], xlnT[:, p * 128:(p + 1) * 128],
                                  Wq_s[:, p, 512:1024], start=(p == 0), stop=(p == 7))
                    P.inc(te.last, "pe", 1)
                blkT(b9)

                def b10(v, j=j, xb=xb):
                    P.wait(v, "pe")
                    if dbg and j == 0:
                        P.inc(v.tensor_copy(dstage[:, 0:1024], pC[:, :]), "dv", 1)
                        P.inc(v.tensor_copy(dstage[:, 1024:1040], RSTY[:, 0:16]),
                              "dv", 1)
                    P.inc(v.scalar_tensor_tensor(x2t[:, :], pC[:, :], RSTY[:, j:j + 1],
                                                 xb[:, :], ALU.mult, ALU.add), "dv", 1)
                blk.vector(b10)
                if dbg and j == 0:
                    def db10(sy):
                        P.wait(sy, "dv")
                        P.inc(sy.dma_start(DBG.ap()[5], dstage[:, :]), "st", 16)
                    blk.sync(db10)

                def b11(sy, j=j):
                    P.wait(sy, "dv")
                    P.inc(sy.dma_start(X2D.ap()[j], x2t[:, :]), "st", 16)
                blk.sync(b11)

        _es.close()
        _esb = ExitStack()
        w1t_s = _esb.enter_context(nc.sbuf_tensor("w1t_s", [128, 8, HID], BF16))
        w2t_s = _esb.enter_context(nc.sbuf_tensor("w2t_s", [128, 32, D], BF16))
        x2b = _esb.enter_context(nc.sbuf_tensor("x2b", [128, D], F32))
        x2b2 = _esb.enter_context(nc.sbuf_tensor("x2b2", [128, D], F32))
        xqb = _esb.enter_context(nc.sbuf_tensor("xqb", [128, D], BF16))
        xqTb = _esb.enter_context(nc.sbuf_tensor("xqTb", [128, D], BF16))
        hbuf = _esb.enter_context(nc.sbuf_tensor("hbuf", [128, HID], F32))
        hqb = _esb.enter_context(nc.sbuf_tensor("hqb", [128, HID], BF16))
        hqT = _esb.enter_context(nc.sbuf_tensor("hqT", [128, HID], BF16))
        obuf = _esb.enter_context(nc.sbuf_tensor("obuf", [128, D], F32))
        bm = _esb.enter_context(nc.sbuf_tensor("bm", [128, 1], F32))
        bnm = _esb.enter_context(nc.sbuf_tensor("bnm", [128, 1], F32))
        bv = _esb.enter_context(nc.sbuf_tensor("bv", [128, 1], F32))
        bq = _esb.enter_context(nc.sbuf_tensor("bq", [128, 1], F32))
        chc = _esb.enter_context(nc.sbuf_tensor("chc", [128, 1], F32))
        c2c = _esb.enter_context(nc.sbuf_tensor("c2c", [128, 1], F32))
        bscr = _esb.enter_context(nc.sbuf_tensor("bscr", [128, 1024], F32))
        pH = _esb.enter_context(nc.psum_tensor("pH", [128, 1024], F32))
        pO = _esb.enter_context(nc.psum_tensor("pO", [128, 1024], F32))
        pTb = _esb.enter_context(nc.psum_tensor("pTb", [128, 1024], BF16))
        if True:
            def bl0(sy):
                P.wait(sy, "st")
                P.wait(sy, "dv")
                P.wait(sy, "sc")
                P.wait(sy, "pe")
                P.wait(sy, "gp")
                sap1 = bass.AP(W1T, 0, [[HID, 128], [128 * HID, 8], [1, HID]])
                P.inc(sy.dma_start(w1t_s[:, :, :], sap1), "ld", 16)
                sap2 = bass.AP(W2T, 0, [[D, 128], [128 * D, 32], [1, D]])
                P.inc(sy.dma_start(w2t_s[:, :, :], sap2), "ld", 16)
            blk.sync(bl0)

            act_tab(P, 6)

            for j in range(nbpre):
                def c0(sy, j=j):
                    P.wait(sy, "sc")
                    P.wait(sy, "dv")
                    if j == 0:
                        P.wait(sy, "st")
                    P.inc(sy.dma_start((x2b if j % 2 == 0 else x2b2)[:, :], X2D.ap()[j]),
                          "ld", 16)
                blk.sync(c0)
                ldj = P.cnt["ld"]

                def c1(v, j=j):
                    v.wait_ge(P.sems["ld"], ldj)
                    buf = x2b if j % 2 == 0 else x2b2
                    P.flush(v, v.tensor_reduce(bm[:, 0:1], buf[:, :], AX.X, ALU.add), "dv")
                    P.flush(v, v.tensor_scalar(bm[:, 0:1], bm[:, 0:1], 1.0 / D, None,
                                          ALU.mult), "dv")
                    P.flush(v, v.tensor_scalar(bnm[:, 0:1], bm[:, 0:1], -1.0, None, ALU.mult),
                          "dv")
                    v.tensor_copy(MEAN2[:, j:j + 1], bm[:, 0:1])
                    P.flush(v, v.tensor_scalar(bscr[:, :], buf[:, :], bm[:, 0:1], None,
                                          ALU.subtract), "dv")
                    P.flush(v, v.tensor_reduce(bq[:, 0:1], bscr[:, :], AX.X, ALU.max,
                                          apply_absolute_value=True), "dv")
                blk.vector(c1)

                def c2(sc, j=j):
                    P.wait(sc, "dv")
                    buf = x2b if j % 2 == 0 else x2b2
                    P.flush(sc, sc.activation(bscr[:, :], buf[:, :], ACT.Square,
                                               bias=bnm[:, 0:1], scale=onecol[:, 0:1],
                                               accum_out=bv[:, 0:1]), "sc")
                blk.scalar(c2)

                def c2v(v):
                    P.wait(v, "sc")
                    P.flush(v, v.tensor_scalar(bv[:, 0:1], bv[:, 0:1], 1.0 / D,
                                               1e-5, ALU.mult, ALU.add), "dv")
                blk.vector(c2v)

                def c2s(sc):
                    P.wait(sc, "dv")
                    P.flush(sc, sc.activation(bv[:, 0:1], bv[:, 0:1], ACT.Ln,
                                               bias=zcol[:, 0:1], scale=onecol[:, 0:1]),
                            "sc")
                blk.scalar(c2s)

                def c2v2(v):
                    P.wait(v, "sc")
                    P.flush(v, v.tensor_scalar(bv[:, 0:1], bv[:, 0:1], -0.5, None,
                                               ALU.mult), "dv")
                blk.vector(c2v2)

                def c2s2(sc):
                    P.wait(sc, "dv")
                    P.flush(sc, sc.activation(bv[:, 0:1], bv[:, 0:1], ACT.Exp,
                                               bias=zcol[:, 0:1], scale=onecol[:, 0:1]),
                            "sc")
                blk.scalar(c2s2)

                def c3(v, j=j):
                    P.wait(v, "sc")
                    P.flush(v, v.tensor_scalar(SCQ[:, j:j + 1], bq[:, 0:1], bv[:, 0:1], 1e-5,
                                          ALU.mult, ALU.max), "dv")
                    P.flush(v, v.reciprocal(bq[:, 0:1], SCQ[:, j:j + 1]), "dv")
                    P.flush(v, v.tensor_scalar(bq[:, 0:1], bq[:, 0:1], 127.0, None, ALU.mult),
                          "dv")
                    P.flush(v, v.tensor_mul(CC2[:, j:j + 1], bv[:, 0:1], bq[:, 0:1]), "dv")
                blk.vector(c3)

            act_tab(P, 10)
            ld_w = P.cnt["ld"]
            for j in range(nbmain):
                def d0(sy, j=j):
                    P.wait(sy, "dv")
                    P.inc(sy.dma_start((x2b if j % 2 == 0 else x2b2)[:, :], X2D.ap()[j]),
                          "ld", 16)
                blk.sync(d0)
                ldj = P.cnt["ld"]

                def d1(v, j=j):
                    v.wait_ge(P.sems["ld"], max(ldj, ld_w))
                    buf = x2b if j % 2 == 0 else x2b2
                    P.flush(v, v.tensor_scalar(bscr[:, :], buf[:, :], MEAN2[:, j:j + 1],
                                          CC2[:, j:j + 1], ALU.subtract, ALU.mult),
                          "dv")
                    P.flush(v, v.tensor_scalar(xqb[:, :], bscr[:, :], MAGIC, -MAGIC,
                                          ALU.add, ALU.add), "dv")
                    P.flush(v, v.tensor_scalar(chc[:, 0:1], SCQ[:, j:j + 1], s1col[:, 0:1],
                                          1.0 / 127.0, ALU.mult, ALU.mult), "dv")
                blk.vector(d1)

                def d2(te):
                    P.wait(te, "dv")
                    P.wait(te, "sc")
                    for p in range(8):
                        te.matmul(pTb[:, p * 128:(p + 1) * 128],
                                  xqb[:, p * 128:(p + 1) * 128],
                                  cidb[:, :], start=True, stop=True, is_transpose=True)
                    P.inc(te.last, "pe", 1)
                blkT(d2)

                def d3(sc):
                    P.wait(sc, "pe")
                    P.inc(sc.activation(xqTb[:, :], pTb[:, :], ACT.Copy,
                                        bias=0.0, scale=onecol[:, 0:1]), "sc", 1)
                blk.scalar(d3)
                ev_xqT = P.cnt["sc"]

                for q in range(4):
                    def d4(te, q=q):
                        P.wait(te, "sc", ev_xqT)
                        P.wait(te, "sc")
                        for p in range(8):
                            te.matmul(pH[:, 0:512], xqTb[:, p * 128:(p + 1) * 128],
                                      w1t_s[:, p, q * 1024:q * 1024 + 512],
                                      start=(p == 0), stop=(p == 7))
                            te.matmul(pH[:, 512:1024], xqTb[:, p * 128:(p + 1) * 128],
                                      w1t_s[:, p, q * 1024 + 512:(q + 1) * 1024],
                                      start=(p == 0), stop=(p == 7))
                        P.inc(te.last, "pe", 1)
                    blkT(d4)

                    def d5(sc, q=q):
                        P.wait(sc, "pe")
                        P.inc(sc.activation(hbuf[:, q * 1024:(q + 1) * 1024], pH[:, :],
                                            ACT.Gelu, bias=zcol[:, 0:1], scale=chc[:, 0:1]),
                              "sc", 1)
                    blk.scalar(d5)

                def d6(v, j=j):
                    P.wait(v, "sc")
                    P.flush(v, v.tensor_reduce(bq[:, 0:1], hbuf[:, :], AX.X, ALU.max,
                                          apply_absolute_value=True), "dv")
                    P.flush(v, v.tensor_scalar(bq[:, 0:1], bq[:, 0:1], 1e-5, None, ALU.max),
                          "dv")
                    P.flush(v, v.tensor_scalar(c2c[:, 0:1], bq[:, 0:1], s2col[:, 0:1],
                                          1.0 / 127.0, ALU.mult, ALU.mult), "dv")
                    P.flush(v, v.reciprocal(bq[:, 0:1], bq[:, 0:1]), "dv")
                    P.flush(v, v.tensor_scalar(bq[:, 0:1], bq[:, 0:1], 127.0, None, ALU.mult),
                          "dv")
                blk.vector(d6)

                qev = []

                def d6b(v):
                    for q in range(4):
                        v.tensor_scalar(bscr[:, :], hbuf[:, q * 1024:(q + 1) * 1024],
                                        bq[:, 0:1], None, ALU.mult)
                        P.inc(v.tensor_scalar(hqb[:, q * 1024:(q + 1) * 1024], bscr[:, :],
                                              MAGIC, -MAGIC, ALU.add, ALU.add), "dv", 1)
                        qev.append(P.cnt["dv"])
                blk.vector(d6b)

                ev_d8 = []
                for w in range(4):
                    def d7(te, w=w):
                        te.wait_ge(P.sems["dv"], ev_d8[w - 1] if w > 0 else qev[0])
                        for b in range(8):
                            te.matmul(pTb[:, b * 128:(b + 1) * 128],
                                      hqb[:, (w * 8 + b) * 128:(w * 8 + b + 1) * 128],
                                      cidb[:, :], start=True, stop=True,
                                      is_transpose=True)
                        P.inc(te.last, "pe", 1)
                    blkT(d7)

                    def d8(v, w=w):
                        P.wait(v, "pe")
                        P.inc(v.tensor_copy(hqT[:, w * 1024:(w + 1) * 1024], pTb[:, :]),
                              "dv", 1)
                        ev_d8.append(P.cnt["dv"])
                    blk.vector(d8)

                def d9(te):
                    P.wait(te, "dv")
                    for p in range(32):
                        te.matmul(pO[:, 0:512], hqT[:, p * 128:(p + 1) * 128],
                                  w2t_s[:, p, 0:512], start=(p == 0), stop=(p == 31))
                        te.matmul(pO[:, 512:1024], hqT[:, p * 128:(p + 1) * 128],
                                  w2t_s[:, p, 512:1024], start=(p == 0), stop=(p == 31))
                    P.inc(te.last, "pe", 1)
                blkT(d9)

                def d10(v, j=j):
                    P.wait(v, "pe")
                    buf = x2b if j % 2 == 0 else x2b2
                    P.inc(v.scalar_tensor_tensor(obuf[:, :], pO[:, :], c2c[:, 0:1],
                                                 buf[:, :], ALU.mult, ALU.add), "dv", 1)
                blk.vector(d10)

                def d11(sy, j=j):
                    P.wait(sy, "dv")
                    dma_x(P, sy, obuf, OUT.ap(), j, store=True, sem="st")
                blk.sync(d11)

        _esb.close()
        def fin(sy):
            P.wait(sy, "st")
        blk.sync(fin)

    nc.finalize()
    return nc


_NC_CACHE = None


def host_inputs(inputs, core):
    ii = np.arange(128)
    tt = np.arange(128)
    maskb = np.where(tt[None, :] > ii[:, None], 0.0, -1e30).astype(np.float32)
    maskm = np.where(tt[None, :] >= ii[:, None], 0.0, -1e30).astype(np.float32)
    d = {
        "x": np.ascontiguousarray(inputs["L0_attn_x"][2 * core:2 * core + 2]),
        "wq": inputs["L0_attn_Wq"], "wk": inputs["L0_attn_Wk"],
        "wv": inputs["L0_attn_Wv"],
        "wab": np.ascontiguousarray(
            np.concatenate([inputs["L0_attn_Wa"], inputs["L0_attn_Wb"]], axis=1)),
        "bab": np.concatenate([inputs["L0_attn_ba"],
                               inputs["L0_attn_bb"]]).reshape(1, 32),
        "wg": inputs["L0_attn_Wg"], "wo": inputs["L0_attn_Wo"],
        "gn": np.ascontiguousarray(inputs["L0_attn_g_norm"].reshape(8, 128).T),
        "w1": inputs["L0_ffn_w1"], "w2": inputs["L0_ffn_w2"],
        "cidb": np.eye(128, dtype=BFNP),
        "cidf": np.eye(128, dtype=np.float32),
        "cmaskb": maskb.astype(BFNP), "cmaskm": maskm.astype(BFNP),
        "cones": np.ones((1, 128), dtype=np.float32),
        "cone128": np.ones((128, 1), dtype=np.float32),
        "crep": np.repeat(np.eye(16, dtype=BFNP), 128, axis=1) if False else np.kron(np.eye(16), np.ones((1, 128))).astype(BFNP),
        "cmaskb4": np.tile(maskb, (1, 4)).astype(BFNP),
        "cmaskm4": np.tile(maskm, (1, 4)).astype(BFNP),
        "babt": np.concatenate([inputs["L0_attn_ba"], inputs["L0_attn_bb"]]).reshape(32, 1),
    }
    return {k: np.ascontiguousarray(np.asarray(v)) for k, v in d.items()}


def kernel(**inputs):
    global _NC_CACHE
    if _NC_CACHE is None:
        _NC_CACHE = build_program()
    nc = _NC_CACHE
    in_maps = [host_inputs(inputs, c) for c in range(NCORES)]
    res = run_bass_kernel_spmd(nc, in_maps, core_ids=list(range(NCORES)))
    out = np.empty((16, SEQ, D), dtype=np.float32)
    for c in range(NCORES):
        out[2 * c:2 * c + 2] = np.asarray(res.results[c]["out"])
    return out


# version 14
# speedup vs baseline: 1.8072x; 1.0658x over previous
import numpy as np
from contextlib import ExitStack
import ml_dtypes
import concourse.bass as bass
import concourse.mybir as mybir
from concourse.bass_utils import run_bass_kernel_spmd

F32 = mybir.dt.float32
BF16 = mybir.dt.bfloat16
ALU = mybir.AluOpType
ACT = mybir.ActivationFunctionType
AX = mybir.AxisListType
BFNP = ml_dtypes.bfloat16

NCORES = 8
SEQ = 2048
D = 1024
H = 16
HD = 64
C = 64
NT = SEQ // C
HID = 4096
MAGIC = 12582912.0
A1STAGE = 99


class TEW:

    def __init__(self, te):
        object.__setattr__(self, "_te", te)
        object.__setattr__(self, "last", None)

    def matmul(self, *a, **k):
        inst = self._te.matmul(*a, **k)
        object.__setattr__(self, "last", inst)
        return inst

    def __getattr__(self, n):
        return getattr(self._te, n)


class Prog:
    def __init__(self, nc, blk):
        self.nc = nc
        self.blk = blk
        self.sems = {}
        self.cnt = {}
        for s in ["ld", "lg", "st", "pe", "dv", "sc", "gp"]:
            self.sems[s] = nc.alloc_semaphore("sem_" + s)
            self.cnt[s] = 0

    def inc(self, inst, name, n=1):
        inst.then_inc(self.sems[name], n)
        self.cnt[name] += n
        return self.cnt[name]

    def flush(self, eng, inst, sem):
        n = self.inc(inst, sem, 1)
        eng.wait_ge(self.sems[sem], n)

    def wait(self, eng, name, val=None):
        v = self.cnt[name] if val is None else val
        if v > 0:
            eng.wait_ge(self.sems[name], v)


def pstep(t):
    return t.ap().ap[0][0]


def dma_x(P, eng, sbuf, dram_ap3, j, store=False, sem="ld"):
    for s in range(2):
        sl = sbuf[64 * s:64 * s + 64, :]
        dr = dram_ap3[s, j * C:(j + 1) * C, :]
        if store:
            P.inc(eng.dma_start(dr, sl), sem, 16)
        else:
            P.inc(eng.dma_start(sl, dr), sem, 16)


def act_tab(P, idx):
    def f(sc):
        sc.add_instruction(mybir.InstLoadActFuncSet(
            name=P.nc.get_next_instruction_name(),
            act_func_set_id=idx, ins=[], outs=[]))
    P.blk.scalar(f)


def build_program(nq=2, na1=NT, na2=NT, nbpre=NT, nbmain=NT, dbg=False):
    nc = bass.Bass("TRN2", target_bir_lowering=False, detect_race_conditions=False)

    D_IN = {}
    def din(name, shape, dt=F32):
        D_IN[name] = nc.dram_tensor(name, shape, dt, kind="ExternalInput")
        return D_IN[name]

    X = din("x", [2, SEQ, D])
    WQ = din("wq", [D, D]); WK = din("wk", [D, D]); WV = din("wv", [D, D])
    WAB = din("wab", [D, 2 * H]); BAB = din("bab", [1, 2 * H])
    BABT = din("babt", [32, 1])
    WG = din("wg", [D, D]); WO = din("wo", [D, D]); GN = din("gn", [128, 8])
    W1 = din("w1", [HID, D]); W2 = din("w2", [D, HID])
    CIDB = din("cidb", [128, 128], BF16); CIDF = din("cidf", [128, 128])
    CMB = din("cmaskb", [128, 128], BF16); CMM = din("cmaskm", [128, 128], BF16)
    CON1 = din("cones", [1, 128]); CON128 = din("cone128", [128, 1])
    CREP = din("crep", [16, H * 128], BF16)
    CMB4 = din("cmaskb4", [128, 512], BF16)
    CMM4 = din("cmaskm4", [128, 512], BF16)

    OUT = nc.dram_tensor("out", [2, SEQ, D], F32, kind="ExternalOutput")
    dkind = "ExternalOutput" if dbg else "Internal"
    DBG = nc.dram_tensor("dbgx", [8, 128, 2048], F32, kind=dkind) if dbg else None
    YD = nc.dram_tensor("ydram", [NT, 128, D], BF16, kind=dkind)
    X2D = nc.dram_tensor("x2dram", [NT, 128, D], F32, kind=dkind)
    W1T = nc.dram_tensor("w1tT", [D, HID], BF16)
    W2T = nc.dram_tensor("w2tT", [HID, D], BF16)

    with nc.Block() as blk:
        def blkT(f):
            blk.tensor(lambda te: f(TEW(te)))

        P = Prog(nc, blk)

        _esp = ExitStack()
        cidb = _esp.enter_context(nc.sbuf_tensor("cidb_s", [128, 128], BF16))
        cidf = _esp.enter_context(nc.sbuf_tensor("cidf_s", [128, 128], F32))
        maskB = _esp.enter_context(nc.sbuf_tensor("maskB_s", [128, 128], BF16))
        maskM = _esp.enter_context(nc.sbuf_tensor("maskM_s", [128, 128], BF16))
        ones1 = _esp.enter_context(nc.sbuf_tensor("ones1_s", [1, 128], F32))
        one128 = _esp.enter_context(nc.sbuf_tensor("one128_s", [128, 1], F32))
        gnsb = _esp.enter_context(nc.sbuf_tensor("gn_s", [128, 8], F32))
        babsb = _esp.enter_context(nc.sbuf_tensor("bab_s", [1, 2 * H], F32))
        crep = _esp.enter_context(nc.sbuf_tensor("crep_s", [16, H * 128], BF16))
        maskB4 = _esp.enter_context(nc.sbuf_tensor("maskB4_s", [128, 512], BF16))
        maskM4 = _esp.enter_context(nc.sbuf_tensor("maskM4_s", [128, 512], BF16))
        nbab_a = _esp.enter_context(nc.sbuf_tensor("nbab_a", [16, 1], F32))
        nbab_b = _esp.enter_context(nc.sbuf_tensor("nbab_b", [16, 1], F32))
        xbuf = [_esp.enter_context(nc.sbuf_tensor(f"xbuf{i}", [128, D], F32)) for i in range(2)]
        LNm = _esp.enter_context(nc.sbuf_tensor("LNm", [128, NT], F32))
        LNr = _esp.enter_context(nc.sbuf_tensor("LNr", [128, NT], F32))
        RSTY = _esp.enter_context(nc.sbuf_tensor("RSTY", [128, NT], F32))
        YSQ = _esp.enter_context(nc.sbuf_tensor("YSQ", [128, NT], F32))
        MEAN2 = _esp.enter_context(nc.sbuf_tensor("MEAN2", [128, NT], F32))
        CC2 = _esp.enter_context(nc.sbuf_tensor("CC2", [128, NT], F32))
        SCQ = _esp.enter_context(nc.sbuf_tensor("SCQ", [128, NT], F32))
        s1col = _esp.enter_context(nc.sbuf_tensor("s1col", [128, 1], F32))
        s2col = _esp.enter_context(nc.sbuf_tensor("s2col", [128, 1], F32))
        zcol = _esp.enter_context(nc.sbuf_tensor("zcol", [128, 1], F32))
        onecol = _esp.enter_context(nc.sbuf_tensor("onecol", [128, 1], F32))
        nonecol = _esp.enter_context(nc.sbuf_tensor("nonecol", [128, 1], F32))

        def L0(sy):
            for dst, src in [(cidb, CIDB), (cidf, CIDF), (maskB, CMB), (maskM, CMM),
                             (ones1, CON1), (one128, CON128), (gnsb, GN), (babsb, BAB),
                         (crep, CREP), (maskB4, CMB4), (maskM4, CMM4)]:
                P.inc(sy.dma_start(dst[:, :], src.ap()), "ld", 16)
            P.inc(sy.dma_start(nbab_a[:, 0:1], BABT.ap()[0:16, :]), "ld", 16)
            P.inc(sy.dma_start(nbab_b[:, 0:1], BABT.ap()[16:32, :]), "ld", 16)
            dma_x(P, sy, xbuf[0], X.ap(), 0)
        blk.sync(L0)

        def L0g(g):
            P.wait(g, "ld")
            g.tensor_scalar(nbab_a[:, 0:1], nbab_a[:, 0:1], -1.0, None, ALU.mult)
            g.tensor_scalar(nbab_b[:, 0:1], nbab_b[:, 0:1], -1.0, None, ALU.mult)
            g.memset(zcol[:, 0:1], 0.0)
            g.memset(onecol[:, 0:1], 1.0)
            P.inc(g.memset(nonecol[:, 0:1], -1.0), "gp", 1)
        blk.gpsimd(L0g)

        act_tab(P, 6)

        _es0 = ExitStack()
        wb0 = _es0.enter_context(nc.sbuf_tensor("wb0", [128, HID], F32))
        wb1 = _es0.enter_context(nc.sbuf_tensor("wb1", [128, HID], F32))
        wpart = _es0.enter_context(nc.sbuf_tensor("wpart", [128, 32], F32))
        wsum = _es0.enter_context(nc.sbuf_tensor("wsum", [128, 1], F32))
        wcmp = _es0.enter_context(nc.sbuf_tensor("wcmp", [128, HID], F32))
        terb = _es0.enter_context(nc.sbuf_tensor("terb", [128, HID], BF16))
        tslab = _es0.enter_context(nc.sbuf_tensor("tslab", [128, 1024], BF16))
        halfs = _es0.enter_context(nc.sbuf_tensor("halfs", [128, 1], F32))
        nhalfs = _es0.enter_context(nc.sbuf_tensor("nhalfs", [128, 1], F32))
        stot = _es0.enter_context(nc.sbuf_tensor("stot", [1, 2], F32))
        wabs = _es0.enter_context(nc.sbuf_tensor("wabs", [128, HID], F32))
        pq0 = _es0.enter_context(nc.psum_tensor("pq0", [128, 512], F32))
        pqt = _es0.enter_context(nc.psum_tensor("pqt", [128, 1024], BF16))
        if True:
            for wi, (WSRC, WDST, NTW, FW, WDc, scol) in enumerate([
                    (W1, W1T, 32, D, HID, s1col), (W2, W2T, 8, HID, D, s2col)][:nq]):
                wap = WSRC.ap().rearrange("(t p) f -> t p f", p=128)
                bufs = [wb0, wb1]

                base_ld = P.cnt["ld"]
                def p1ld(sy, base=base_ld):
                    for t in range(NTW):
                        if t >= 2:
                            P.wait(sy, "sc", 0)
                        P.inc(sy.dma_start(bufs[t % 2][:, 0:FW], wap[t]), "ld", 16)
                sc_base = P.cnt["sc"]
                def p1(sy):
                    for t in range(NTW):
                        if t >= 2:
                            sy.wait_ge(P.sems["sc"], sc_base + (t - 1))
                        P.inc(sy.dma_start(bufs[t % 2][:, 0:FW], wap[t]), "ld", 16)
                blk.sync(p1)

                def p1s(sc, base=base_ld):
                    P.wait(sc, "gp")
                    for t in range(NTW):
                        sc.wait_ge(P.sems["ld"], base + 16 * (t + 1))
                        P.flush(sc, sc.activation(wabs[:, 0:FW], bufs[t % 2][:, 0:FW],
                                                   ACT.Abs,
                                                   accum_out=wpart[:, t:t + 1],
                                                   bias=zcol[:, 0:1],
                                                   scale=onecol[:, 0:1]), "sc")
                blk.scalar(p1s)

                def p1v(v):
                    P.wait(v, "sc")
                    P.flush(v, v.tensor_reduce(wsum[:, 0:1], wpart[:, 0:NTW], AX.X,
                                               ALU.add), "dv")
                blk.vector(p1v)

                def p1pe(te):
                    P.wait(te, "dv")
                    te.matmul(pq0[0:1, 0:1], one128[:, 0:1], wsum[:, 0:1],
                              start=True, stop=True)
                    P.inc(te.last, "pe", 1)
                blkT(p1pe)

                def p1v2(v, wi=wi):
                    P.wait(v, "pe")
                    P.flush(v, v.tensor_scalar(stot[0:1, wi:wi + 1], pq0[0:1, 0:1],
                                                1.0 / (NTW * 128 * FW), 1e-5,
                                                ALU.mult, ALU.max), "dv")
                blk.vector(p1v2)

                def p1pe2(te, wi=wi):
                    P.wait(te, "dv")
                    te.matmul(pq0[:, 1:2], ones1[0:1, :], stot[0:1, wi:wi + 1],
                              start=True, stop=True)
                    P.inc(te.last, "pe", 1)
                blkT(p1pe2)

                def p1v3(v, scol=scol):
                    P.wait(v, "pe")
                    P.flush(v, v.tensor_scalar(halfs[:, 0:1], pq0[:, 1:2], 0.5, None,
                                                ALU.mult), "dv")
                    P.flush(v, v.tensor_scalar(nhalfs[:, 0:1], pq0[:, 1:2], -0.5, None,
                                               ALU.mult), "dv")
                    P.flush(v, v.tensor_scalar(scol[:, 0:1], pq0[:, 1:2], -1.0, None,
                                               ALU.mult), "dv")
                blk.vector(p1v3)
                ev_scales = P.cnt["dv"]

                base2 = P.cnt["ld"]
                nblk = FW // 128
                ev_pe_tile = {}
                for t in range(NTW):
                    def p2ld(sy, t=t):
                        if t >= 2:
                            P.wait(sy, "dv")
                        P.inc(sy.dma_start(bufs[t % 2][:, 0:FW], wap[t]), "ld", 16)
                    blk.sync(p2ld)
                    ld_t = P.cnt["ld"]

                    def p2v(v, t=t, ld_t=ld_t):
                        v.wait_ge(P.sems["ld"], ld_t)
                        if t == 0:
                            P.wait(v, "dv", ev_scales)
                        if t >= 1:
                            P.wait(v, "pe", ev_pe_tile[t - 1])
                        P.flush(v, v.tensor_scalar(wcmp[:, 0:FW], bufs[t % 2][:, 0:FW],
                                                    halfs[:, 0:1], None, ALU.is_gt),
                                "dv")
                        P.inc(v.scalar_tensor_tensor(terb[:, 0:FW], bufs[t % 2][:, 0:FW],
                                                     nhalfs[:, 0:1], wcmp[:, 0:FW],
                                                     ALU.is_lt, ALU.subtract), "dv", 1)
                    blk.vector(p2v)
                    ev_ter = P.cnt["dv"]

                    for w0 in range(0, nblk, 8):
                        def p2pe(te, w0=w0, ev=ev_ter):
                            P.wait(te, "dv", ev)
                            P.wait(te, "sc")
                            for b in range(8):
                                te.matmul(pqt[:, b * 128:(b + 1) * 128],
                                          terb[:, (w0 + b) * 128:(w0 + b + 1) * 128],
                                          cidb[:, :], start=True, stop=True,
                                          is_transpose=True)
                            P.inc(te.last, "pe", 1)
                        blkT(p2pe)
                        ev_tr = P.cnt["pe"]

                        def p2gp(sc, ev=ev_tr):
                            P.wait(sc, "pe", ev)
                            P.wait(sc, "st")
                            P.inc(sc.activation(tslab[:, :], pqt[:, :], ACT.Copy,
                                                bias=0.0, scale=onecol[:, 0:1]), "sc", 1)
                        blk.scalar(p2gp)
                        ev_ev = P.cnt["sc"]

                        def p2st(sy, t=t, w0=w0, ev=ev_ev, WDST=WDST, WDc=WDc):
                            P.wait(sy, "sc", ev)
                            off = (w0 * 128) * WDc + t * 128
                            dst = bass.AP(WDST, off, [[WDc, 128], [128 * WDc, 8], [1, 128]])
                            P.inc(sy.dma_start(
                                dst, tslab[:, :].rearrange("p (b f) -> p b f", b=8)),
                                "st", 16)
                        blk.sync(p2st)
                    ev_pe_tile[t] = P.cnt["pe"]
        _es0.close()
        quant_st_done = P.cnt["st"]

        _es = ExitStack()
        Wk_s = _es.enter_context(nc.sbuf_tensor("Wk_s", [128, 8, D], BF16))
        Wq_s = _es.enter_context(nc.sbuf_tensor("Wq_s", [128, 8, D], BF16))
        Wv_s = _es.enter_context(nc.sbuf_tensor("Wv_s", [128, 8, D], BF16))
        Wab_s = _es.enter_context(nc.sbuf_tensor("Wab_s", [128, 8, 2 * H], BF16))
        xln = _es.enter_context(nc.sbuf_tensor("xln", [128, D], BF16))
        xlnT = _es.enter_context(nc.sbuf_tensor("xlnT", [128, D], BF16))
        Ktok = _es.enter_context(nc.sbuf_tensor("Ktok", [128, D], BF16))
        Qtok = _es.enter_context(nc.sbuf_tensor("Qtok", [128, D], BF16))
        Vtok = _es.enter_context(nc.sbuf_tensor("Vtok", [128, D], BF16))
        Ktb = _es.enter_context(nc.sbuf_tensor("Ktb", [128, H * 128], BF16))
        Qtb = _es.enter_context(nc.sbuf_tensor("Qtb", [128, H * 128], BF16))
        Kfb = _es.enter_context(nc.sbuf_tensor("Kfb", [128, H * 128], BF16))
        Qfb = _es.enter_context(nc.sbuf_tensor("Qfb", [128, H * 128], BF16))
        Cf1 = _es.enter_context(nc.sbuf_tensor("Cf1", [128, H * 128], BF16))
        Cf2 = _es.enter_context(nc.sbuf_tensor("Cf2", [128, H * 128], BF16))
        Cf4 = _es.enter_context(nc.sbuf_tensor("Cf4", [128, H * 128], BF16))
        Cf8 = _es.enter_context(nc.sbuf_tensor("Cf8", [128, H * 128], BF16))
        Cf16 = _es.enter_context(nc.sbuf_tensor("Cf16", [128, H * 128], BF16))
        Cf32 = _es.enter_context(nc.sbuf_tensor("Cf32", [128, H * 128], BF16))
        Dsl = _es.enter_context(nc.sbuf_tensor("Dsl", [128, 8 * 128], BF16))
        Mts = _es.enter_context(nc.sbuf_tensor("Mts", [128, H * 128], BF16))
        Usb = _es.enter_context(nc.sbuf_tensor("Usb", [128, H * HD], BF16))
        Usc = _es.enter_context(nc.sbuf_tensor("Usc", [128, H * HD], BF16))
        Ssl = _es.enter_context(nc.sbuf_tensor("Ssl", [128, H * HD], F32))
        Sbf = _es.enter_context(nc.sbuf_tensor("Sbf", [128, H * HD], BF16))
        ytile = _es.enter_context(nc.sbuf_tensor("ytile", [128, D], BF16))
        Escr = _es.enter_context(nc.sbuf_tensor("Escr", [128, 1024], F32))
        Epre = _es.enter_context(nc.sbuf_tensor("Epre", [128, 1024], BF16))
        scrA = _es.enter_context(nc.sbuf_tensor("scrA", [128, 1024], F32))
        stat32 = _es.enter_context(nc.sbuf_tensor("stat32", [128, 32], F32))
        st_hi = _es.enter_context(nc.sbuf_tensor("st_hi", [128, 32], BF16))
        st_lo = _es.enter_context(nc.sbuf_tensor("st_lo", [128, 32], BF16))
        Prow = _es.enter_context(nc.sbuf_tensor("Prow", [16, 128], F32))
        zero16 = _es.enter_context(nc.sbuf_tensor("zero16", [16, 128], F32))
        zaT = _es.enter_context(nc.sbuf_tensor("zaT", [16, 128], F32))
        zbT = _es.enter_context(nc.sbuf_tensor("zbT", [16, 128], F32))
        nlaT = _es.enter_context(nc.sbuf_tensor("nlaT", [16, 128], F32))
        nlbT = _es.enter_context(nc.sbuf_tensor("nlbT", [16, 128], F32))
        lkT = _es.enter_context(nc.sbuf_tensor("lkT", [16, 128], F32))
        lqT = _es.enter_context(nc.sbuf_tensor("lqT", [16, 128], F32))
        r_c1n = _es.enter_context(nc.sbuf_tensor("r_c1n", [16, 128], F32))
        r_s1b = _es.enter_context(nc.sbuf_tensor("r_s1b", [16, 128], F32))
        r_s1m = _es.enter_context(nc.sbuf_tensor("r_s1m", [16, 128], F32))
        r_rhsc = _es.enter_context(nc.sbuf_tensor("r_rhsc", [16, 128], BF16))
        r_g = _es.enter_context(nc.sbuf_tensor("r_g", [16, 128], BF16))
        r_rc = _es.enter_context(nc.sbuf_tensor("r_rc", [16, 128], BF16))
        r_sigb = _es.enter_context(nc.sbuf_tensor("r_sigb", [16, 128], BF16))
        r_gc = _es.enter_context(nc.sbuf_tensor("r_gc", [16, 128], BF16))
        r_gcpre = _es.enter_context(nc.sbuf_tensor("r_gcpre", [16, 128], F32))
        r_rcpre = _es.enter_context(nc.sbuf_tensor("r_rcpre", [16, 128], F32))
        c1n_hi = _es.enter_context(nc.sbuf_tensor("c1n_hi", [16, 128], BF16))
        c1n_lo = _es.enter_context(nc.sbuf_tensor("c1n_lo", [16, 128], BF16))
        s1b_hi = _es.enter_context(nc.sbuf_tensor("s1b_hi", [16, 128], BF16))
        s1b_lo = _es.enter_context(nc.sbuf_tensor("s1b_lo", [16, 128], BF16))
        s1m_hi = _es.enter_context(nc.sbuf_tensor("s1m_hi", [16, 128], BF16))
        s1m_lo = _es.enter_context(nc.sbuf_tensor("s1m_lo", [16, 128], BF16))
        tsplit = _es.enter_context(nc.sbuf_tensor("tsplit", [16, 128], F32))
        backTb = _es.enter_context(nc.sbuf_tensor("backTb", [128, 80], BF16))
        backTc = _es.enter_context(nc.sbuf_tensor("backTc", [128, 16], F32))
        ysq = _es.enter_context(nc.sbuf_tensor("ysq", [128, 1], F32))
        mcol = _es.enter_context(nc.sbuf_tensor("mcol", [128, 1], F32))
        vcol = _es.enter_context(nc.sbuf_tensor("vcol", [128, 1], F32))
        negm = _es.enter_context(nc.sbuf_tensor("negm", [128, 1], F32))
        gate = _es.enter_context(nc.sbuf_tensor("gate", [128, D], BF16))
        x2t = _es.enter_context(nc.sbuf_tensor("x2t", [128, D], F32))
        dstage = _es.enter_context(nc.sbuf_tensor("dstage", [128, 2048], F32)) if dbg else None
        pA = _es.enter_context(nc.psum_tensor("pA", [128, 1024], F32))
        pB = _es.enter_context(nc.psum_tensor("pB", [128, 1024], F32))
        pC = _es.enter_context(nc.psum_tensor("pC", [128, 1024], F32))
        pD = _es.enter_context(nc.psum_tensor("pD", [128, 512], F32))
        pT = _es.enter_context(nc.psum_tensor("pT", [128, 1024], BF16))
        if True:
            Cfs = {1: Cf1, 2: Cf2, 4: Cf4, 8: Cf8, 16: Cf16, 32: Cf32}

            def LW(g):
                P.wait(g, "st")
                P.wait(g, "dv")
                P.wait(g, "sc")
                P.wait(g, "pe")
                for dst, src, cols in [(Wk_s, WK, D), (Wq_s, WQ, D), (Wv_s, WV, D),
                                       (Wab_s, WAB, 2 * H)]:
                    sap = bass.AP(src, 0, [[cols, 128], [128 * cols, 8], [1, cols]])
                    P.inc(g.dma_start(dst[:, :, :], sap), "lg", 16)
                g.memset(Ktb[:, :], 0.0)
                g.memset(Qtb[:, :], 0.0)
                g.memset(Ssl[:, :], 0.0)
                g.memset(Sbf[:, :], 0.0)
                P.inc(g.memset(zero16[:, :], 0.0), "gp", 1)
            blk.gpsimd(LW)

            def dbgdump(slot, src_ap, cols, j):
                if not dbg or j != 0:
                    return

                def dv(v):
                    P.wait(v, "pe")
                    P.wait(v, "sc")
                    P.wait(v, "gp")
                    P.inc(v.tensor_copy(dstage[:, 0:cols], src_ap), "dv", 1)
                blk.vector(dv)

                def dsy(sy):
                    P.wait(sy, "dv")
                    P.inc(sy.dma_start(DBG.ap()[slot, :, 0:cols], dstage[:, 0:cols]),
                          "st", 16)
                blk.sync(dsy)

            R_RHSC, R_G, R_RC, R_SIGB, R_GC = 0, 16, 32, 48, 64

            for j in range(na1):
                xb = xbuf[j % 2]
                if j + 1 < NT:
                    def pref(sy, j=j):
                        P.wait(sy, "dv")
                        dma_x(P, sy, xbuf[(j + 1) % 2], X.ap(), j + 1)
                    blk.sync(pref)

                ld_need = P.cnt["ld"]

                def a1(v, j=j, xb=xb):
                    v.wait_ge(P.sems["ld"], ld_need)
                    if j == 0:
                        P.wait(v, "gp")
                    P.flush(v, v.tensor_reduce(mcol[:, 0:1], xb[:, :], AX.X, ALU.add),
                            "dv")
                    P.flush(v, v.tensor_scalar(mcol[:, 0:1], mcol[:, 0:1], 1.0 / D,
                                               None, ALU.mult), "dv")
                    P.flush(v, v.tensor_scalar(negm[:, 0:1], mcol[:, 0:1], -1.0, None,
                                               ALU.mult), "dv")
                blk.vector(a1)

                def a2s(sc, xb=xb):
                    P.wait(sc, "dv")
                    P.flush(sc, sc.activation(scrA[:, :], xb[:, :], ACT.Square,
                                               bias=negm[:, 0:1], scale=onecol[:, 0:1],
                                               accum_out=vcol[:, 0:1]), "sc")
                blk.scalar(a2s)

                def a2v(v):
                    P.wait(v, "sc")
                    P.flush(v, v.tensor_scalar(vcol[:, 0:1], vcol[:, 0:1], 1.0 / D, 1e-5,
                                               ALU.mult, ALU.add), "dv")
                blk.vector(a2v)

                def a2s2(sc):
                    P.wait(sc, "dv")
                    P.flush(sc, sc.activation(vcol[:, 0:1], vcol[:, 0:1], ACT.Ln,
                                              bias=zcol[:, 0:1], scale=onecol[:, 0:1]),
                            "sc")
                blk.scalar(a2s2)

                def a2v2(v):
                    P.wait(v, "sc")
                    P.flush(v, v.tensor_scalar(vcol[:, 0:1], vcol[:, 0:1], -0.5, None,
                                               ALU.mult), "dv")
                blk.vector(a2v2)

                def a2s3(sc):
                    P.wait(sc, "dv")
                    P.flush(sc, sc.activation(vcol[:, 0:1], vcol[:, 0:1], ACT.Exp,
                                              bias=zcol[:, 0:1], scale=onecol[:, 0:1]),
                            "sc")
                blk.scalar(a2s3)

                def a3v(v, j=j, xb=xb):
                    P.wait(v, "sc")
                    P.inc(v.tensor_scalar(xln[:, :], xb[:, :], mcol[:, 0:1], vcol[:, 0:1],
                                          ALU.subtract, ALU.mult), "dv", 1)
                    v.tensor_copy(LNm[:, j:j + 1], mcol[:, 0:1])
                    P.inc(v.tensor_copy(LNr[:, j:j + 1], vcol[:, 0:1]), "dv", 1)
                blk.vector(a3v)

                dbgdump(0, xln[:, :], D, j)
                def a4pe(te):
                    P.wait(te, "dv")
                    P.wait(te, "sc")
                    for p in range(8):
                        te.matmul(pT[:, p * 128:(p + 1) * 128], xln[:, p * 128:(p + 1) * 128],
                                  cidb[:, :], start=True, stop=True, is_transpose=True)
                    P.inc(te.last, "pe", 1)
                blkT(a4pe)

                def a4gp(sc):
                    P.wait(sc, "pe")
                    P.inc(sc.activation(xlnT[:, :], pT[:, :], ACT.Copy,
                                        bias=0.0, scale=onecol[:, 0:1]), "sc", 1)
                blk.scalar(a4gp)

                if A1STAGE < 2:
                    continue
                def a5pe(te):
                    P.wait(te, "sc")
                    P.wait(te, "gp")
                    P.wait(te, "lg")
                    P.wait(te, "dv")
                    P.wait(te, "sc")
                    for p in range(8):
                        for ps, Wsb in [(pA, Wk_s), (pB, Wq_s), (pC, Wv_s)]:
                            te.matmul(ps[:, 0:512], xlnT[:, p * 128:(p + 1) * 128],
                                      Wsb[:, p, 0:512], start=(p == 0), stop=(p == 7))
                            te.matmul(ps[:, 512:1024], xlnT[:, p * 128:(p + 1) * 128],
                                      Wsb[:, p, 512:1024], start=(p == 0), stop=(p == 7))
                        te.matmul(pD[0:16, 0:128], Wab_s[:, p, 0:16],
                                  xlnT[:, p * 128:(p + 1) * 128],
                                  start=(p == 0), stop=False)
                        te.matmul(pD[0:16, 128:256], Wab_s[:, p, 16:32],
                                  xlnT[:, p * 128:(p + 1) * 128],
                                  start=False, stop=(p == 7))
                    P.inc(te.last, "pe", 1)
                blkT(a5pe)
                ev_proj = P.cnt["pe"]

                def a6v(v):
                    P.wait(v, "pe", ev_proj)
                    P.inc(v.tensor_copy(Ktok[:, :], pA[:, :]), "dv", 1)
                blk.vector(a6v)

                def a6g(v):
                    P.wait(v, "pe", ev_proj)
                    P.inc(v.tensor_copy(Qtok[:, :], pB[:, :]), "dv", 1)
                    P.inc(v.tensor_copy(Vtok[:, :], pC[:, :]), "dv", 1)
                blk.vector(a6g)

                def a6s(sc):
                    P.wait(sc, "pe", ev_proj)
                    P.wait(sc, "dv")
                    P.inc(sc.activation(scrA[:, :], pA[:, :], ACT.Square, bias=zcol[:, 0:1], scale=onecol[:, 0:1]), "sc", 1)
                blk.scalar(a6s)

                def a7v(v):
                    P.wait(v, "sc")
                    P.flush(v, v.tensor_reduce(
                        stat32[:, 0:16].rearrange("p (a b) -> p a b", a=16, b=1),
                        scrA[:, :].rearrange("p (a b) -> p a b", a=16, b=64),
                        AX.X, ALU.add), "dv")
                blk.vector(a7v)

                def a7s(sc):
                    P.wait(sc, "dv")
                    P.inc(sc.activation(scrA[:, :], pB[:, :], ACT.Square, bias=zcol[:, 0:1], scale=onecol[:, 0:1]), "sc", 1)
                blk.scalar(a7s)

                def a8v(v):
                    P.wait(v, "sc")
                    P.flush(v, v.tensor_reduce(
                        stat32[:, 16:32].rearrange("p (a b) -> p a b", a=16, b=1),
                        scrA[:, :].rearrange("p (a b) -> p a b", a=16, b=64),
                        AX.X, ALU.add), "dv")
                    P.inc(v.tensor_copy(Sbf[:, :], Ssl[:, :]), "dv", 1)
                blk.vector(a8v)

                dbgdump(1, Ktok[:, :], D, j)
                dbgdump(2, Vtok[:, :], D, j)
                if A1STAGE < 3:
                    continue
                def a10s(sc):
                    P.wait(sc, "pe")
                    P.wait(sc, "dv")
                    P.inc(sc.activation(zaT[:, :], pD[0:16, 0:128], ACT.Exp,
                                        bias=nbab_a[:, 0:1], scale=nonecol[0:16, 0:1]),
                          "sc", 1)
                    P.inc(sc.activation(zbT[:, :], pD[0:16, 128:256], ACT.Exp,
                                        bias=nbab_b[:, 0:1], scale=nonecol[0:16, 0:1]),
                          "sc", 1)
                blk.scalar(a10s)

                def a9v(v):
                    P.wait(v, "sc")
                    P.flush(v, v.tensor_copy(st_hi[:, :], stat32[:, :]), "dv")
                    P.flush(v, v.tensor_sub(st_lo[:, :], stat32[:, :], st_hi[:, :]),
                            "dv")
                    P.inc(v.tensor_scalar(zaT[:, :], zaT[:, :], 1.0, None, ALU.add),
                          "dv", 1)
                    P.inc(v.tensor_scalar(zbT[:, :], zbT[:, :], 1.0, None, ALU.add),
                          "dv", 1)
                blk.vector(a9v)

                def a9pe(te):
                    P.wait(te, "dv")
                    P.wait(te, "sc")
                    te.matmul(pT[0:16, 0:128], st_hi[:, 0:16], cidb[:, :],
                              start=True, stop=True, is_transpose=True)
                    te.matmul(pT[0:16, 128:256], st_lo[:, 0:16], cidb[:, :],
                              start=True, stop=True, is_transpose=True)
                    te.matmul(pT[0:16, 256:384], st_hi[:, 16:32], cidb[:, :],
                              start=True, stop=True, is_transpose=True)
                    te.matmul(pT[0:16, 384:512], st_lo[:, 16:32], cidb[:, :],
                              start=True, stop=True, is_transpose=True)
                    P.inc(te.last, "pe", 1)
                blkT(a9pe)

                def a9v2(v):
                    P.wait(v, "pe")
                    P.flush(v, v.tensor_copy(lkT[:, :], pT[0:16, 0:128]), "dv")
                    P.flush(v, v.tensor_add(lkT[:, :], lkT[:, :], pT[0:16, 128:256]),
                            "dv")
                    P.flush(v, v.tensor_copy(lqT[:, :], pT[0:16, 256:384]), "dv")
                    P.flush(v, v.tensor_add(lqT[:, :], lqT[:, :], pT[0:16, 384:512]),
                            "dv")
                    P.flush(v, v.tensor_scalar(lkT[:, :], lkT[:, :], 1e-24, None,
                                               ALU.add), "dv")
                    P.flush(v, v.tensor_scalar(lqT[:, :], lqT[:, :], 1e-24, None,
                                               ALU.add), "dv")
                blk.vector(a9v2)

                if dbg and j == 0:
                    def dpre(v):
                        P.wait(v, "pe")
                        v.tensor_copy(dstage[0:128, 0:32], stat32[:, :])
                        v.tensor_copy(dstage[0:128, 32:64], st_hi[:, :])
                        v.tensor_copy(dstage[0:128, 64:96], st_lo[:, :])
                        v.tensor_copy(dstage[0:16, 96:224], lkT[:, :])
                        v.tensor_copy(dstage[0:16, 224:352], lqT[:, :])
                        v.tensor_copy(dstage[0:16, 352:480], pT[0:16, 0:128])
                        P.inc(v.tensor_copy(dstage[0:16, 480:608], pT[0:16, 128:256]),
                              "dv", 1)
                    blk.vector(dpre)

                    def dpre2(sy):
                        P.wait(sy, "dv")
                        P.inc(sy.dma_start(DBG.ap()[3], dstage[:, :]), "st", 16)
                    blk.sync(dpre2)

                def a11s(sc):
                    P.wait(sc, "dv")
                    z16, o16 = zcol[0:16, 0:1], onecol[0:16, 0:1]
                    P.flush(sc, sc.activation(nlaT[:, :], zaT[:, :], ACT.Ln, bias=z16,
                                               scale=o16), "sc")
                    P.flush(sc, sc.activation(nlbT[:, :], zbT[:, :], ACT.Ln, bias=z16,
                                              scale=o16), "sc")
                    P.flush(sc, sc.activation(lkT[:, :], lkT[:, :], ACT.Ln, bias=z16,
                                              scale=o16), "sc")
                    P.flush(sc, sc.activation(lqT[:, :], lqT[:, :], ACT.Ln, bias=z16,
                                              scale=o16), "sc")
                blk.scalar(a11s)

                if A1STAGE < 4:
                    continue

                def a12v(v):
                    P.wait(v, "sc")
                    P.flush(v, v.tensor_tensor_scan(Prow[:, 0:64], nlaT[:, 0:64],
                                               zero16[:, 0:64], 0.0, ALU.add, ALU.add), "dv")
                    P.flush(v, v.tensor_tensor_scan(Prow[:, 64:128], nlaT[:, 64:128],
                                               zero16[:, 64:128], 0.0, ALU.add, ALU.add),
                          "dv")
                    P.flush(v, v.scalar_tensor_tensor(r_c1n[:, :], lkT[:, :], -0.5,
                                                 Prow[:, :], ALU.mult, ALU.add), "dv")
                    P.flush(v, v.tensor_copy(c1n_hi[:, :], r_c1n[:, :]), "dv")
                    P.flush(v, v.tensor_sub(c1n_lo[:, :], r_c1n[:, :], c1n_hi[:, :]), "dv")
                    P.flush(v, v.tensor_sub(r_s1b[:, :], nlaT[:, :], nlbT[:, :]), "dv")
                    P.flush(v, v.tensor_sub(r_s1b[:, :], r_s1b[:, :], Prow[:, :]), "dv")
                    P.flush(v, v.scalar_tensor_tensor(r_s1b[:, :], lkT[:, :], -0.5,
                                                 r_s1b[:, :], ALU.mult, ALU.add), "dv")
                    P.flush(v, v.tensor_copy(s1b_hi[:, :], r_s1b[:, :]), "dv")
                    P.flush(v, v.tensor_sub(s1b_lo[:, :], r_s1b[:, :], s1b_hi[:, :]), "dv")
                    P.flush(v, v.scalar_tensor_tensor(r_s1m[:, :], lqT[:, :], -0.5,
                                                 Prow[:, :], ALU.mult, ALU.subtract),
                          "dv")
                    P.flush(v, v.tensor_copy(s1m_hi[:, :], r_s1m[:, :]), "dv")
                    P.flush(v, v.tensor_sub(s1m_lo[:, :], r_s1m[:, :], s1m_hi[:, :]), "dv")
                    P.flush(v, v.tensor_scalar(r_rcpre[:, 0:64], r_c1n[:, 0:64],
                                          Prow[:, 63:64], None, ALU.subtract), "dv")
                    P.flush(v, v.tensor_scalar(r_rcpre[:, 64:128], r_c1n[:, 64:128],
                                          Prow[:, 127:128], None, ALU.subtract), "dv")
                    P.flush(v, v.tensor_scalar(r_gcpre[:, 0:64], zero16[:, 0:64],
                                          Prow[:, 63:64], None, ALU.subtract), "dv")
                    P.flush(v, v.tensor_scalar(r_gcpre[:, 64:128], zero16[:, 64:128],
                                          Prow[:, 127:128], None, ALU.subtract), "dv")
                blk.vector(a12v)

                def a13s(sc):
                    P.wait(sc, "dv")
                    z16, o16, n16 = zcol[0:16, 0:1], onecol[0:16, 0:1], nonecol[0:16, 0:1]
                    P.flush(sc, sc.activation(r_rhsc[:, :], r_s1b[:, :], ACT.Exp,
                                        bias=z16, scale=o16), "sc")
                    P.flush(sc, sc.activation(r_g[:, :], r_s1m[:, :], ACT.Exp,
                                        bias=z16, scale=o16), "sc")
                    P.flush(sc, sc.activation(r_rc[:, :], r_rcpre[:, :], ACT.Exp,
                                        bias=z16, scale=o16), "sc")
                    P.flush(sc, sc.activation(r_sigb[:, :], nlbT[:, :], ACT.Exp,
                                        bias=z16, scale=n16), "sc")
                    P.flush(sc, sc.activation(r_gc[:, :], r_gcpre[:, :], ACT.Exp,
                                        bias=z16, scale=o16), "sc")
                blk.scalar(a13s)

                def a14pe(te):
                    P.wait(te, "sc")
                    for i, cls in enumerate([r_rhsc, r_g, r_rc, r_sigb, r_gc,
                                             c1n_hi, c1n_lo]):
                        te.matmul(pT[:, 512 + i * 16:512 + (i + 1) * 16], cls[:, :],
                                  cidb[0:16, 0:16], start=True, stop=True,
                                  is_transpose=True)
                    P.inc(te.last, "pe", 1)
                blkT(a14pe)

                def a14v(v):
                    P.wait(v, "pe")
                    P.flush(v, v.tensor_copy(backTb[:, :], pT[:, 512:592]), "dv")
                    P.flush(v, v.tensor_copy(backTc[:, :], pT[:, 592:608]), "dv")
                    P.flush(v, v.tensor_add(backTc[:, :], backTc[:, :], pT[:, 608:624]),
                            "dv")
                blk.vector(a14v)

                if A1STAGE < 5:
                    continue
                def a15v(v):
                    P.wait(v, "pe")
                    ps_k, ps_kt = pstep(Ktok), pstep(Ktb)
                    P.inc(v.tensor_copy(
                        bass.AP(Ktb, 0, [[ps_kt, 64], [128, H], [1, 64]]),
                        bass.AP(Ktok, 0, [[ps_k, 64], [64, H], [1, 64]])), "dv", 1)
                    ps_q, ps_qt = pstep(Qtok), pstep(Qtb)
                    P.inc(v.tensor_copy(
                        bass.AP(Qtb, 0, [[ps_qt, 64], [128, H], [1, 64]]),
                        bass.AP(Qtok, 0, [[ps_q, 64], [64, H], [1, 64]])), "dv", 1)
                blk.vector(a15v)

                def a15g(g):
                    P.wait(g, "dv")
                    ps_k, ps_kt = pstep(Ktok), pstep(Ktb)
                    P.inc(g.tensor_copy(
                        bass.AP(Ktb, 64 * ps_kt + 64, [[ps_kt, 64], [128, H], [1, 64]]),
                        bass.AP(Ktok, 64 * ps_k, [[ps_k, 64], [64, H], [1, 64]])), "gp", 1)
                    ps_q, ps_qt = pstep(Qtok), pstep(Qtb)
                    P.inc(g.tensor_copy(
                        bass.AP(Qtb, 64 * ps_qt + 64, [[ps_qt, 64], [128, H], [1, 64]]),
                        bass.AP(Qtok, 64 * ps_q, [[ps_q, 64], [64, H], [1, 64]])), "gp", 1)
                blk.gpsimd(a15g)

                for (tb, fb) in [(Ktb, Kfb), (Qtb, Qfb)]:
                    for w in range(2):
                        def a16pe(te, tb=tb, w=w):
                            P.wait(te, "gp")
                            P.wait(te, "sc")
                            P.wait(te, "dv")
                            for b in range(8):
                                hh = w * 8 + b
                                te.matmul(pT[:, b * 128:(b + 1) * 128],
                                          tb[:, hh * 128:(hh + 1) * 128],
                                          cidb[:, :], start=True, stop=True,
                                          is_transpose=True)
                            P.inc(te.last, "pe", 1)
                        blkT(a16pe)

                        def a16g(sc, fb=fb, w=w):
                            P.wait(sc, "pe")
                            P.inc(sc.activation(fb[:, w * 1024:(w + 1) * 1024], pT[:, :],
                                                ACT.Copy, bias=0.0,
                                                scale=onecol[:, 0:1]), "sc", 1)
                        blk.scalar(a16g)

                if A1STAGE < 6:
                    continue
                for half in range(2):
                    h0 = half * 8

                    def g1(te, h0=h0):
                        P.wait(te, "gp")
                        P.wait(te, "dv")
                        P.wait(te, "sc")
                        for hh in range(8):
                            ksl = Kfb[:, (h0 + hh) * 128:(h0 + hh + 1) * 128]
                            te.matmul(pA[:, hh * 128:(hh + 1) * 128], ksl, ksl,
                                      start=True, stop=True)
                            te.matmul(pC[:, hh * 128:(hh + 1) * 128], ksl,
                                      Qfb[:, (h0 + hh) * 128:(h0 + hh + 1) * 128],
                                      start=True, stop=True)
                            te.matmul(pD[:, hh * 64:(hh + 1) * 64], ksl,
                                      Sbf[:, (h0 + hh) * 64:(h0 + hh + 1) * 64],
                                      start=True, stop=True)
                        for bank in range(2):
                            for q in range(4):
                                hh = bank * 4 + q
                                oh = crep[:, (h0 + hh) * 128:(h0 + hh + 1) * 128]
                                te.matmul(pB[:, hh * 128:(hh + 1) * 128], oh,
                                          s1b_hi[:, :], start=(q == 0), stop=False)
                                te.matmul(pB[:, hh * 128:(hh + 1) * 128], oh,
                                          s1b_lo[:, :], start=False, stop=False)
                            te.matmul(pB[:, bank * 512:(bank + 1) * 512], cidb[:, :],
                                      maskB4[:, :], start=False, stop=True)
                        P.inc(te.last, "pe", 1)
                    blkT(g1)

                    def e1(v, h0=h0):
                        P.wait(v, "pe")
                        cb = bass.AP(backTc, h0,
                                     [[pstep(backTc), 128], [1, 8], [0, 128]])
                        P.inc(v.tensor_tensor(
                            Escr[:, :].rearrange("p (a b) -> p a b", a=8, b=128),
                            pB[:, :].rearrange("p (a b) -> p a b", a=8, b=128),
                            cb, ALU.add), "dv", 1)
                    blk.vector(e1)

                    def e1s(sc):
                        P.wait(sc, "dv")
                        P.inc(sc.activation(Epre[:, :], Escr[:, :], ACT.Exp, bias=zcol[:, 0:1], scale=onecol[:, 0:1]), "sc", 1)
                    blk.scalar(e1s)

                    def e1m(v, h0=h0):
                        P.wait(v, "sc")
                        P.inc(v.tensor_tensor(Cf1[:, h0 * 128:(h0 + 8) * 128],
                                              Epre[:, :], pA[:, :], ALU.mult), "dv", 1)
                        rb = bass.AP(backTb, R_RHSC + h0,
                                     [[pstep(backTb), 128], [1, 8], [0, 64]])
                        P.inc(v.tensor_tensor(
                            scrA[:, 0:512].rearrange("p (a b) -> p a b", a=8, b=64),
                            pD[:, 0:512].rearrange("p (a b) -> p a b", a=8, b=64),
                            rb, ALU.mult), "dv", 1)
                        sb = bass.AP(backTb, R_SIGB + h0,
                                     [[pstep(backTb), 128], [1, 8], [0, 64]])
                        P.inc(v.tensor_tensor(
                            scrA[:, 512:1024].rearrange("p (a b) -> p a b", a=8, b=64),
                            Vtok[:, h0 * 64:(h0 + 8) * 64].rearrange(
                                "p (a b) -> p a b", a=8, b=64),
                            sb, ALU.mult), "dv", 1)
                        P.inc(v.tensor_sub(Usb[:, h0 * 64:(h0 + 8) * 64],
                                           scrA[:, 512:1024], scrA[:, 0:512]), "dv", 1)
                    blk.vector(e1m)

                    if A1STAGE < 7:
                        continue
                    if half == 0:
                        dbgdump(4, Cf1[:, 0:1024], 1024, j)
                        dbgdump(5, Usb[:, 0:512], 512, j)
                    def ap1(te, k, h0=h0):
                        P.wait(te, "dv")
                        for b in range(8):
                            csl = Cfs[k][:, (h0 + b) * 128:(h0 + b + 1) * 128]
                            te.matmul(pD[:, b * 64:(b + 1) * 64], csl,
                                      Usb[:, (h0 + b) * 64:(h0 + b + 1) * 64],
                                      start=True, stop=True)
                        P.inc(te.last, "pe", 1)

                    def ap2(v, k, h0=h0):
                        P.wait(v, "pe")
                        op = ALU.subtract if k == 1 else ALU.add
                        P.inc(v.tensor_tensor(Usb[:, h0 * 64:(h0 + 8) * 64],
                                              Usb[:, h0 * 64:(h0 + 8) * 64],
                                              pD[:, 0:512], op), "dv", 1)

                    def sq1(te, h0=h0):
                        P.wait(te, "dv")
                        P.wait(te, "sc")
                        for b in range(8):
                            te.matmul(pT[:, b * 128:(b + 1) * 128],
                                      Cfs[1][:, (h0 + b) * 128:(h0 + b + 1) * 128],
                                      cidb[:, :], start=True, stop=True,
                                      is_transpose=True)
                        P.inc(te.last, "pe", 1)
                    blkT(sq1)

                    def sq2(sc):
                        P.wait(sc, "pe")
                        P.inc(sc.activation(Dsl[:, :], pT[:, :], ACT.Copy,
                                            bias=0.0, scale=onecol[:, 0:1]), "sc", 1)
                    blk.scalar(sq2)

                    for k in [1, 2, 4, 8, 16]:
                        def sqm(te, k=k, h0=h0):
                            P.wait(te, "sc")
                            P.wait(te, "dv")
                            for b in range(8):
                                te.matmul(pA[:, b * 128:(b + 1) * 128],
                                          Dsl[:, b * 128:(b + 1) * 128],
                                          Cfs[k][:, (h0 + b) * 128:(h0 + b + 1) * 128],
                                          start=True, stop=True)
                            if k < 16:
                                for b in range(8):
                                    te.matmul(pB[:, b * 128:(b + 1) * 128],
                                              Cfs[k][:, (h0 + b) * 128:(h0 + b + 1) * 128],
                                              Dsl[:, b * 128:(b + 1) * 128],
                                              start=True, stop=True)
                            P.inc(te.last, "pe", 1)
                        blkT(sqm)

                        def sqe_sc(sc, k=k, h0=h0):
                            P.wait(sc, "pe")
                            P.inc(sc.activation(Cfs[2 * k][:, h0 * 128:(h0 + 8) * 128],
                                                pA[:, :], ACT.Copy,
                                                bias=0.0, scale=onecol[:, 0:1]), "sc", 1)
                        blk.scalar(sqe_sc)

                        if k < 16:
                            def sqe_dv(v, k=k):
                                P.wait(v, "pe")
                                P.inc(v.tensor_copy(Dsl[:, :], pB[:, :]), "dv", 1)
                            blk.vector(sqe_dv)

                        def ap1s(te, k=k):
                            P.wait(te, "sc")
                            ap1(te, 2 * k)
                        blkT(ap1s)
                        blk.vector(lambda v, k=k: ap2(v, 2 * k))

                    blkT(lambda te: ap1(te, 1))
                    blk.vector(lambda v: ap2(v, 1))

                    if A1STAGE < 8:
                        continue
                    if half == 0:
                        dbgdump(6, Usb[:, 0:512], 512, j)
                    def m1(te, h0=h0):
                        P.wait(te, "dv")
                        for bank in range(2):
                            for q in range(4):
                                hh = bank * 4 + q
                                oh = crep[:, (h0 + hh) * 128:(h0 + hh + 1) * 128]
                                te.matmul(pB[:, hh * 128:(hh + 1) * 128], oh,
                                          s1m_hi[:, :], start=(q == 0), stop=False)
                                te.matmul(pB[:, hh * 128:(hh + 1) * 128], oh,
                                          s1m_lo[:, :], start=False, stop=False)
                            te.matmul(pB[:, bank * 512:(bank + 1) * 512], cidb[:, :],
                                      maskM4[:, :], start=False, stop=True)
                        P.inc(te.last, "pe", 1)
                    blkT(m1)

                    def m2(v, h0=h0):
                        P.wait(v, "pe")
                        cb = bass.AP(backTc, h0,
                                     [[pstep(backTc), 128], [1, 8], [0, 128]])
                        P.inc(v.tensor_tensor(
                            Escr[:, :].rearrange("p (a b) -> p a b", a=8, b=128),
                            pB[:, :].rearrange("p (a b) -> p a b", a=8, b=128),
                            cb, ALU.add), "dv", 1)
                    blk.vector(m2)

                    def m3(sc):
                        P.wait(sc, "dv")
                        P.inc(sc.activation(Epre[:, :], Escr[:, :], ACT.Exp, bias=zcol[:, 0:1], scale=onecol[:, 0:1]), "sc", 1)
                    blk.scalar(m3)

                    def m4(v, h0=h0):
                        P.wait(v, "sc")
                        P.inc(v.tensor_tensor(Mts[:, h0 * 128:(h0 + 8) * 128],
                                              Epre[:, :], pC[:, :], ALU.mult), "dv", 1)
                        rc = bass.AP(backTb, R_RC + h0,
                                     [[pstep(backTb), 128], [1, 8], [0, 64]])
                        P.inc(v.tensor_tensor(
                            Usc[:, h0 * 64:(h0 + 8) * 64].rearrange(
                                "p (a b) -> p a b", a=8, b=64),
                            Usb[:, h0 * 64:(h0 + 8) * 64].rearrange(
                                "p (a b) -> p a b", a=8, b=64),
                            rc, ALU.mult), "dv", 1)
                    blk.vector(m4)

                    def o1(te, h0=h0):
                        P.wait(te, "dv")
                        for b in range(8):
                            msl = Mts[:, (h0 + b) * 128:(h0 + b + 1) * 128]
                            te.matmul(pD[:, b * 64:(b + 1) * 64], msl,
                                      Usb[:, (h0 + b) * 64:(h0 + b + 1) * 64],
                                      start=True, stop=True)
                        for b in range(8):
                            qsl = Qfb[:, (h0 + b) * 128:(h0 + b + 1) * 128]
                            te.matmul(pB[:, b * 64:(b + 1) * 64], qsl,
                                      Sbf[:, (h0 + b) * 64:(h0 + b + 1) * 64],
                                      start=True, stop=True)
                        P.inc(te.last, "pe", 1)
                    blkT(o1)

                    def o2(v, h0=h0):
                        P.wait(v, "pe")
                        gb = bass.AP(backTb, R_G + h0,
                                     [[pstep(backTb), 128], [1, 8], [0, 64]])
                        P.inc(v.tensor_tensor(
                            scrA[:, 0:512].rearrange("p (a b) -> p a b", a=8, b=64),
                            pB[:, 0:512].rearrange("p (a b) -> p a b", a=8, b=64),
                            gb, ALU.mult), "dv", 1)
                        P.inc(v.tensor_tensor(ytile[:, h0 * 64:(h0 + 8) * 64],
                                              scrA[:, 0:512], pD[:, 0:512], ALU.add),
                              "dv", 1)
                    blk.vector(o2)

                    def su1(te, h0=h0):
                        P.wait(te, "dv")
                        for b in range(8):
                            ksl = Ktb[:, (h0 + b) * 128:(h0 + b + 1) * 128]
                            te.matmul(pD[:, b * 64:(b + 1) * 64], ksl,
                                      Usc[:, (h0 + b) * 64:(h0 + b + 1) * 64],
                                      start=True, stop=True)
                        P.inc(te.last, "pe", 1)
                    blkT(su1)

                    def su2(v, h0=h0):
                        P.wait(v, "pe")
                        gc = bass.AP(backTb, R_GC + h0,
                                     [[pstep(backTb), 128], [1, 8], [0, 64]])
                        P.inc(v.tensor_tensor(
                            scrA[:, 0:512].rearrange("p (a b) -> p a b", a=8, b=64),
                            Ssl[:, h0 * 64:(h0 + 8) * 64].rearrange(
                                "p (a b) -> p a b", a=8, b=64),
                            gc, ALU.mult), "dv", 1)
                        P.inc(v.tensor_tensor(Ssl[:, h0 * 64:(h0 + 8) * 64],
                                              scrA[:, 0:512], pD[:, 0:512], ALU.add),
                              "dv", 1)
                    blk.vector(su2)

                dbgdump(7, ytile[:, :], D, j)
                def ys1(v, j=j):
                    P.flush(v, v.tensor_mul(scrA[:, :], ytile[:, :], ytile[:, :]),
                            "dv")
                    P.inc(v.tensor_reduce(YSQ[:, j:j + 1], scrA[:, :], AX.X, ALU.add),
                          "dv", 1)
                blk.vector(ys1)

                def ys3(sy, j=j):
                    P.wait(sy, "dv")
                    P.inc(sy.dma_start(YD.ap()[j], ytile[:, :]), "st", 16)
                blk.sync(ys3)

            def LW2(g):
                P.wait(g, "pe")
                for dst, src in [(Wk_s, WG), (Wq_s, WO)]:
                    sap = bass.AP(src, 0, [[D, 128], [128 * D, 8], [1, D]])
                    P.inc(g.dma_start(dst[:, :, :], sap), "lg", 16)
            blk.gpsimd(LW2)
            ld_w2 = P.cnt["lg"]

            def foldgn(v):
                v.wait_ge(P.sems["lg"], ld_w2)
                for p in range(8):
                    P.flush(v, v.tensor_scalar(Wq_s[:, p, :], Wq_s[:, p, :],
                                               gnsb[:, p:p + 1], None, ALU.mult), "dv")
            blk.vector(foldgn)

            def ysb1(v):
                P.wait(v, "dv")
                P.flush(v, v.tensor_scalar(RSTY[:, 0:NT], YSQ[:, 0:NT], 1.0 / D, 1e-5,
                                           ALU.mult, ALU.add), "dv")
            blk.vector(ysb1)

            def ysb2(sc):
                P.wait(sc, "dv")
                P.inc(sc.activation(RSTY[:, 0:NT], RSTY[:, 0:NT], ACT.Ln,
                                    bias=zcol[:, 0:1], scale=onecol[:, 0:1]), "sc", 1)
            blk.scalar(ysb2)

            def ysb3(v):
                P.wait(v, "sc")
                P.flush(v, v.tensor_scalar(RSTY[:, 0:NT], RSTY[:, 0:NT], -0.5, None,
                                           ALU.mult), "dv")
            blk.vector(ysb3)

            def ysb4(sc):
                P.wait(sc, "dv")
                P.inc(sc.activation(RSTY[:, 0:NT], RSTY[:, 0:NT], ACT.Exp,
                                    bias=zcol[:, 0:1], scale=onecol[:, 0:1]), "sc", 1)
            blk.scalar(ysb4)

            act_tab(P, 18)

            for j in range(na2):
                xb = xbuf[j % 2]

                def b0(sy, j=j, xb=xb):
                    P.wait(sy, "dv")
                    P.wait(sy, "sc")
                    dma_x(P, sy, xb, X.ap(), j)
                    P.inc(sy.dma_start(ytile[:, :], YD.ap()[j]), "ld", 16)
                blk.sync(b0)
                ldj = P.cnt["ld"]

                def b1(v, j=j, xb=xb):
                    v.wait_ge(P.sems["ld"], ldj)
                    P.inc(v.tensor_scalar(xln[:, :], xb[:, :], LNm[:, j:j + 1],
                                          LNr[:, j:j + 1], ALU.subtract, ALU.mult),
                          "dv", 1)
                blk.vector(b1)

                def b2(te):
                    P.wait(te, "dv")
                    P.wait(te, "sc")
                    for p in range(8):
                        te.matmul(pT[:, p * 128:(p + 1) * 128],
                                  xln[:, p * 128:(p + 1) * 128],
                                  cidb[:, :], start=True, stop=True, is_transpose=True)
                    P.inc(te.last, "pe", 1)
                blkT(b2)

                def b3(sc):
                    P.wait(sc, "pe")
                    P.inc(sc.activation(xlnT[:, :], pT[:, :], ACT.Copy,
                                        bias=0.0, scale=onecol[:, 0:1]), "sc", 1)
                blk.scalar(b3)

                def b4(te):
                    P.wait(te, "lg")
                    P.wait(te, "sc")
                    for p in range(8):
                        te.matmul(pA[:, 0:512], xlnT[:, p * 128:(p + 1) * 128],
                                  Wk_s[:, p, 0:512], start=(p == 0), stop=(p == 7))
                        te.matmul(pA[:, 512:1024], xlnT[:, p * 128:(p + 1) * 128],
                                  Wk_s[:, p, 512:1024], start=(p == 0), stop=(p == 7))
                    P.inc(te.last, "pe", 1)
                blkT(b4)

                def b5(sc):
                    P.wait(sc, "pe")
                    P.inc(sc.activation(gate[:, :], pA[:, :], ACT.Silu, bias=zcol[:, 0:1], scale=onecol[:, 0:1]), "sc", 1)
                blk.scalar(b5)

                def b6(v):
                    P.wait(v, "sc")
                    P.inc(v.tensor_mul(xln[:, :], ytile[:, :], gate[:, :]), "dv", 1)
                blk.vector(b6)

                def b7(te):
                    P.wait(te, "dv")
                    P.wait(te, "sc")
                    for p in range(8):
                        te.matmul(pT[:, p * 128:(p + 1) * 128],
                                  xln[:, p * 128:(p + 1) * 128],
                                  cidb[:, :], start=True, stop=True, is_transpose=True)
                    P.inc(te.last, "pe", 1)
                blkT(b7)

                def b8(sc):
                    P.wait(sc, "pe")
                    P.inc(sc.activation(xlnT[:, :], pT[:, :], ACT.Copy,
                                        bias=0.0, scale=onecol[:, 0:1]), "sc", 1)
                blk.scalar(b8)

                def b9(te):
                    P.wait(te, "sc")
                    P.wait(te, "dv")
                    for p in range(8):
                        te.matmul(pC[:, 0:512], xlnT[:, p * 128:(p + 1) * 128],
                                  Wq_s[:, p, 0:512], start=(p == 0), stop=(p == 7))
                        te.matmul(pC[:, 512:1024], xlnT[:, p * 128:(p + 1) * 128],
                                  Wq_s[:, p, 512:1024], start=(p == 0), stop=(p == 7))
                    P.inc(te.last, "pe", 1)
                blkT(b9)

                def b10(v, j=j, xb=xb):
                    P.wait(v, "pe")
                    if dbg and j == 0:
                        P.inc(v.tensor_copy(dstage[:, 0:1024], pC[:, :]), "dv", 1)
                        P.inc(v.tensor_copy(dstage[:, 1024:1040], RSTY[:, 0:16]),
                              "dv", 1)
                    P.inc(v.scalar_tensor_tensor(x2t[:, :], pC[:, :], RSTY[:, j:j + 1],
                                                 xb[:, :], ALU.mult, ALU.add), "dv", 1)
                blk.vector(b10)
                if dbg and j == 0:
                    def db10(sy):
                        P.wait(sy, "dv")
                        P.inc(sy.dma_start(DBG.ap()[5], dstage[:, :]), "st", 16)
                    blk.sync(db10)

                def b11(sy, j=j):
                    P.wait(sy, "dv")
                    P.inc(sy.dma_start(X2D.ap()[j], x2t[:, :]), "st", 16)
                blk.sync(b11)

        _es.close()
        _esb = ExitStack()
        w1t_s = _esb.enter_context(nc.sbuf_tensor("w1t_s", [128, 8, HID], BF16))
        w2t_s = _esb.enter_context(nc.sbuf_tensor("w2t_s", [128, 32, D], BF16))
        x2b = _esb.enter_context(nc.sbuf_tensor("x2b", [128, D], F32))
        x2b2 = _esb.enter_context(nc.sbuf_tensor("x2b2", [128, D], F32))
        xqb = _esb.enter_context(nc.sbuf_tensor("xqb", [128, D], BF16))
        xqTb = _esb.enter_context(nc.sbuf_tensor("xqTb", [128, D], BF16))
        hbuf = _esb.enter_context(nc.sbuf_tensor("hbuf", [128, HID], F32))
        hqb = _esb.enter_context(nc.sbuf_tensor("hqb", [128, HID], BF16))
        hqT = _esb.enter_context(nc.sbuf_tensor("hqT", [128, HID], BF16))
        obuf = _esb.enter_context(nc.sbuf_tensor("obuf", [128, D], F32))
        bm = _esb.enter_context(nc.sbuf_tensor("bm", [128, 1], F32))
        bnm = _esb.enter_context(nc.sbuf_tensor("bnm", [128, 1], F32))
        bv = _esb.enter_context(nc.sbuf_tensor("bv", [128, 1], F32))
        bq = _esb.enter_context(nc.sbuf_tensor("bq", [128, 1], F32))
        chc = _esb.enter_context(nc.sbuf_tensor("chc", [128, 1], F32))
        c2c = _esb.enter_context(nc.sbuf_tensor("c2c", [128, 1], F32))
        bscr = _esb.enter_context(nc.sbuf_tensor("bscr", [128, 1024], F32))
        SUMX = _esb.enter_context(nc.sbuf_tensor("SUMX", [128, NT], F32))
        MAXX = _esb.enter_context(nc.sbuf_tensor("MAXX", [128, NT], F32))
        MINX = _esb.enter_context(nc.sbuf_tensor("MINX", [128, NT], F32))
        SQX = _esb.enter_context(nc.sbuf_tensor("SQX", [128, NT], F32))
        pH = _esb.enter_context(nc.psum_tensor("pH", [128, 1024], F32))
        pO = _esb.enter_context(nc.psum_tensor("pO", [128, 1024], F32))
        pTb = _esb.enter_context(nc.psum_tensor("pTb", [128, 1024], BF16))
        if True:
            def bl0(sy):
                P.wait(sy, "st")
                P.wait(sy, "dv")
                P.wait(sy, "sc")
                P.wait(sy, "pe")
                P.wait(sy, "gp")
                sap1 = bass.AP(W1T, 0, [[HID, 128], [128 * HID, 8], [1, HID]])
                P.inc(sy.dma_start(w1t_s[:, :, :], sap1), "ld", 16)
                sap2 = bass.AP(W2T, 0, [[D, 128], [128 * D, 32], [1, D]])
                P.inc(sy.dma_start(w2t_s[:, :, :], sap2), "ld", 16)
            blk.sync(bl0)

            act_tab(P, 6)

            evr = {}
            evs = {}
            for j in range(nbpre):
                def c0(sy, j=j):
                    if j == 0:
                        P.wait(sy, "st")
                    if j >= 2:
                        sy.wait_ge(P.sems["dv"], evr[j - 2])
                    P.inc(sy.dma_start((x2b if j % 2 == 0 else x2b2)[:, :], X2D.ap()[j]),
                          "ld", 16)
                blk.sync(c0)
                ldj = P.cnt["ld"]

                def c1(v, j=j):
                    v.wait_ge(P.sems["ld"], ldj)
                    buf = x2b if j % 2 == 0 else x2b2
                    P.flush(v, v.tensor_reduce(SUMX[:, j:j + 1], buf[:, :], AX.X, ALU.add),
                            "dv")
                    evr[j] = P.cnt["dv"]
                blk.vector(c1)

            sN = slice(0, nbpre)

            def cta(v):
                P.wait(v, "dv")
                P.flush(v, v.tensor_scalar(MEAN2[:, sN], SUMX[:, sN], 1.0 / D, None,
                                           ALU.mult), "dv")
                P.flush(v, v.tensor_scalar(MINX[:, sN], MEAN2[:, sN], -1.0, None,
                                           ALU.mult), "dv")
            blk.vector(cta)
            ev_negm = P.cnt["dv"]

            evr = {}
            for j in range(nbpre):
                def c0b(sy, j=j):
                    if j < 2:
                        sy.wait_ge(P.sems["dv"], ev_negm)
                    else:
                        sy.wait_ge(P.sems["dv"], evr[j - 2])
                        sy.wait_ge(P.sems["sc"], evs[j - 2])
                    P.inc(sy.dma_start((x2b if j % 2 == 0 else x2b2)[:, :], X2D.ap()[j]),
                          "ld", 16)
                blk.sync(c0b)
                ldj = P.cnt["ld"]

                def c1b(v, j=j):
                    v.wait_ge(P.sems["ld"], ldj)
                    v.wait_ge(P.sems["dv"], ev_negm)
                    buf = x2b if j % 2 == 0 else x2b2
                    P.flush(v, v.tensor_scalar(bscr[:, :], buf[:, :], MEAN2[:, j:j + 1],
                                               None, ALU.subtract), "dv")
                    P.flush(v, v.tensor_reduce(MAXX[:, j:j + 1], bscr[:, :], AX.X, ALU.max,
                                               apply_absolute_value=True), "dv")
                    evr[j] = P.cnt["dv"]
                blk.vector(c1b)

                def c2b(sc, j=j):
                    sc.wait_ge(P.sems["ld"], ldj)
                    sc.wait_ge(P.sems["dv"], ev_negm)
                    buf = x2b if j % 2 == 0 else x2b2
                    P.flush(sc, sc.activation(hbuf[:, 0:D], buf[:, :], ACT.Square,
                                              bias=MINX[:, j:j + 1], scale=onecol[:, 0:1],
                                              accum_out=SQX[:, j:j + 1]), "sc")
                    evs[j] = P.cnt["sc"]
                blk.scalar(c2b)

            def ct1(v):
                P.wait(v, "sc")
                P.wait(v, "dv")
                P.flush(v, v.tensor_scalar(bscr[:, 32:32 + nbpre], SQX[:, sN],
                                           1.0 / D, 1e-5, ALU.mult, ALU.add), "dv")
            blk.vector(ct1)

            def ct2(sc):
                P.wait(sc, "dv")
                P.inc(sc.activation(bscr[:, 32:32 + nbpre], bscr[:, 32:32 + nbpre],
                                    ACT.Ln, bias=zcol[:, 0:1], scale=onecol[:, 0:1]),
                      "sc", 1)
            blk.scalar(ct2)

            def ct3(v):
                P.wait(v, "sc")
                P.flush(v, v.tensor_scalar(bscr[:, 32:32 + nbpre], bscr[:, 32:32 + nbpre],
                                           -0.5, None, ALU.mult), "dv")
            blk.vector(ct3)

            def ct4(sc):
                P.wait(sc, "dv")
                P.inc(sc.activation(bscr[:, 64:64 + nbpre], bscr[:, 32:32 + nbpre],
                                    ACT.Exp, bias=zcol[:, 0:1], scale=onecol[:, 0:1]),
                      "sc", 1)
            blk.scalar(ct4)

            def ct5(v):
                P.wait(v, "sc")
                P.flush(v, v.tensor_tensor(SCQ[:, sN], MAXX[:, sN],
                                           bscr[:, 64:64 + nbpre], ALU.mult), "dv")
                P.flush(v, v.tensor_scalar(SCQ[:, sN], SCQ[:, sN], 1e-5, None, ALU.max),
                        "dv")
                P.flush(v, v.reciprocal(bscr[:, 96:96 + nbpre], SCQ[:, sN]), "dv")
                P.flush(v, v.scalar_tensor_tensor(CC2[:, sN], bscr[:, 96:96 + nbpre],
                                                  127.0, bscr[:, 64:64 + nbpre],
                                                  ALU.mult, ALU.mult), "dv")
            blk.vector(ct5)

            act_tab(P, 10)
            ld_w = P.cnt["ld"]
            for j in range(nbmain):
                def d0(sy, j=j):
                    P.wait(sy, "dv")
                    P.inc(sy.dma_start((x2b if j % 2 == 0 else x2b2)[:, :], X2D.ap()[j]),
                          "ld", 16)
                blk.sync(d0)
                ldj = P.cnt["ld"]

                def d1(v, j=j):
                    v.wait_ge(P.sems["ld"], max(ldj, ld_w))
                    buf = x2b if j % 2 == 0 else x2b2
                    P.flush(v, v.tensor_scalar(bscr[:, :], buf[:, :], MEAN2[:, j:j + 1],
                                          CC2[:, j:j + 1], ALU.subtract, ALU.mult),
                          "dv")
                    P.flush(v, v.tensor_scalar(xqb[:, :], bscr[:, :], MAGIC, -MAGIC,
                                          ALU.add, ALU.add), "dv")
                    P.flush(v, v.tensor_scalar(chc[:, 0:1], SCQ[:, j:j + 1], s1col[:, 0:1],
                                          1.0 / 127.0, ALU.mult, ALU.mult), "dv")
                blk.vector(d1)

                def d2(te):
                    P.wait(te, "dv")
                    P.wait(te, "sc")
                    for p in range(8):
                        te.matmul(pTb[:, p * 128:(p + 1) * 128],
                                  xqb[:, p * 128:(p + 1) * 128],
                                  cidb[:, :], start=True, stop=True, is_transpose=True)
                    P.inc(te.last, "pe", 1)
                blkT(d2)

                def d3(sc):
                    P.wait(sc, "pe")
                    P.inc(sc.activation(xqTb[:, :], pTb[:, :], ACT.Copy,
                                        bias=0.0, scale=onecol[:, 0:1]), "sc", 1)
                blk.scalar(d3)
                ev_xqT = P.cnt["sc"]

                for q in range(4):
                    def d4(te, q=q):
                        P.wait(te, "sc", ev_xqT)
                        P.wait(te, "sc")
                        for p in range(8):
                            te.matmul(pH[:, 0:512], xqTb[:, p * 128:(p + 1) * 128],
                                      w1t_s[:, p, q * 1024:q * 1024 + 512],
                                      start=(p == 0), stop=(p == 7))
                            te.matmul(pH[:, 512:1024], xqTb[:, p * 128:(p + 1) * 128],
                                      w1t_s[:, p, q * 1024 + 512:(q + 1) * 1024],
                                      start=(p == 0), stop=(p == 7))
                        P.inc(te.last, "pe", 1)
                    blkT(d4)

                    def d5(sc, q=q):
                        P.wait(sc, "pe")
                        P.inc(sc.activation(hbuf[:, q * 1024:(q + 1) * 1024], pH[:, :],
                                            ACT.Gelu, bias=zcol[:, 0:1], scale=chc[:, 0:1]),
                              "sc", 1)
                    blk.scalar(d5)

                def d6(v, j=j):
                    P.wait(v, "sc")
                    P.flush(v, v.tensor_reduce(bq[:, 0:1], hbuf[:, :], AX.X, ALU.max,
                                          apply_absolute_value=True), "dv")
                    P.flush(v, v.tensor_scalar(bq[:, 0:1], bq[:, 0:1], 1e-5, None, ALU.max),
                          "dv")
                    P.flush(v, v.tensor_scalar(c2c[:, 0:1], bq[:, 0:1], s2col[:, 0:1],
                                          1.0 / 127.0, ALU.mult, ALU.mult), "dv")
                    P.flush(v, v.reciprocal(bq[:, 0:1], bq[:, 0:1]), "dv")
                    P.flush(v, v.tensor_scalar(bq[:, 0:1], bq[:, 0:1], 127.0, None, ALU.mult),
                          "dv")
                blk.vector(d6)

                qev = []

                def d6b(v):
                    for q in range(4):
                        v.tensor_scalar(bscr[:, :], hbuf[:, q * 1024:(q + 1) * 1024],
                                        bq[:, 0:1], None, ALU.mult)
                        P.inc(v.tensor_scalar(hqb[:, q * 1024:(q + 1) * 1024], bscr[:, :],
                                              MAGIC, -MAGIC, ALU.add, ALU.add), "dv", 1)
                        qev.append(P.cnt["dv"])
                blk.vector(d6b)

                ev_d8 = []
                for w in range(4):
                    def d7(te, w=w):
                        te.wait_ge(P.sems["dv"], ev_d8[w - 1] if w > 0 else qev[0])
                        for b in range(8):
                            te.matmul(pTb[:, b * 128:(b + 1) * 128],
                                      hqb[:, (w * 8 + b) * 128:(w * 8 + b + 1) * 128],
                                      cidb[:, :], start=True, stop=True,
                                      is_transpose=True)
                        P.inc(te.last, "pe", 1)
                    blkT(d7)

                    def d8(v, w=w):
                        P.wait(v, "pe")
                        P.inc(v.tensor_copy(hqT[:, w * 1024:(w + 1) * 1024], pTb[:, :]),
                              "dv", 1)
                        ev_d8.append(P.cnt["dv"])
                    blk.vector(d8)

                def d9(te):
                    P.wait(te, "dv")
                    for p in range(32):
                        te.matmul(pO[:, 0:512], hqT[:, p * 128:(p + 1) * 128],
                                  w2t_s[:, p, 0:512], start=(p == 0), stop=(p == 31))
                        te.matmul(pO[:, 512:1024], hqT[:, p * 128:(p + 1) * 128],
                                  w2t_s[:, p, 512:1024], start=(p == 0), stop=(p == 31))
                    P.inc(te.last, "pe", 1)
                blkT(d9)

                def d10(v, j=j):
                    P.wait(v, "pe")
                    buf = x2b if j % 2 == 0 else x2b2
                    P.inc(v.scalar_tensor_tensor(obuf[:, :], pO[:, :], c2c[:, 0:1],
                                                 buf[:, :], ALU.mult, ALU.add), "dv", 1)
                blk.vector(d10)

                def d11(sy, j=j):
                    P.wait(sy, "dv")
                    dma_x(P, sy, obuf, OUT.ap(), j, store=True, sem="st")
                blk.sync(d11)

        _esb.close()
        def fin(sy):
            P.wait(sy, "st")
        blk.sync(fin)

    nc.finalize()
    return nc


_NC_CACHE = None


def host_inputs(inputs, core):
    ii = np.arange(128)
    tt = np.arange(128)
    maskb = np.where(tt[None, :] > ii[:, None], 0.0, -1e30).astype(np.float32)
    maskm = np.where(tt[None, :] >= ii[:, None], 0.0, -1e30).astype(np.float32)
    d = {
        "x": np.ascontiguousarray(inputs["L0_attn_x"][2 * core:2 * core + 2]),
        "wq": inputs["L0_attn_Wq"], "wk": inputs["L0_attn_Wk"],
        "wv": inputs["L0_attn_Wv"],
        "wab": np.ascontiguousarray(
            np.concatenate([inputs["L0_attn_Wa"], inputs["L0_attn_Wb"]], axis=1)),
        "bab": np.concatenate([inputs["L0_attn_ba"],
                               inputs["L0_attn_bb"]]).reshape(1, 32),
        "wg": inputs["L0_attn_Wg"], "wo": inputs["L0_attn_Wo"],
        "gn": np.ascontiguousarray(inputs["L0_attn_g_norm"].reshape(8, 128).T),
        "w1": inputs["L0_ffn_w1"], "w2": inputs["L0_ffn_w2"],
        "cidb": np.eye(128, dtype=BFNP),
        "cidf": np.eye(128, dtype=np.float32),
        "cmaskb": maskb.astype(BFNP), "cmaskm": maskm.astype(BFNP),
        "cones": np.ones((1, 128), dtype=np.float32),
        "cone128": np.ones((128, 1), dtype=np.float32),
        "crep": np.repeat(np.eye(16, dtype=BFNP), 128, axis=1) if False else np.kron(np.eye(16), np.ones((1, 128))).astype(BFNP),
        "cmaskb4": np.tile(maskb, (1, 4)).astype(BFNP),
        "cmaskm4": np.tile(maskm, (1, 4)).astype(BFNP),
        "babt": np.concatenate([inputs["L0_attn_ba"], inputs["L0_attn_bb"]]).reshape(32, 1),
    }
    return {k: np.ascontiguousarray(np.asarray(v)) for k, v in d.items()}


def kernel(**inputs):
    global _NC_CACHE
    if _NC_CACHE is None:
        _NC_CACHE = build_program()
    nc = _NC_CACHE
    in_maps = [host_inputs(inputs, c) for c in range(NCORES)]
    res = run_bass_kernel_spmd(nc, in_maps, core_ids=list(range(NCORES)))
    out = np.empty((16, SEQ, D), dtype=np.float32)
    for c in range(NCORES):
        out[2 * c:2 * c + 2] = np.asarray(res.results[c]["out"])
    return out
